# version 19
# speedup vs baseline: 1.1394x; 1.0029x over previous
import contextlib

import numpy as np
import ml_dtypes

import concourse.bass as bass
import concourse.mybir as mybir
from concourse.bass_utils import run_bass_kernel_spmd

F32 = mybir.dt.float32
BF16 = mybir.dt.bfloat16
U8 = mybir.dt.uint8
AF = mybir.ActivationFunctionType
OP = mybir.AluOpType
BF = ml_dtypes.bfloat16

NCORES = 8
E = 1024
DIN = 2048
NST = 16
DCONV = 4
DTR = 64
BATCH = 2
SEQ = 16384
TPC = SEQ // 4
DSL = DIN // NCORES
CH = 512
EPS = 1e-5


class Prog:

    ENGINES = ("sp", "act", "dve", "pe", "pool")

    def __init__(self, nc, stack):
        self.nc = nc
        self.stack = stack
        self.sems = {}
        self.squeues = {}
        self.equeues = {e: [] for e in self.ENGINES}
        for e in self.ENGINES:
            self._get_sem(e)
        self.last_writer = {}
        self.readers = {}

    def _get_sem(self, key):
        if key not in self.sems:
            self.sems[key] = self.stack.enter_context(self.nc.semaphore("sem_" + str(key)))
            self.squeues[key] = []
        return self.sems[key]

    def _append(self, engine, fn, reads, writes, sem_key, inc, flagged):
        need = {}

        def add(src):
            if src is None:
                return
            sk, idx, eng2 = src
            if eng2 == engine and sk == "pe" and engine == "pe":
                return
            if need.get(sk, -1) < idx:
                need[sk] = idx
        for r in reads:
            add(self.last_writer.get(r))
        for w in writes:
            add(self.last_writer.get(w))
            for rr in self.readers.get(w, ()):
                add(rr)
        for sk, idx in need.items():
            self.squeues[sk][idx][3] = True
        rec = [need, fn, inc, flagged, sem_key]
        self.equeues[engine].append(rec)
        sidx = len(self.squeues[sem_key])
        self.squeues[sem_key].append(rec)
        src = (sem_key, sidx, engine)
        for r in reads:
            self.readers.setdefault(r, []).append(src)
        for w in writes:
            self.last_writer[w] = src
            self.readers[w] = []

    def op(self, engine, fn, reads=(), writes=(), inc=1):
        self._append(engine, fn, reads, writes, engine, inc, False)

    def dma(self, engine, out, in_, stream, reads=(), writes=(), **kw):
        self._get_sem(stream)
        self._append(engine, lambda e: e.dma_start(out=out, in_=in_, **kw),
                     reads, writes, stream, 16, True)

    def collective(self, fn, reads=(), writes=()):
        self._get_sem("cc")
        self._append("pool", fn, reads, writes, "cc", 1, True)

    def emit(self, block):
        decos = dict(sp=block.sync, act=block.scalar, dve=block.vector,
                     pe=block.tensor, pool=block.gpsimd)
        cums = {}
        for sk, q in self.squeues.items():
            c = 0
            cl = []
            for rec in q:
                if rec[3]:
                    c += rec[2]
                cl.append(c)
            cums[sk] = cl
        for ename in self.ENGINES:
            queue = self.equeues[ename]
            if not queue:
                continue

            def body(eng, queue=queue):
                floor = {}
                for (need, fn, inc, flagged, sem_key) in queue:
                    for sk, idx in need.items():
                        val = cums[sk][idx]
                        if floor.get(sk, 0) < val:
                            eng.wait_ge(self.sems[sk], val)
                            floor[sk] = val
                    instr = fn(eng)
                    if flagged:
                        instr.then_inc(self.sems[sem_key], inc)

            decos[ename](body)



def emit_ln(p, env, src, dst, ntiles, width, gb=None, gbkeys=(("s_gb",),)):
    xb, sq = env["lnxb"], env["lnsq"]
    pstatA, pstatB = env["pstatA"], env["pstatB"]
    rows = env["rows"]
    pbc1, pbc2 = env["pbc1"], env["pbc2"]
    onesb, ones1f, cst = env["onesb"], env["ones1f"], env["cst"]
    scr = env["lnscr"]
    inv_n = 1.0 / (128 * ntiles)

    for t in range(ntiles):
        a, k = src[t]
        s2_ = t % 4
        p.op("act", lambda e, a=a, s2_=s2_: e.activation(
            out=xb[:, s2_ * width:(s2_ + 1) * width], in_=a, func=AF.Copy),
             reads=[k], writes=[("lnxb", s2_)])
        p.op("act", lambda e, a=a, s2_=s2_: e.activation(
            out=sq[:, s2_ * width:(s2_ + 1) * width], in_=a, func=AF.Square),
             reads=[k], writes=[("lnsq", s2_)])
        p.op("pe", lambda e, s2_=s2_, t=t: e.matmul(out=pstatA[0:1, :], lhsT=onesb[:, 0:1],
                                                    rhs=xb[:, s2_ * width:(s2_ + 1) * width],
                                                    start=(t == 0), stop=(t == ntiles - 1)),
             reads=[("lnxb", s2_), ("onesb",)], writes=[("pstatA",)])
        p.op("pe", lambda e, s2_=s2_, t=t: e.matmul(out=pstatB[0:1, :], lhsT=onesb[:, 0:1],
                                                    rhs=sq[:, s2_ * width:(s2_ + 1) * width],
                                                    start=(t == 0), stop=(t == ntiles - 1)),
             reads=[("lnsq", s2_), ("onesb",)], writes=[("pstatB",)])
    m_row = rows[0:1, 0:width]
    var_row = rows[0:1, width:2 * width]
    rstd_row = rows[0:1, 2 * width:3 * width]
    tr_row = rows[0:1, 3 * width:4 * width]
    p.op("dve", lambda e: e.tensor_scalar(out=m_row, in0=pstatA[0:1, :], scalar1=inv_n,
                                          scalar2=0.0, op0=OP.mult, op1=OP.add),
         reads=[("pstatA",)], writes=[("m_row",)])
    p.op("act", lambda e: e.activation(out=var_row, in_=m_row, func=AF.Square),
         reads=[("m_row",)], writes=[("var_row",)])
    p.op("dve", lambda e: e.scalar_tensor_tensor(out=var_row, in0=pstatB[0:1, :], scalar=inv_n,
                                                 in1=var_row, op0=OP.mult, op1=OP.subtract),
         reads=[("pstatB",), ("var_row",)], writes=[("var_row",)])
    p.op("act", lambda e: e.activation(out=rstd_row, in_=var_row, func=AF.Ln, bias=cst[0:1, 2:3]),
         reads=[("var_row",), ("cst",)], writes=[("rstd_row",)])
    p.op("act", lambda e: e.activation(out=rstd_row, in_=rstd_row, func=AF.Exp, scale=cst[0:1, 3:4]),
         reads=[("rstd_row",), ("cst",)], writes=[("rstd_row",)])
    p.op("dve", lambda e: e.tensor_tensor(out=tr_row, in0=m_row, in1=rstd_row, op=OP.mult),
         reads=[("m_row",), ("rstd_row",)], writes=[("tr_row",)])
    p.op("pe", lambda e: e.matmul(out=pbc1[:, 0:width], lhsT=ones1f[0:1, :], rhs=tr_row,
                                  start=True, stop=True),
         reads=[("tr_row",), ("ones1f",)], writes=[("pbc1",)])
    p.op("pe", lambda e: e.matmul(out=pbc2[:, 0:width], lhsT=ones1f[0:1, :], rhs=rstd_row,
                                  start=True, stop=True),
         reads=[("rstd_row",), ("ones1f",)], writes=[("pbc2",)])
    for t in range(ntiles):
        a, k = src[t]
        d, dk = dst[t]
        s = scr[t % 2]
        sk = ("lnscr", t % 2)
        p.op("dve", lambda e, a=a, s=s: e.tensor_tensor(out=s[:], in0=a, in1=pbc2[:, 0:width], op=OP.mult),
             reads=[k, ("pbc2",)], writes=[sk])
        if gb is None:
            p.op("dve", lambda e, s=s, d=d: e.tensor_tensor(out=d, in0=s[:], in1=pbc1[:, 0:width], op=OP.subtract),
                 reads=[sk, ("pbc1",)], writes=[dk])
        else:
            p.op("dve", lambda e, s=s: e.tensor_tensor(out=s[:], in0=s[:], in1=pbc1[:, 0:width], op=OP.subtract),
                 reads=[sk, ("pbc1",)], writes=[sk])
            g_ap, b_ap = gb[t]
            p.op("dve", lambda e, s=s, d=d, g_ap=g_ap, b_ap=b_ap: e.tensor_scalar(
                out=d, in0=s[:], scalar1=g_ap, scalar2=b_ap, op0=OP.mult, op1=OP.add),
                 reads=[sk] + list(gbkeys), writes=[dk])


def emit_weight_convert(p, env, kdram, wf_ap, n_ktiles, n_mtiles, mode, tag):
    cva, cvb, cvw = env["cva"], env["cvb"], env["cvw"]
    width = n_mtiles * 128
    HW = min(width, 512)
    mth = HW // 128
    for kt in range(n_ktiles):
        for h in range(width // HW):
            c0 = h * HW
            if mode == "fold":
                p.dma("pool", out=cva[:, 0:HW], in_=kdram[kt * 128:(kt + 1) * 128, c0:c0 + HW],
                      stream="cva", reads=[("cva_rd",)], writes=[("cva",)])
                p.dma("pool", out=cvb[:, 0:HW], in_=kdram[E + kt * 128:E + (kt + 1) * 128, c0:c0 + HW],
                      stream="cvb", reads=[("cvb_rd",)], writes=[("cvb",)])
                p.op("dve", lambda e: e.tensor_tensor(out=cvw[:, 0:HW], in0=cva[:, 0:HW],
                                                      in1=cvb[:, 0:HW], op=OP.subtract),
                     reads=[("cva",), ("cvb",), ("cvw_rd",)], writes=[("cvw",), ("cva_rd",), ("cvb_rd",)])
            else:
                p.dma("pool", out=cva[:, 0:HW], in_=kdram[kt * 128:(kt + 1) * 128, c0:c0 + HW],
                      stream="cva", reads=[("cva_rd",)], writes=[("cva",)])
                p.op("dve", lambda e: e.tensor_scalar(out=cvw[:, 0:HW], in0=cva[:, 0:HW],
                                                      scalar1=2.0, scalar2=-1.0, op0=OP.mult, op1=OP.add),
                     reads=[("cva",), ("cvw_rd",)], writes=[("cvw",), ("cva_rd",)])
            for mtl in range(mth):
                mt = h * mth + mtl
                last = (mtl == mth - 1)
                p.dma("sp", out=wf_ap[mt, kt], in_=cvw[:, mtl * 128:(mtl + 1) * 128],
                      stream="wfst", reads=[("cvw",)],
                      writes=[(tag, mt, kt), ("cvw_rd",)] if last else [(tag, mt, kt)])


def emit_ffn(p, env, nxb_tiles, res_tiles, wf0, wf1, wf2, n1k, n1m, n2m, n3m):
    emit_ffn_l1(p, env, nxb_tiles, wf0, n1k, n1m)
    emit_ffn_l2(p, env, wf1, n1m, n2m)
    emit_ffn_l3(p, env, res_tiles, wf2, n2m, n3m)


def emit_ffn_l1(p, env, nxb_tiles, wf0, n1k, n1m):
    wbuf0 = env["wbuf0"]
    h1b = env["h1b"]
    pL = env["pL"]
    s0x4 = env["s0x4"]
    for mt in range(n1m):
        wb = wbuf0[mt % len(wbuf0)]
        wk = ("wbuf0", mt % len(wbuf0))
        p.dma("sp", out=wb[:].rearrange("p (k f) -> p k f", k=n1k), in_=wf0[mt],
              stream="wb0_%d" % (mt % len(wbuf0)), reads=[wk + ("free",), ("wfdone", 0)], writes=[wk])
        ps = pL[mt % 4]
        pk = ("pL", mt % 4)
        for k in range(n1k):
            a, ak = nxb_tiles[k]
            last = (k == n1k - 1)
            p.op("pe", lambda e, wb=wb, k=k, a=a, ps=ps, s=(k == 0), t=last: e.matmul(
                out=ps[:], lhsT=wb[:, k * 128:(k + 1) * 128], rhs=a, start=s, stop=t),
                 reads=[wk, ak], writes=[pk, wk + ("free",)] if last else [pk])
        p.op("act", lambda e, ps=ps, mt=mt: e.activation(
            out=h1b[:, mt * CH:(mt + 1) * CH], in_=ps[:], func=AF.Relu, scale=s0x4[:]),
             reads=[pk, ("s0x4",)], writes=[("h1b", mt)])


def emit_ffn_l2(p, env, wf1, n1m, n2m):
    wbuf1 = env["wbuf1"]
    h1b, h2b = env["h1b"], env["h2b"]
    pL = env["pL"]
    s1x2 = env["s1x2"]
    for mt in range(n2m):
        wb = wbuf1[mt % len(wbuf1)]
        wk = ("wbuf1", mt % len(wbuf1))
        p.dma("sp", out=wb[:].rearrange("p (k f) -> p k f", k=n1m), in_=wf1[mt],
              stream="wb1_%d" % (mt % len(wbuf1)), reads=[wk + ("free",), ("wfdone", 1)], writes=[wk])
        ps = pL[mt % 4]
        pk = ("pL", mt % 4)
        for k in range(n1m):
            last = (k == n1m - 1)
            p.op("pe", lambda e, wb=wb, k=k, ps=ps, s=(k == 0), t=last: e.matmul(
                out=ps[:], lhsT=wb[:, k * 128:(k + 1) * 128],
                rhs=h1b[:, k * CH:(k + 1) * CH], start=s, stop=t),
                 reads=[wk, ("h1b", k)], writes=[pk, wk + ("free",)] if last else [pk])
        p.op("act", lambda e, ps=ps, mt=mt: e.activation(
            out=h2b[:, mt * CH:(mt + 1) * CH], in_=ps[:], func=AF.Relu, scale=s1x2[:]),
             reads=[pk, ("s1x2",)], writes=[("h2b", mt)])


def emit_ffn_l3(p, env, res_tiles, wf2, n2m, n3m):
    wbuf2 = env["wbuf2"]
    h2b = env["h2b"]
    pL = env["pL"]
    halfs2 = env["halfs2"]
    for mt in range(n3m):
        wb = wbuf2[mt % len(wbuf2)]
        wk = ("wbuf2", mt % len(wbuf2))
        p.dma("sp", out=wb[:].rearrange("p (k f) -> p k f", k=n2m), in_=wf2[mt],
              stream="wb2_%d" % (mt % len(wbuf2)), reads=[wk + ("free",), ("wfdone", 2)], writes=[wk])
        ps = pL[mt % 4]
        pk = ("pL", mt % 4)
        for k in range(n2m):
            last = (k == n2m - 1)
            p.op("pe", lambda e, wb=wb, k=k, ps=ps, s=(k == 0), t=last: e.matmul(
                out=ps[:], lhsT=wb[:, k * 128:(k + 1) * 128],
                rhs=h2b[:, k * CH:(k + 1) * CH], start=s, stop=t),
                 reads=[wk, ("h2b", k)], writes=[pk, wk + ("free",)] if last else [pk])
        r, rk = res_tiles[mt]
        p.op("dve", lambda e, ps=ps, r=r: e.scalar_tensor_tensor(
            out=r, in0=ps[:], scalar=halfs2[:], in1=r, op0=OP.mult, op1=OP.add),
             reads=[pk, rk, ("halfs2",)], writes=[rk])


def make_ffn_env(nc, stack, env, nwb=2):
    sb = lambda name, shape, dt: stack.enter_context(nc.sbuf_tensor(name, shape, dt))
    ps = lambda name, shape: stack.enter_context(nc.psum_tensor(name, shape, F32))
    env["cst"] = sb("cst", [128, 8], F32)
    env["onesb"] = sb("onesb", [128, 1], BF16)
    env["ones1f"] = sb("ones1f", [1, 128], F32)
    env["cva"] = sb("cva", [128, 2048], BF16)
    env["cvb"] = sb("cvb", [128, 2048], BF16)
    env["cvw"] = sb("cvw", [128, 2048], BF16)
    env["wbuf0"] = [sb("wbuf0_%d" % i, [128, 8 * 128], BF16) for i in range(3)]
    env["wbuf1"] = [sb("wbuf1_%d" % i, [128, 32 * 128], BF16) for i in range(3)]
    env["wbuf2"] = [sb("wbuf2_%d" % i, [128, 32 * 128], BF16) for i in range(nwb)]
    env["h1b"] = sb("h1b", [128, 32 * CH], BF16)
    env["h2b"] = sb("h2b", [128, 32 * CH], BF16)
    env["pL"] = [ps("pL%d" % i, [128, CH]) for i in range(4)]
    env["s0x4"] = sb("s0x4", [128, 1], F32)
    env["s1x2"] = sb("s1x2", [128, 1], F32)
    env["halfs2"] = sb("halfs2", [128, 1], F32)
    env["lnxb"] = sb("lnxb", [128, 4 * CH], BF16)
    env["lnsq"] = sb("lnsq", [128, 4 * CH], BF16)
    env["lnscr"] = [sb("lnscr%d" % i, [128, CH], F32) for i in range(2)]
    env["rows"] = sb("rows", [1, 4 * CH], F32)
    env["pstatA"] = ps("pstatA", [1, CH])
    env["pstatB"] = ps("pstatB", [1, CH])
    env["pbc1"] = ps("pbc1", [128, CH])
    env["pbc2"] = ps("pbc2", [128, CH])


def emit_consts_scales(p, env, s_s0, s_s1, s_s2):
    cst, onesb, ones1f = env["cst"], env["onesb"], env["ones1f"]
    p.op("dve", lambda e: e.memset(cst[:, 0:1], 1.0), writes=[("cst",)])
    p.op("dve", lambda e: e.memset(cst[:, 1:2], -1.0), writes=[("cst",)])
    p.op("dve", lambda e: e.memset(cst[:, 2:3], EPS), writes=[("cst",)])
    p.op("dve", lambda e: e.memset(cst[:, 3:4], -0.5), writes=[("cst",)])
    p.op("dve", lambda e: e.memset(onesb[:], 1.0), writes=[("onesb",)])
    p.op("dve", lambda e: e.memset(ones1f[:], 1.0), writes=[("ones1f",)])
    for name, src, f in (("s0x4", s_s0, 4.0), ("s1x2", s_s1, 2.0), ("halfs2", s_s2, 0.5)):
        dst = env[name]
        p.op("dve", lambda e, dst=dst, src=src, f=f: e.tensor_scalar(
            out=dst[:], in0=src[:], scalar1=f, scalar2=0.0, op0=OP.mult, op1=OP.add),
             reads=[(name, "in")], writes=[(name,)])


def setup_ffn_weights(nc, p, env, k0, k1, k2):
    wf0d = nc.dram_tensor("wf0d", [32, 8, 128, 128], BF16).ap()
    wf1d = nc.dram_tensor("wf1d", [32, 32, 128, 128], BF16).ap()
    wf2d = nc.dram_tensor("wf2d", [8, 32, 128, 128], BF16).ap()
    emit_weight_convert(p, env, k0, wf0d, 8, 32, "fold", "wf0")
    p.op("dve", lambda e: e.memset(env["cst"][:, 4:5], 0.0),
         reads=[("wf0", 31, 7)], writes=[("wfdone", 0)])
    emit_weight_convert(p, env, k1, wf1d, 32, 32, "pm1", "wf1")
    p.op("dve", lambda e: e.memset(env["cst"][:, 5:6], 0.0),
         reads=[("wf1", 31, 31)], writes=[("wfdone", 1)])
    emit_weight_convert(p, env, k2, wf2d, 32, 8, "pm1", "wf2")
    p.op("dve", lambda e: e.memset(env["cst"][:, 6:7], 0.0),
         reads=[("wf2", 7, 31)], writes=[("wfdone", 2)])
    wf0 = [wf0d[mt].rearrange("k p f -> p k f") for mt in range(32)]
    wf1 = [wf1d[mt].rearrange("k p f -> p k f") for mt in range(32)]
    wf2 = [wf2d[mt].rearrange("k p f -> p k f") for mt in range(8)]
    return wf0, wf1, wf2



def build_launch1(T=TPC):
    nc = bass.Bass("TRN2", target_bir_lowering=False, debug=False)
    xT = nc.dram_tensor("xT", [E, T], F32, kind="ExternalInput").ap()
    k0 = nc.dram_tensor("k0", [2 * E, 4096], U8, kind="ExternalInput").ap()
    k1 = nc.dram_tensor("k1", [4096, 4096], U8, kind="ExternalInput").ap()
    k2 = nc.dram_tensor("k2", [4096, E], U8, kind="ExternalInput").ap()
    s0r = nc.dram_tensor("s0r", [128, 1], F32, kind="ExternalInput").ap()
    s1r = nc.dram_tensor("s1r", [128, 1], F32, kind="ExternalInput").ap()
    s2r = nc.dram_tensor("s2r", [128, 1], F32, kind="ExternalInput").ap()
    lng = nc.dram_tensor("lng", [128, 8], F32, kind="ExternalInput").ap()
    lnb = nc.dram_tensor("lnb", [128, 8], F32, kind="ExternalInput").ap()
    xn_out = nc.dram_tensor("xn_out", [E, T], BF16, kind="ExternalOutput").ap()
    x2p_out = nc.dram_tensor("x2p_out", [E, T], F32, kind="ExternalOutput").ap()

    with contextlib.ExitStack() as stack:
        env = {}
        sb = lambda name, shape, dt: stack.enter_context(nc.sbuf_tensor(name, shape, dt))
        make_ffn_env(nc, stack, env)
        xc = [sb("xc%d" % t, [128, CH], F32) for t in range(8)]
        xin = [sb("xin%d" % t, [128, CH], F32) for t in range(8)]
        nxb = [sb("nxb%d" % t, [128, CH], BF16) for t in range(8)]
        s_s0 = sb("s_s0", [128, 1], F32)
        s_s1 = sb("s_s1", [128, 1], F32)
        s_s2 = sb("s_s2", [128, 1], F32)
        s_g = sb("s_g", [128, 8], F32)
        s_b = sb("s_b", [128, 8], F32)

        p = Prog(nc, stack)
        p.dma("sp", out=s_s0[:], in_=s0r, stream="ss0", writes=[("s0x4", "in")])
        p.dma("sp", out=s_s1[:], in_=s1r, stream="ss1", writes=[("s1x2", "in")])
        p.dma("sp", out=s_s2[:], in_=s2r, stream="ss2", writes=[("halfs2", "in")])
        p.dma("sp", out=s_g[:], in_=lng, stream="sgv", writes=[("s_g",)])
        p.dma("sp", out=s_b[:], in_=lnb, stream="sbv", writes=[("s_gb",)])
        emit_consts_scales(p, env, s_s0, s_s1, s_s2)
        wf0, wf1, wf2 = setup_ffn_weights(nc, p, env, k0, k1, k2)

        nch = T // CH
        store_keys = []
        for c in range(nch):
            c0 = c * CH
            for t in range(8):
                p.dma("sp", out=xc[t][:], in_=xT[t * 128:(t + 1) * 128, c0:c0 + CH],
                      stream="xc%d" % t, writes=[("xc", t)])
            emit_ln(p, env, [(xc[t][:], ("xc", t)) for t in range(8)],
                    [(nxb[t][:], ("nxb", t)) for t in range(8)], 8, CH, gb=None)
            emit_ffn(p, env, [(nxb[t][:], ("nxb", t)) for t in range(8)],
                     [(xc[t][:], ("xc", t)) for t in range(8)],
                     wf0, wf1, wf2, 8, 32, 32, 8)
            emit_ln(p, env, [(xc[t][:], ("xc", t)) for t in range(8)],
                    [(xin[t][:], ("xin", t)) for t in range(8)], 8, CH, gb=None)
            for t in range(8):
                p.op("dve", lambda e, t=t: e.tensor_tensor(out=xc[t][:], in0=xc[t][:], in1=xin[t][:], op=OP.add),
                     reads=[("xc", t), ("xin", t)], writes=[("xc", t)])
                skey = ("x2p_st", c, t)
                p.dma("sp", out=x2p_out[t * 128:(t + 1) * 128, c0:c0 + CH], in_=xc[t][:],
                      stream="x2pst", reads=[("xc", t)], writes=[skey])
                store_keys.append(skey)
            emit_ln(p, env, [(xin[t][:], ("xin", t)) for t in range(8)],
                    [(nxb[t][:], ("nxb", t)) for t in range(8)], 8, CH,
                    gb=[(s_g[:, t:t + 1], s_b[:, t:t + 1]) for t in range(8)], gbkeys=[("s_g",), ("s_gb",)])
            for t in range(8):
                skey = ("xn_st", c, t)
                p.dma("sp", out=xn_out[t * 128:(t + 1) * 128, c0:c0 + CH], in_=nxb[t][:],
                      stream="xnst", reads=[("nxb", t)], writes=[skey])
                store_keys.append(skey)
        p.op("pool", lambda e: e.nop(nofuse=True), reads=store_keys[-16:])

        with nc.Block() as block:
            p.emit(block)
    return nc



def build_launch2(Lb=SEQ, SCH=1024):
    nc = bass.Bass("TRN2", target_bir_lowering=False, debug=False)
    xn_in = nc.dram_tensor("xn_in", [BATCH * E, Lb], BF16, kind="ExternalInput").ap()
    winT = nc.dram_tensor("winT", [E, 2 * DSL], F32, kind="ExternalInput").ap()
    wxT = nc.dram_tensor("wxT", [DSL, 96], F32, kind="ExternalInput").ap()
    wdtT = nc.dram_tensor("wdtT", [DTR, DSL], F32, kind="ExternalInput").ap()
    cw = nc.dram_tensor("cw", [128, 2 * DCONV], F32, kind="ExternalInput").ap()
    cb = nc.dram_tensor("cb", [128, 2], F32, kind="ExternalInput").ap()
    dtb = nc.dram_tensor("dtb", [128, 2], F32, kind="ExternalInput").ap()
    alog = nc.dram_tensor("alog", [128, 2 * NST], F32, kind="ExternalInput").ap()
    dsl_in = nc.dram_tensor("dsl", [128, 2], F32, kind="ExternalInput").ap()
    yg_out = nc.dram_tensor("yg_out", [BATCH * DSL, Lb], BF16, kind="ExternalOutput").ap()

    ar_in = nc.dram_tensor("ar_in", [96, SCH], F32).ap()
    ar_out = nc.dram_tensor("ar_out", [96, SCH], F32).ap()

    NT = DSL // 128
    NJ = SCH // 512

    with contextlib.ExitStack() as stack:
        sb = lambda name, shape, dt: stack.enter_context(nc.sbuf_tensor(name, shape, dt))
        psf = lambda name, shape: stack.enter_context(nc.psum_tensor(name, shape, F32))
        xnb = [sb("xnb%d" % t, [128, SCH], BF16) for t in range(8)]
        winsb = sb("winsb", [128, 8 * 4 * 128], BF16)
        wxsb = sb("wxsb", [128, 2 * 96], BF16)
        wdtsb = sb("wdtsb", [DTR, 2 * 128], BF16)
        s_cw = sb("s_cw", [128, 2 * DCONV], F32)
        s_cb = sb("s_cb", [128, 2], F32)
        s_dtb = sb("s_dtb", [128, 2], F32)
        s_aneg = sb("s_aneg", [128, 2 * NST], F32)
        s_d = sb("s_d", [128, 2], F32)
        cst = sb("cstl2", [128, 2], F32)
        u_raw = [sb("u_raw%d" % t, [128, DCONV - 1 + SCH], F32) for t in range(NT)]
        z = [sb("z%d" % t, [128, SCH], F32) for t in range(NT)]
        conv = [sb("conv%d" % t, [128, SCH], F32) for t in range(NT)]
        us = [sb("us%d" % t, [128, SCH], F32) for t in range(NT)]
        ub = [sb("ub%d" % t, [128, SCH], BF16) for t in range(NT)]
        dbcs = sb("dbcs", [96, SCH], F32)
        dtinb = sb("dtinb", [DTR, SCH], BF16)
        dt = [sb("dt%d" % t, [128, SCH], F32) for t in range(NT)]
        dtu = [sb("dtu%d" % t, [128, SCH], F32) for t in range(NT)]
        sp1 = sb("sp1", [128, 512], F32)
        sp2 = sb("sp2", [128, 512], F32)
        bcB = sb("bcB", [128, SCH], F32)
        bcC = sb("bcC", [128, SCH], F32)
        dA = sb("dA", [128, SCH], F32)
        dBu = sb("dBu", [128, SCH], F32)
        hb = sb("hb", [128, SCH], F32)
        hcb = sb("hcb", [128, SCH], F32)
        y = [sb("y%d" % t, [128, SCH], F32) for t in range(NT)]
        ygb = sb("ygb", [128, SCH], BF16)
        hcar = [sb("hcar%d" % t, [128, NST], F32) for t in range(NT)]
        pW = [psf("pW%d" % i, [128, 512]) for i in range(4)]

        p = Prog(nc, stack)
        for k in range(8):
            p.dma("pool", out=winsb[:, k * 512:(k + 1) * 512], in_=winT[k * 128:(k + 1) * 128, :],
                  stream="winl", writes=[("winsb",)])
        for kt in range(NT):
            p.dma("pool", out=wxsb[:, kt * 96:(kt + 1) * 96], in_=wxT[kt * 128:(kt + 1) * 128, :],
                  stream="wxl", writes=[("wxsb",)])
        p.dma("pool", out=wdtsb[:], in_=wdtT, stream="wdtl", writes=[("wdtsb",)])
        p.dma("sp", out=s_cw[:], in_=cw, stream="scw", writes=[("s_cw",)])
        p.dma("sp", out=s_cb[:], in_=cb, stream="scb", writes=[("s_cb",)])
        p.dma("sp", out=s_dtb[:], in_=dtb, stream="sdtb", writes=[("s_dtb",)])
        p.dma("sp", out=s_aneg[:], in_=alog, stream="saneg", writes=[("s_aneg",)])
        p.dma("sp", out=s_d[:], in_=dsl_in, stream="sdv", writes=[("s_d",)])
        p.op("dve", lambda e: e.memset(cst[:, 0:1], 1.0), writes=[("cst2",)])
        p.op("dve", lambda e: e.memset(cst[:, 1:2], -1.0), writes=[("cst2",)])
        p.op("act", lambda e: e.activation(out=s_aneg[:], in_=s_aneg[:], func=AF.Exp),
             reads=[("s_aneg",)], writes=[("s_aneg",)])
        p.op("dve", lambda e: e.tensor_scalar(out=s_aneg[:], in0=s_aneg[:], scalar1=-1.0,
                                              scalar2=0.0, op0=OP.mult, op1=OP.add),
             reads=[("s_aneg",)], writes=[("s_aneg",)])

        store_keys = []
        for b in range(BATCH):
            for t in range(NT):
                p.op("dve", lambda e, t=t: e.memset(hcar[t][:], 0.0),
                     reads=[("hcar_rd", t)],
                     writes=[("hcar", t, n) for n in range(NST)])
            for ci in range(Lb // SCH):
                c0 = ci * SCH
                for t8 in range(8):
                    p.dma("sp", out=xnb[t8][:], in_=xn_in[b * E + t8 * 128:b * E + (t8 + 1) * 128, c0:c0 + SCH],
                          stream="xnb%d" % t8, writes=[("xnb", t8)])
                for t in range(NT):
                    if ci == 0:
                        p.op("dve", lambda e, t=t: e.memset(u_raw[t][:, 0:DCONV - 1], 0.0),
                             reads=[("u_raw", t)], writes=[("ucar", t)])
                    else:
                        p.op("dve", lambda e, t=t: e.tensor_copy(
                            out=u_raw[t][:, 0:DCONV - 1], in_=u_raw[t][:, SCH:SCH + DCONV - 1]),
                             reads=[("u_raw", t)], writes=[("ucar", t)])
                for mt in range(4):
                    for j in range(NJ):
                        pp = pW[(mt * NJ + j) % 4]
                        pk = ("pW", (mt * NJ + j) % 4)
                        for k in range(8):
                            p.op("pe", lambda e, pp=pp, k=k, mt=mt, j=j, s=(k == 0), t_=(k == 7): e.matmul(
                                out=pp[:], lhsT=winsb[:, k * 512 + mt * 128:k * 512 + (mt + 1) * 128],
                                rhs=xnb[k][:, j * 512:(j + 1) * 512], start=s, stop=t_),
                                 reads=[("winsb",), ("xnb", k)], writes=[pk])
                        if mt < NT:
                            t = mt
                            p.op("act", lambda e, pp=pp, t=t, j=j: e.activation(
                                out=u_raw[t][:, DCONV - 1 + j * 512:DCONV - 1 + (j + 1) * 512],
                                in_=pp[:], func=AF.Copy),
                                 reads=[pk, ("ucar", t)], writes=[("u_raw", t)])
                        else:
                            t = mt - NT
                            p.op("act", lambda e, pp=pp, t=t, j=j: e.activation(
                                out=z[t][:, j * 512:(j + 1) * 512], in_=pp[:], func=AF.Copy),
                                 reads=[pk, ("z_free", t)], writes=[("z", t)])
                for t in range(NT):
                    p.op("dve", lambda e, t=t: e.tensor_scalar(
                        out=conv[t][:], in0=u_raw[t][:, 0:SCH], scalar1=s_cw[:, t * DCONV:t * DCONV + 1],
                        scalar2=0.0, op0=OP.mult, op1=OP.add),
                         reads=[("u_raw", t), ("ucar", t), ("s_cw",), ("conv_rd", t)], writes=[("conv", t)])
                    for kk in range(1, DCONV):
                        p.op("dve", lambda e, t=t, kk=kk: e.scalar_tensor_tensor(
                            out=conv[t][:], in0=u_raw[t][:, kk:kk + SCH],
                            scalar=s_cw[:, t * DCONV + kk:t * DCONV + kk + 1],
                            in1=conv[t][:], op0=OP.mult, op1=OP.add),
                             reads=[("u_raw", t), ("conv", t), ("s_cw",)], writes=[("conv", t)])
                    p.op("act", lambda e, t=t: e.activation(
                        out=us[t][:], in_=conv[t][:], func=AF.Silu, bias=s_cb[:, t:t + 1]),
                         reads=[("conv", t), ("s_cb",), ("us_free", t)], writes=[("us", t), ("conv_rd", t)])
                    p.op("act", lambda e, t=t: e.activation(out=ub[t][:], in_=us[t][:], func=AF.Copy),
                         reads=[("us", t)], writes=[("ub", t)])
                for j in range(NJ):
                    pp = pW[j % 2]
                    pk = ("pW", j % 2)
                    for kt in range(NT):
                        p.op("pe", lambda e, pp=pp, kt=kt, j=j, s=(kt == 0), t_=(kt == NT - 1): e.matmul(
                            out=pp[0:96, :], lhsT=wxsb[:, kt * 96:(kt + 1) * 96],
                            rhs=ub[kt][:, j * 512:(j + 1) * 512], start=s, stop=t_),
                             reads=[("wxsb",), ("ub", kt)], writes=[pk])
                    p.op("act", lambda e, pp=pp, j=j: e.activation(
                        out=dbcs[0:96, j * 512:(j + 1) * 512], in_=pp[0:96, :], func=AF.Copy),
                         reads=[pk, ("dbcs_rd",)], writes=[("dbcs",)])
                p.dma("sp", out=ar_in, in_=dbcs[0:96, :], stream="arin", reads=[("dbcs",)], writes=[("ar_in",), ("dbcs_rd",)])
                p.collective(lambda e: e.collective_compute(
                    "AllReduce", OP.add, replica_groups=[list(range(NCORES))],
                    ins=[ar_in.opt()], outs=[ar_out.opt()]),
                     reads=[("ar_in",)], writes=[("ar_out",)])
                p.dma("pool", out=dtinb[:], in_=ar_out[0:DTR, :], stream="dtinb",
                      reads=[("ar_out",)], writes=[("dtinb",)])
                for mt in range(NT):
                    for j in range(NJ):
                        pp = pW[2 + (j % 2)]
                        pk = ("pW", 2 + (j % 2))
                        p.op("pe", lambda e, pp=pp, mt=mt, j=j: e.matmul(
                            out=pp[:], lhsT=wdtsb[:, mt * 128:(mt + 1) * 128],
                            rhs=dtinb[:, j * 512:(j + 1) * 512], start=True, stop=True),
                             reads=[("wdtsb",), ("dtinb",)], writes=[pk])
                        p.op("act", lambda e, pp=pp, mt=mt, j=j: e.activation(
                            out=dt[mt][:, j * 512:(j + 1) * 512], in_=pp[:], func=AF.Relu,
                            bias=s_dtb[:, mt:mt + 1]),
                             reads=[pk, ("s_dtb",)], writes=[("dt", mt)])
                        p.op("act", lambda e, pp=pp, mt=mt: e.activation(
                            out=sp1[:], in_=pp[:], func=AF.Abs, bias=s_dtb[:, mt:mt + 1]),
                             reads=[pk, ("s_dtb",), ("sp1_rd",)], writes=[("sp1",)])
                        p.op("act", lambda e: e.activation(out=sp2[:], in_=sp1[:], func=AF.Exp,
                                                           scale=cst[:, 1:2]),
                             reads=[("sp1",), ("cst2",), ("sp2_rd",)], writes=[("sp2",), ("sp1_rd",)])
                        p.op("act", lambda e: e.activation(out=sp1[:], in_=sp2[:], func=AF.Ln,
                                                           bias=cst[:, 0:1]),
                             reads=[("sp2",), ("cst2",), ("sp1_rd",)], writes=[("sp1",), ("sp2_rd",)])
                        p.op("dve", lambda e, mt=mt, j=j: e.tensor_tensor(
                            out=dt[mt][:, j * 512:(j + 1) * 512],
                            in0=dt[mt][:, j * 512:(j + 1) * 512], in1=sp1[:], op=OP.add),
                             reads=[("dt", mt), ("sp1",)], writes=[("dt", mt), ("sp1_rd",)])
                for t in range(NT):
                    p.op("dve", lambda e, t=t: e.tensor_tensor(out=dtu[t][:], in0=dt[t][:], in1=us[t][:], op=OP.mult),
                         reads=[("dt", t), ("us", t)], writes=[("dtu", t)])
                for n in range(NST):
                    p.dma("sp", out=bcB[:], in_=ar_out[DTR + n, :].partition_broadcast(128),
                          stream="bcB", reads=[("ar_out",)], writes=[("bcB",)])
                    p.dma("sp", out=bcC[:], in_=ar_out[DTR + NST + n, :].partition_broadcast(128),
                          stream="bcC", reads=[("ar_out",)], writes=[("bcC",)])
                    for t in range(NT):
                        p.op("act", lambda e, t=t, n=n: e.activation(
                            out=dA[:], in_=dt[t][:], func=AF.Exp, scale=s_aneg[:, t * NST + n:t * NST + n + 1]),
                             reads=[("dt", t), ("s_aneg",), ("dA_rd",)], writes=[("dA",)])
                        p.op("dve", lambda e, t=t: e.tensor_tensor(out=dBu[:], in0=dtu[t][:], in1=bcB[:], op=OP.mult),
                             reads=[("dtu", t), ("bcB",), ("dBu_rd",)], writes=[("dBu",)])
                        p.op("dve", lambda e, t=t, n=n: e.tensor_tensor_scan(
                            out=hb[:], data0=dA[:], data1=dBu[:], initial=hcar[t][:, n:n + 1],
                            op0=OP.mult, op1=OP.add),
                             reads=[("dA",), ("dBu",), ("hcar", t, n), ("hb_rd",)],
                             writes=[("hb",), ("dA_rd",), ("dBu_rd",)])
                        p.op("act", lambda e, t=t, n=n: e.activation(
                            out=hcar[t][:, n:n + 1], in_=hb[:, SCH - 1:SCH], func=AF.Copy),
                             reads=[("hb",)], writes=[("hcar", t, n), ("hcar_rd", t)])
                        if n == 0:
                            p.op("dve", lambda e, t=t: e.tensor_tensor(out=y[t][:], in0=hb[:], in1=bcC[:], op=OP.mult),
                                 reads=[("hb",), ("bcC",), ("y_free", t)], writes=[("y", t), ("hb_rd",)])
                        else:
                            p.op("dve", lambda e, t=t: e.tensor_tensor(out=hcb[:], in0=hb[:], in1=bcC[:], op=OP.mult),
                                 reads=[("hb",), ("bcC",), ("hcb_rd",)], writes=[("hcb",), ("hb_rd",)])
                            p.op("dve", lambda e, t=t: e.tensor_tensor(out=y[t][:], in0=y[t][:], in1=hcb[:], op=OP.add),
                                 reads=[("y", t), ("hcb",)], writes=[("y", t), ("hcb_rd",)])

                for t in range(NT):
                    p.op("dve", lambda e, t=t: e.scalar_tensor_tensor(
                        out=y[t][:], in0=us[t][:], scalar=s_d[:, t:t + 1], in1=y[t][:],
                        op0=OP.mult, op1=OP.add),
                         reads=[("us", t), ("y", t), ("s_d",)], writes=[("y", t), ("us_free", t)])
                    p.op("act", lambda e, t=t: e.activation(out=z[t][:], in_=z[t][:], func=AF.Silu),
                         reads=[("z", t)], writes=[("z", t)])
                    p.op("dve", lambda e, t=t: e.tensor_tensor(out=ygb[:], in0=y[t][:], in1=z[t][:], op=OP.mult),
                         reads=[("y", t), ("z", t), ("ygb_rd",)], writes=[("ygb",), ("y_free", t), ("z_free", t)])
                    skey = ("yg_st", b, ci, t)
                    p.dma("sp", out=yg_out[b * DSL + t * 128:b * DSL + (t + 1) * 128, c0:c0 + SCH],
                          in_=ygb[:], stream="ygst", reads=[("ygb",)], writes=[skey, ("ygb_rd",)])
                    store_keys.append(skey)

        p.op("pool", lambda e: e.nop(nofuse=True), reads=store_keys[-2 * NT:])
        with nc.Block() as block:
            p.emit(block)
    return nc



def build_launch3(T=TPC):
    nc = bass.Bass("TRN2", target_bir_lowering=False, debug=False)
    yg_in = nc.dram_tensor("yg_in", [DIN, T], BF16, kind="ExternalInput").ap()
    x2p_in = nc.dram_tensor("x2p_in", [E, T], F32, kind="ExternalInput").ap()
    woutT = nc.dram_tensor("woutT", [DIN, E], F32, kind="ExternalInput").ap()
    k0 = nc.dram_tensor("k0", [2 * E, 4096], U8, kind="ExternalInput").ap()
    k1 = nc.dram_tensor("k1", [4096, 4096], U8, kind="ExternalInput").ap()
    k2 = nc.dram_tensor("k2", [4096, E], U8, kind="ExternalInput").ap()
    s0r = nc.dram_tensor("s0r", [128, 1], F32, kind="ExternalInput").ap()
    s1r = nc.dram_tensor("s1r", [128, 1], F32, kind="ExternalInput").ap()
    s2r = nc.dram_tensor("s2r", [128, 1], F32, kind="ExternalInput").ap()
    out_d = nc.dram_tensor("outT", [E, T], F32, kind="ExternalOutput").ap()

    with contextlib.ExitStack() as stack:
        env = {}
        sb = lambda name, shape, dt: stack.enter_context(nc.sbuf_tensor(name, shape, dt))
        make_ffn_env(nc, stack, env, nwb=1)
        woutsb = sb("woutsb", [128, 16 * 1024], BF16)
        ygt = [sb("ygt%d" % t, [128, CH], BF16) for t in range(16)]
        x3 = [sb("x3_%d" % t, [128, CH], F32) for t in range(8)]
        nxb = [sb("nxb%d" % t, [128, CH], BF16) for t in range(8)]
        s_s0 = sb("s_s0", [128, 1], F32)
        s_s1 = sb("s_s1", [128, 1], F32)
        s_s2 = sb("s_s2", [128, 1], F32)

        p = Prog(nc, stack)
        p.dma("sp", out=s_s0[:], in_=s0r, stream="ss0", writes=[("s0x4", "in")])
        p.dma("sp", out=s_s1[:], in_=s1r, stream="ss1", writes=[("s1x2", "in")])
        p.dma("sp", out=s_s2[:], in_=s2r, stream="ss2", writes=[("halfs2", "in")])
        emit_consts_scales(p, env, s_s0, s_s1, s_s2)
        for k in range(16):
            p.dma("pool", out=woutsb[:, k * 1024:(k + 1) * 1024], in_=woutT[k * 128:(k + 1) * 128, :],
                  stream="woutl", writes=[("woutsb",)])
        wf0, wf1, wf2 = setup_ffn_weights(nc, p, env, k0, k1, k2)

        nch = T // CH
        store_keys = []
        for c in range(nch):
            c0 = c * CH
            for t in range(16):
                p.dma("sp", out=ygt[t][:], in_=yg_in[t * 128:(t + 1) * 128, c0:c0 + CH],
                      stream="ygt%d" % t, writes=[("ygt", t)])
            for t in range(8):
                p.dma("sp", out=x3[t][:], in_=x2p_in[t * 128:(t + 1) * 128, c0:c0 + CH],
                      stream="x3l%d" % t, writes=[("x3", t)])
            for mt in range(8):
                pp = env["pL"][mt % 4]
                pk = ("pL", mt % 4)
                for k in range(16):
                    p.op("pe", lambda e, pp=pp, k=k, mt=mt, s=(k == 0), t_=(k == 15): e.matmul(
                        out=pp[:], lhsT=woutsb[:, k * 1024 + mt * 128:k * 1024 + (mt + 1) * 128],
                        rhs=ygt[k][:], start=s, stop=t_),
                         reads=[("woutsb",), ("ygt", k)], writes=[pk])
                p.op("dve", lambda e, pp=pp, mt=mt: e.scalar_tensor_tensor(
                    out=x3[mt][:], in0=pp[:], scalar=env["cst"][:, 0:1], in1=x3[mt][:],
                    op0=OP.mult, op1=OP.add),
                     reads=[pk, ("x3", mt), ("cst",)], writes=[("x3", mt)])

            emit_ln(p, env, [(x3[t][:], ("x3", t)) for t in range(8)],
                    [(nxb[t][:], ("nxb", t)) for t in range(8)], 8, CH, gb=None)
            emit_ffn(p, env, [(nxb[t][:], ("nxb", t)) for t in range(8)],
                     [(x3[t][:], ("x3", t)) for t in range(8)],
                     wf0, wf1, wf2, 8, 32, 32, 8)
            for t in range(8):
                skey = ("out_st", c, t)
                p.dma("sp", out=out_d[t * 128:(t + 1) * 128, c0:c0 + CH], in_=x3[t][:],
                      stream="outst", reads=[("x3", t)], writes=[skey])
                store_keys.append(skey)
        p.op("pool", lambda e: e.nop(nofuse=True), reads=store_keys[-8:])
        with nc.Block() as block:
            p.emit(block)
    return nc




def make_persist_env(nc, stack, env):
    sb = lambda name, shape, dt: stack.enter_context(nc.sbuf_tensor(name, shape, dt))
    env["cst"] = sb("cst", [128, 8], F32)
    env["onesb"] = sb("onesb", [128, 1], BF16)
    env["ones1f"] = sb("ones1f", [1, 128], F32)
    env["s0x4"] = sb("s0x4", [128, 1], F32)
    env["s1x2"] = sb("s1x2", [128, 1], F32)
    env["halfs2"] = sb("halfs2", [128, 1], F32)


def make_ffn_phase_env(nc, stack, env, nwb=2, tag=""):
    sb = lambda name, shape, dt: stack.enter_context(nc.sbuf_tensor(tag + name, shape, dt))
    ps = lambda name, shape: stack.enter_context(nc.psum_tensor(tag + name, shape, F32))
    env["wbuf0"] = [sb("wbuf0_%d" % i, [128, 8 * 128], BF16) for i in range(3)]
    env["wbuf1"] = [sb("wbuf1_%d" % i, [128, 32 * 128], BF16) for i in range(3)]
    env["wbuf2"] = [sb("wbuf2_%d" % i, [128, 32 * 128], BF16) for i in range(nwb)]
    env["h1b"] = sb("h1b", [128, 32 * CH], BF16)
    env["h2b"] = sb("h2b", [128, 32 * CH], BF16)
    env["pL"] = [ps("pL%d" % i, [128, CH]) for i in range(4)]
    env["lnxb"] = sb("lnxb", [128, 4 * CH], BF16)
    env["lnsq"] = sb("lnsq", [128, 4 * CH], BF16)
    env["lnscr"] = [sb("lnscr%d" % i, [128, CH], F32) for i in range(2)]
    env["rows"] = sb("rows", [1, 4 * CH], F32)
    env["pstatA"] = ps("pstatA", [1, CH])
    env["pstatB"] = ps("pstatB", [1, CH])
    env["pbc1"] = ps("pbc1", [128, CH])
    env["pbc2"] = ps("pbc2", [128, CH])


def phase_barrier(p, keys):
    for e in Prog.ENGINES:
        p.op(e, lambda en: en.nop(nofuse=True), reads=list(keys))


def build_fused(T=TPC, SCH=512):
    SCH = min(SCH, T)
    Lb = 4 * T
    NT = DSL // 128
    NJ = SCH // 512
    NCH1 = T // CH
    NQ = T // SCH
    nc = bass.Bass("TRN2", target_bir_lowering=False, debug=False)
    xT = nc.dram_tensor("xT", [E, T], F32, kind="ExternalInput").ap()
    k0 = nc.dram_tensor("k0", [2 * E, 4096], U8, kind="ExternalInput").ap()
    k1 = nc.dram_tensor("k1", [4096, 4096], U8, kind="ExternalInput").ap()
    k2 = nc.dram_tensor("k2", [4096, E], U8, kind="ExternalInput").ap()
    s0r = nc.dram_tensor("s0r", [128, 1], F32, kind="ExternalInput").ap()
    s1r = nc.dram_tensor("s1r", [128, 1], F32, kind="ExternalInput").ap()
    s2r = nc.dram_tensor("s2r", [128, 1], F32, kind="ExternalInput").ap()
    lng = nc.dram_tensor("lng", [128, 8], F32, kind="ExternalInput").ap()
    lnb = nc.dram_tensor("lnb", [128, 8], F32, kind="ExternalInput").ap()
    winT = nc.dram_tensor("winT", [E, 2 * DSL], F32, kind="ExternalInput").ap()
    wxT = nc.dram_tensor("wxT", [DSL, 96], F32, kind="ExternalInput").ap()
    wdtT = nc.dram_tensor("wdtT", [DTR, DSL], F32, kind="ExternalInput").ap()
    cw = nc.dram_tensor("cw", [128, 2 * DCONV], F32, kind="ExternalInput").ap()
    cb = nc.dram_tensor("cb", [128, 2], F32, kind="ExternalInput").ap()
    dtb = nc.dram_tensor("dtb", [128, 2], F32, kind="ExternalInput").ap()
    alog = nc.dram_tensor("alog", [128, 2 * NST], F32, kind="ExternalInput").ap()
    dsl_in = nc.dram_tensor("dsl", [128, 2], F32, kind="ExternalInput").ap()
    woutTs = nc.dram_tensor("woutTs", [DSL, E], F32, kind="ExternalInput").ap()
    out_d = nc.dram_tensor("outT", [E, T], F32, kind="ExternalOutput").ap()

    x2p_d = nc.dram_tensor("x2p_d", [E, T], F32).ap()
    ag_in = [nc.dram_tensor("ag_in%d" % k, [E, CH], BF16).ap() for k in range(NCH1)]
    ag_out = [nc.dram_tensor("ag_out%d" % k, [NCORES * E, CH], BF16).ap() for k in range(NCH1)]
    rs_in = [nc.dram_tensor("rs_in%d" % q, [NCORES * E, SCH], F32).ap() for q in range(NQ)]
    rs_out = [nc.dram_tensor("rs_out%d" % q, [E, SCH], F32).ap() for q in range(NQ)]
    ar_in = [[nc.dram_tensor("ar_in%d_%d" % (b, i), [96, SCH], F32).ap() for i in range(3)]
             for b in range(BATCH)]
    ar_out = [[nc.dram_tensor("ar_out%d_%d" % (b, i), [96, SCH], F32).ap() for i in range(3)]
              for b in range(BATCH)]

    with contextlib.ExitStack() as stack:
        env = {}
        make_persist_env(nc, stack, env)
        sbp = lambda name, shape, dt: stack.enter_context(nc.sbuf_tensor(name, shape, dt))
        s_s0 = sbp("s_s0", [128, 1], F32)
        s_s1 = sbp("s_s1", [128, 1], F32)
        s_s2 = sbp("s_s2", [128, 1], F32)
        p = Prog(nc, stack)
        p.dma("sp", out=s_s0[:], in_=s0r, stream="ss0", writes=[("s0x4", "in")])
        p.dma("sp", out=s_s1[:], in_=s1r, stream="ss1", writes=[("s1x2", "in")])
        p.dma("sp", out=s_s2[:], in_=s2r, stream="ss2", writes=[("halfs2", "in")])
        emit_consts_scales(p, env, s_s0, s_s1, s_s2)

        ph1 = contextlib.ExitStack()
        env1 = dict(env)
        make_ffn_phase_env(nc, ph1, env1, nwb=2, tag="p1")
        sb1 = lambda name, shape, dt: ph1.enter_context(nc.sbuf_tensor(name, shape, dt))
        env1["cva"] = sb1("cva", [128, 512], BF16)
        env1["cvb"] = sb1("cvb", [128, 512], BF16)
        env1["cvw"] = sb1("cvw", [128, 512], BF16)
        xc = [[sb1("xc%d_%d" % (i, t), [128, CH], F32) for t in range(8)] for i in range(2)]
        xin = [sb1("xin%d" % t, [128, CH], F32) for t in range(8)]
        nxb = [[sb1("nxb%d_%d" % (i, t), [128, CH], BF16) for t in range(8)] for i in range(2)]

        xnout = [sb1("xnout%d" % t, [128, CH], BF16) for t in range(8)]
        s_g = sb1("s_g", [128, 8], F32)
        s_b = sb1("s_b", [128, 8], F32)
        p.dma("sp", out=s_g[:], in_=lng, stream="sgv", writes=[("s_g",)])
        p.dma("sp", out=s_b[:], in_=lnb, stream="sbv", writes=[("s_gb",)])

        def p1_load(c):
            i = c % 2
            for t in range(8):
                p.dma("pool", out=xc[i][t][:], in_=xT[t * 128:(t + 1) * 128, c * CH:(c + 1) * CH],
                      stream="xc%d" % t, writes=[("xc", i, t)])

        def p1_ln1(c):
            i = c % 2
            emit_ln(p, env1, [(xc[i][t][:], ("xc", i, t)) for t in range(8)],
                    [(nxb[i][t][:], ("nxb", i, t)) for t in range(8)], 8, CH, gb=None)

        def p1_tail1(c):
            i = c % 2
            emit_ln(p, env1, [(xc[i][t][:], ("xc", i, t)) for t in range(8)],
                    [(xin[t][:], ("xin", t)) for t in range(8)], 8, CH, gb=None)
            for t in range(8):
                p.op("dve", lambda e, i=i, t=t: e.tensor_tensor(
                    out=xc[i][t][:], in0=xc[i][t][:], in1=xin[t][:], op=OP.add),
                     reads=[("xc", i, t), ("xin", t)], writes=[("xc", i, t)])
                p.dma("act", out=x2p_d[t * 128:(t + 1) * 128, c * CH:(c + 1) * CH], in_=xc[i][t][:],
                      stream="x2pst", reads=[("xc", i, t)], writes=[("x2p_st", c, t)])

        def p1_tail2(c):
            i = c % 2
            emit_ln(p, env1, [(xin[t][:], ("xin", t)) for t in range(8)],
                    [(xnout[t][:], ("xnout", t)) for t in range(8)], 8, CH,
                    gb=[(s_g[:, t:t + 1], s_b[:, t:t + 1]) for t in range(8)],
                    gbkeys=[("s_g",), ("s_gb",)])
            for t in range(8):
                p.dma("act", out=ag_in[c][t * 128:(t + 1) * 128, :], in_=xnout[t][:],
                      stream="xnst", reads=[("xnout", t)], writes=[("xn_st", c, t)])
            p.collective(lambda e, c=c: e.collective_compute(
                "AllGather", OP.bypass, replica_groups=[list(range(NCORES))],
                ins=[ag_in[c].opt()], outs=[ag_out[c].opt()]),
                 reads=[("xn_st", c, t) for t in range(8)], writes=[("ag", c)])

        p1_load(0)
        p1_ln1(0)
        wf0, wf1, wf2 = setup_ffn_weights(nc, p, env1, k0, k1, k2)
        for c in range(NCH1):
            i = c % 2
            if c + 1 < NCH1:
                p1_load(c + 1)
            emit_ffn_l1(p, env1, [(nxb[i][t][:], ("nxb", i, t)) for t in range(8)], wf0, 8, 32)
            if c + 1 < NCH1:
                p1_ln1(c + 1)
            if c >= 1:
                p1_tail2(c - 1)
            emit_ffn_l2(p, env1, wf1, 32, 32)
            emit_ffn_l3(p, env1, [(xc[i][t][:], ("xc", i, t)) for t in range(8)], wf2, 32, 8)
            p1_tail1(c)
        p1_tail2(NCH1 - 1)
        ph1.close()
        phase_barrier(p, [("x2p_st", NCH1 - 1, 7), ("wfdone", 2), ("cvw",), ("cva",), ("cvb",),
                          ("ag", NCH1 - 1)])

        ph2 = contextlib.ExitStack()
        sb2 = lambda name, shape, dt: ph2.enter_context(nc.sbuf_tensor(name, shape, dt))
        ps2 = lambda name, shape: ph2.enter_context(nc.psum_tensor(name, shape, F32))
        xnb = [sb2("xnb%d" % t, [128, SCH], BF16) for t in range(8)]
        winsb = sb2("winsb", [128, 8 * 4 * 128], BF16)
        wxsb = sb2("wxsb", [128, 2 * 96], BF16)
        wdtsb = sb2("wdtsb", [DTR, 2 * 128], BF16)
        woutsb2 = sb2("woutsb2", [128, 2 * 1024], BF16)
        s_cw = sb2("s_cw", [128, 2 * DCONV], F32)
        s_cb = sb2("s_cb", [128, 2], F32)
        s_dtb = sb2("s_dtb", [128, 2], F32)
        s_aneg = sb2("s_aneg", [128, 2 * NST], F32)
        s_d = sb2("s_d", [128, 2], F32)
        cst2 = sb2("cstl2", [128, 2], F32)
        u_raw = [[[sb2("ur%d_%d_%d" % (b, i, t), [128, DCONV - 1 + SCH], F32) for t in range(NT)]
                  for i in range(3)] for b in range(BATCH)]
        zt = [[[sb2("z%d_%d_%d" % (b, i, t), [128, SCH], F32) for t in range(NT)]
               for i in range(3)] for b in range(BATCH)]
        ust = [[[sb2("us%d_%d_%d" % (b, i, t), [128, SCH], F32) for t in range(NT)]
                for i in range(3)] for b in range(BATCH)]
        conv = [sb2("conv%d" % t, [128, SCH], F32) for t in range(NT)]
        ub = [sb2("ub%d" % t, [128, SCH], BF16) for t in range(NT)]
        dbcs = sb2("dbcs", [96, SCH], F32)
        dtinb = sb2("dtinb", [DTR, SCH], BF16)
        dt = [[sb2("dt%d_%d" % (b, t), [128, SCH], F32) for t in range(NT)] for b in range(BATCH)]
        dtu = [sb2("dtu%d" % t, [128, SCH], F32) for t in range(NT)]
        sp1 = sb2("sp1", [128, 512], F32)
        sp2 = sb2("sp2", [128, 512], F32)
        bcB = [sb2("bcB%d" % i, [128, SCH], F32) for i in range(2)]
        bcC = [sb2("bcC%d" % i, [128, SCH], F32) for i in range(2)]
        dA = [[sb2("dA%d_%d" % (t, i), [128, SCH], F32) for i in range(2)] for t in range(NT)]
        dBu = [sb2("dBu%d" % t, [128, SCH], F32) for t in range(NT)]
        hb = [sb2("hb%d" % i, [128, SCH], F32) for i in range(4)]
        hcb = [sb2("hcb%d" % t, [128, SCH], F32) for t in range(NT)]
        y = [sb2("y%d" % t, [128, SCH], F32) for t in range(NT)]
        ygb = [[sb2("ygb%d_%d" % (b, t), [128, SCH], BF16) for t in range(NT)] for b in range(BATCH)]
        motile = [sb2("motile%d" % i, [128, 512], F32) for i in range(2)]
        hcar = [[sb2("hcar%d_%d" % (b, t), [128, NST], F32) for t in range(NT)] for b in range(BATCH)]
        pW = [ps2("pW%d" % i, [128, 512]) for i in range(4)]

        for k in range(8):
            p.dma("pool", out=winsb[:, k * 512:(k + 1) * 512], in_=winT[k * 128:(k + 1) * 128, :],
                  stream="winl", writes=[("winsb",)])
        for kt in range(NT):
            p.dma("pool", out=wxsb[:, kt * 96:(kt + 1) * 96], in_=wxT[kt * 128:(kt + 1) * 128, :],
                  stream="wxl", writes=[("wxsb",)])
        p.dma("pool", out=wdtsb[:], in_=wdtT, stream="wdtl", writes=[("wdtsb",)])
        for kt in range(NT):
            p.dma("pool", out=woutsb2[:, kt * 1024:(kt + 1) * 1024], in_=woutTs[kt * 128:(kt + 1) * 128, :],
                  stream="wo2l", writes=[("woutsb2",)])
        p.dma("sp", out=s_cw[:], in_=cw, stream="scw", writes=[("s_cw",)])
        p.dma("sp", out=s_cb[:], in_=cb, stream="scb", writes=[("s_cb",)])
        p.dma("sp", out=s_dtb[:], in_=dtb, stream="sdtb", writes=[("s_dtb",)])
        p.dma("sp", out=s_aneg[:], in_=alog, stream="saneg", writes=[("s_aneg",)])
        p.dma("sp", out=s_d[:], in_=dsl_in, stream="sdv", writes=[("s_d",)])
        p.op("dve", lambda e: e.memset(cst2[:, 0:1], 1.0), writes=[("cst2",)])
        p.op("dve", lambda e: e.memset(cst2[:, 1:2], -1.0), writes=[("cst2",)])
        p.op("act", lambda e: e.activation(out=s_aneg[:], in_=s_aneg[:], func=AF.Exp),
             reads=[("s_aneg",)], writes=[("s_aneg",)])
        p.op("dve", lambda e: e.tensor_scalar(out=s_aneg[:], in0=s_aneg[:], scalar1=-1.0,
                                              scalar2=0.0, op0=OP.mult, op1=OP.add),
             reads=[("s_aneg",)], writes=[("s_aneg",)])
        for b in range(BATCH):
            for t in range(NT):
                p.op("dve", lambda e, b=b, t=t: e.memset(hcar[b][t][:], 0.0),
                     writes=[("hcar", b, t, n) for n in range(NST)])

        def rro(b, ci):
            r = b * 4 + ci % 4
            o = (ci // 4) * SCH
            return r, o

        def pre_phase(b, ci):
            par = ci % 3
            r, o = rro(b, ci)
            kc = o // CH
            ur = u_raw[b][par]
            z = zt[b][par]
            us = ust[b][par]
            for t8 in range(8):
                p.dma("pool", out=xnb[t8][:],
                      in_=ag_out[kc][r * E + t8 * 128:r * E + (t8 + 1) * 128, :],
                      stream="xnb%d" % t8, reads=[("ag", kc)], writes=[("xnb", t8)])
            for t in range(NT):
                if ci == 0:
                    p.op("dve", lambda e, b=b, par=par, t=t: e.memset(ur[t][:, 0:DCONV - 1], 0.0),
                         reads=[("ur", b, par, t)], writes=[("ucar", b, par, t)])
                else:
                    pprev = (ci - 1) % 3
                    p.op("dve", lambda e, b=b, par=par, t=t, pprev=pprev: e.tensor_copy(
                        out=u_raw[b][par][t][:, 0:DCONV - 1],
                        in_=u_raw[b][pprev][t][:, SCH:SCH + DCONV - 1]),
                         reads=[("ur", b, pprev, t), ("ur", b, par, t)],
                         writes=[("ucar", b, par, t)])
            for mt in range(4):
                pp = pW[mt % 4]
                pk = ("pW", mt % 4)
                for k in range(8):
                    p.op("pe", lambda e, pp=pp, k=k, mt=mt, s=(k == 0), t_=(k == 7): e.matmul(
                        out=pp[:], lhsT=winsb[:, k * 512 + mt * 128:k * 512 + (mt + 1) * 128],
                        rhs=xnb[k][:], start=s, stop=t_),
                         reads=[("winsb",), ("xnb", k)], writes=[pk])
                if mt < NT:
                    t = mt
                    p.op("act", lambda e, pp=pp, t=t: e.activation(
                        out=ur[t][:, DCONV - 1:DCONV - 1 + SCH], in_=pp[:], func=AF.Copy),
                         reads=[pk, ("ucar", b, par, t)], writes=[("ur", b, par, t)])
                else:
                    t = mt - NT
                    p.op("act", lambda e, pp=pp, t=t: e.activation(
                        out=z[t][:], in_=pp[:], func=AF.Copy),
                         reads=[pk, ("z", b, par, t)], writes=[("z", b, par, t)])
            for t in range(NT):
                p.op("dve", lambda e, t=t, ur=ur: e.tensor_scalar(
                    out=conv[t][:], in0=ur[t][:, 0:SCH], scalar1=s_cw[:, t * DCONV:t * DCONV + 1],
                    scalar2=0.0, op0=OP.mult, op1=OP.add),
                     reads=[("ur", b, par, t), ("ucar", b, par, t), ("s_cw",), ("conv", t)],
                     writes=[("conv", t)])
                for kk in range(1, DCONV):
                    p.op("dve", lambda e, t=t, kk=kk, ur=ur: e.scalar_tensor_tensor(
                        out=conv[t][:], in0=ur[t][:, kk:kk + SCH],
                        scalar=s_cw[:, t * DCONV + kk:t * DCONV + kk + 1],
                        in1=conv[t][:], op0=OP.mult, op1=OP.add),
                         reads=[("ur", b, par, t), ("conv", t), ("s_cw",)], writes=[("conv", t)])
                p.op("act", lambda e, t=t, us=us: e.activation(
                    out=us[t][:], in_=conv[t][:], func=AF.Silu, bias=s_cb[:, t:t + 1]),
                     reads=[("conv", t), ("s_cb",), ("us", b, par, t)], writes=[("us", b, par, t)])
                p.op("act", lambda e, t=t, us=us: e.activation(out=ub[t][:], in_=us[t][:], func=AF.Copy),
                     reads=[("us", b, par, t), ("ub", t)], writes=[("ub", t)])
            pp = pW[0]
            pk = ("pW", 0)
            for kt in range(NT):
                p.op("pe", lambda e, pp=pp, kt=kt, s=(kt == 0), t_=(kt == NT - 1): e.matmul(
                    out=pp[0:96, :], lhsT=wxsb[:, kt * 96:(kt + 1) * 96],
                    rhs=ub[kt][:], start=s, stop=t_),
                     reads=[("wxsb",), ("ub", kt)], writes=[pk])
            p.op("act", lambda e, pp=pp: e.activation(
                out=dbcs[0:96, :], in_=pp[0:96, :], func=AF.Copy),
                 reads=[pk, ("dbcs",)], writes=[("dbcs",)])
            p.dma("sp", out=ar_in[b][par], in_=dbcs[0:96, :], stream="arin%d%d" % (b, par),
                  reads=[("dbcs",)], writes=[("ar_in", b, par)])
            p.collective(lambda e, b=b, par=par: e.collective_compute(
                "AllReduce", OP.add, replica_groups=[list(range(NCORES))],
                ins=[ar_in[b][par].opt()], outs=[ar_out[b][par].opt()]),
                 reads=[("ar_in", b, par), ("ar_out", b, par)], writes=[("ar_out", b, par)])

        def post_dt(b, ci):
            par = ci % 3
            p.dma("pool", out=dtinb[:], in_=ar_out[b][par][0:DTR, :], stream="dtinb",
                  reads=[("ar_out", b, par), ("dtinb",)], writes=[("dtinb",)])
            for mt in range(NT):
                pp = pW[2 + mt % 2]
                pk = ("pW", 2 + mt % 2)
                p.op("pe", lambda e, pp=pp, mt=mt: e.matmul(
                    out=pp[:], lhsT=wdtsb[:, mt * 128:(mt + 1) * 128],
                    rhs=dtinb[:], start=True, stop=True),
                     reads=[("wdtsb",), ("dtinb",)], writes=[pk])
                p.op("act", lambda e, pp=pp, b=b, mt=mt: e.activation(
                    out=dt[b][mt][:], in_=pp[:], func=AF.Relu, bias=s_dtb[:, mt:mt + 1]),
                     reads=[pk, ("s_dtb",), ("dt", b, mt)], writes=[("dt", b, mt)])
                p.op("act", lambda e, pp=pp, mt=mt: e.activation(
                    out=sp1[:], in_=pp[:], func=AF.Abs, bias=s_dtb[:, mt:mt + 1]),
                     reads=[pk, ("s_dtb",), ("sp1",)], writes=[("sp1",)])
                p.op("act", lambda e: e.activation(out=sp2[:], in_=sp1[:], func=AF.Exp,
                                                   scale=cst2[:, 1:2]),
                     reads=[("sp1",), ("cst2",), ("sp2",)], writes=[("sp2",)])
                p.op("act", lambda e: e.activation(out=sp1[:], in_=sp2[:], func=AF.Ln,
                                                   bias=cst2[:, 0:1]),
                     reads=[("sp2",), ("cst2",), ("sp1",)], writes=[("sp1",)])
                p.op("dve", lambda e, b=b, mt=mt: e.tensor_tensor(
                    out=dt[b][mt][:], in0=dt[b][mt][:], in1=sp1[:], op=OP.add),
                     reads=[("dt", b, mt), ("sp1",)], writes=[("dt", b, mt)])

        def post_scan(b, ci, mid=None):
            par = ci % 3
            r, o = rro(b, ci)
            q = o // SCH
            us = ust[b][par]
            z = zt[b][par]
            for t in range(NT):
                p.op("dve", lambda e, b=b, t=t, us=us: e.tensor_tensor(
                    out=dtu[t][:], in0=dt[b][t][:], in1=us[t][:], op=OP.mult),
                     reads=[("dt", b, t), ("us", b, par, t), ("dtu", t)], writes=[("dtu", t)])
            for n in range(NST):
                if n == 8 and mid is not None:
                    mid()
                i2 = n % 2
                p.dma("sp", out=bcB[i2][:], in_=ar_out[b][par][DTR + n, :].partition_broadcast(128),
                      stream="bcB%d" % i2, reads=[("ar_out", b, par), ("bcB", i2)], writes=[("bcB", i2)])
                p.dma("pool", out=bcC[i2][:], in_=ar_out[b][par][DTR + NST + n, :].partition_broadcast(128),
                      stream="bcC%d" % i2, reads=[("ar_out", b, par), ("bcC", i2)], writes=[("bcC", i2)])
                for t in range(NT):
                    hi = (n * NT + t) % 4
                    p.op("act", lambda e, b=b, t=t, n=n, i2=i2: e.activation(
                        out=dA[t][i2][:], in_=dt[b][t][:], func=AF.Exp,
                        scale=s_aneg[:, t * NST + n:t * NST + n + 1]),
                         reads=[("dt", b, t), ("s_aneg",), ("dA", t, i2)], writes=[("dA", t, i2)])
                    p.op("dve", lambda e, t=t, i2=i2: e.tensor_tensor(
                        out=dBu[t][:], in0=dtu[t][:], in1=bcB[i2][:], op=OP.mult),
                         reads=[("dtu", t), ("bcB", i2), ("dBu", t)], writes=[("dBu", t)])
                    p.op("dve", lambda e, b=b, t=t, n=n, hi=hi, i2=i2: e.tensor_tensor_scan(
                        out=hb[hi][:], data0=dA[t][i2][:], data1=dBu[t][:],
                        initial=hcar[b][t][:, n:n + 1], op0=OP.mult, op1=OP.add),
                         reads=[("dA", t, i2), ("dBu", t), ("hcar", b, t, n), ("hb", hi)],
                         writes=[("hb", hi)])
                    p.op("act", lambda e, b=b, t=t, n=n, hi=hi: e.activation(
                        out=hcar[b][t][:, n:n + 1], in_=hb[hi][:, SCH - 1:SCH], func=AF.Copy),
                         reads=[("hb", hi)], writes=[("hcar", b, t, n)])
                    if n == 0:
                        p.op("dve", lambda e, t=t, i2=i2, hi=hi: e.tensor_tensor(
                            out=y[t][:], in0=hb[hi][:], in1=bcC[i2][:], op=OP.mult),
                             reads=[("hb", hi), ("bcC", i2), ("y", t)], writes=[("y", t)])
                    else:
                        p.op("dve", lambda e, t=t, i2=i2, hi=hi: e.tensor_tensor(
                            out=hcb[t][:], in0=hb[hi][:], in1=bcC[i2][:], op=OP.mult),
                             reads=[("hb", hi), ("bcC", i2), ("hcb", t)], writes=[("hcb", t)])
                        p.op("dve", lambda e, t=t: e.tensor_tensor(
                            out=y[t][:], in0=y[t][:], in1=hcb[t][:], op=OP.add),
                             reads=[("y", t), ("hcb", t)], writes=[("y", t)])
            for t in range(NT):
                p.op("dve", lambda e, b=b, t=t, us=us: e.scalar_tensor_tensor(
                    out=y[t][:], in0=us[t][:], scalar=s_d[:, t:t + 1], in1=y[t][:],
                    op0=OP.mult, op1=OP.add),
                     reads=[("us", b, par, t), ("y", t), ("s_d",)], writes=[("y", t)])
                p.op("act", lambda e, t=t, z=z: e.activation(out=z[t][:], in_=z[t][:], func=AF.Silu),
                     reads=[("z", b, par, t)], writes=[("z", b, par, t)])
                p.op("dve", lambda e, b=b, t=t, z=z: e.tensor_tensor(
                    out=ygb[b][t][:], in0=y[t][:], in1=z[t][:], op=OP.mult),
                     reads=[("y", t), ("z", b, par, t), ("ygb", b, t)], writes=[("ygb", b, t)])

        def post_wout(b, ci):
            r, o = rro(b, ci)
            q = o // SCH
            for mt in range(8):
                pp = pW[mt % 2]
                pk = ("pW", mt % 2)
                for kt in range(NT):
                    p.op("pe", lambda e, pp=pp, b=b, kt=kt, mt=mt, s=(kt == 0), t_=(kt == NT - 1): e.matmul(
                        out=pp[:], lhsT=woutsb2[:, kt * 1024 + mt * 128:kt * 1024 + (mt + 1) * 128],
                        rhs=ygb[b][kt][:], start=s, stop=t_),
                         reads=[("woutsb2",), ("ygb", b, kt)], writes=[pk])
                mo = motile[mt % 2]
                mok = ("motile", mt % 2)
                p.op("act", lambda e, pp=pp, mo=mo: e.activation(out=mo[:], in_=pp[:], func=AF.Copy),
                     reads=[pk, mok], writes=[mok])
                p.dma("act", out=rs_in[q][r * E + mt * 128:r * E + (mt + 1) * 128, :],
                      in_=mo[:], stream="rsst", reads=[mok], writes=[("rs_st", q)])

        NCI = Lb // SCH
        for b in range(BATCH):
            pre_phase(b, 0)
        for b in range(BATCH):
            if NCI > 1:
                pre_phase(b, 1)
        for b in range(BATCH):
            post_dt(b, 0)
        NRANK_CI = T // SCH
        for ci in range(NCI):
            for b in range(BATCH):
                mid = (lambda b=b, ci=ci: post_dt(b, ci + 1)) if ci + 1 < NCI else None
                post_scan(b, ci, mid=mid)
                if ci + 2 < NCI:
                    pre_phase(b, ci + 2)
            for b in range(BATCH):
                post_wout(b, ci)
            if ci % 4 == 3:
                q = ci // 4
                p.collective(lambda e, q=q: e.collective_compute(
                    "ReduceScatter", OP.add, replica_groups=[list(range(NCORES))],
                    ins=[rs_in[q].opt()], outs=[rs_out[q].opt()]),
                     reads=[("rs_st", q)], writes=[("rs_done", q)])
        ph2.close()
        hkeys = [("hcar", b, t, n) for b in range(BATCH) for t in range(NT) for n in range(NST)]
        phase_barrier(p, [("rs_done", q) for q in range(NQ)] + hkeys
                      + [("ucar", b, par, t) for b in range(BATCH) for par in range(3) for t in range(NT)])

        ph3 = contextlib.ExitStack()
        env3 = dict(env)
        make_ffn_phase_env(nc, ph3, env3, nwb=2, tag="p3")
        sb3 = lambda name, shape, dt: ph3.enter_context(nc.sbuf_tensor(name, shape, dt))
        x3 = [[sb3("x3_%d_%d" % (i, t), [128, CH], F32) for t in range(8)] for i in range(2)]
        mo3 = [sb3("mo3_%d" % t, [128, CH], F32) for t in range(8)]
        nxb3 = [[sb3("nxb3_%d_%d" % (i, t), [128, CH], BF16) for t in range(8)] for i in range(2)]
        store_keys = []

        def p3_load(c):
            i = c % 2
            c0 = c * CH
            q = c0 // SCH
            qo = c0 - q * SCH
            for t in range(8):
                p.dma("pool", out=x3[i][t][:], in_=x2p_d[t * 128:(t + 1) * 128, c0:c0 + CH],
                      stream="x3l%d" % t, writes=[("x3", i, t)])
                p.dma("pool", out=mo3[t][:], in_=rs_out[q][t * 128:(t + 1) * 128, qo:qo + CH],
                      stream="mo%d" % t, reads=[("rs_done", q)], writes=[("mo3", t)])

        def p3_ln(c):
            i = c % 2
            for t in range(8):
                p.op("dve", lambda e, i=i, t=t: e.tensor_tensor(
                    out=x3[i][t][:], in0=x3[i][t][:], in1=mo3[t][:], op=OP.add),
                     reads=[("x3", i, t), ("mo3", t)], writes=[("x3", i, t)])
            emit_ln(p, env3, [(x3[i][t][:], ("x3", i, t)) for t in range(8)],
                    [(nxb3[i][t][:], ("nxb3", i, t)) for t in range(8)], 8, CH, gb=None)

        p3_load(0)
        p3_ln(0)
        for c in range(NCH1):
            i = c % 2
            if c + 1 < NCH1:
                p3_load(c + 1)
            emit_ffn_l1(p, env3, [(nxb3[i][t][:], ("nxb3", i, t)) for t in range(8)], wf0, 8, 32)
            if c + 1 < NCH1:
                p3_ln(c + 1)
            emit_ffn_l2(p, env3, wf1, 32, 32)
            emit_ffn_l3(p, env3, [(x3[i][t][:], ("x3", i, t)) for t in range(8)], wf2, 32, 8)
            for t in range(8):
                skey = ("out_st", c, t)
                p.dma("act", out=out_d[t * 128:(t + 1) * 128, c * CH:(c + 1) * CH], in_=x3[i][t][:],
                      stream="outst", reads=[("x3", i, t)], writes=[skey])
                store_keys.append(skey)
        ph3.close()
        p.op("pool", lambda e: e.nop(nofuse=True), reads=store_keys[-8:])
        with nc.Block() as block:
            p.emit(block)
    return nc


def _forward_fused(inputs):
    x = np.asarray(inputs["L0_attn_x"], np.float32)
    L = x.shape[1]
    T = L // 4
    s0, s1, s2, k0, k1, k2 = _prep_common(inputs)
    lng = np.ascontiguousarray(np.asarray(inputs["L0_attn_ln_g"], np.float32).reshape(8, 128).T)
    lnb = np.ascontiguousarray(np.asarray(inputs["L0_attn_ln_b"], np.float32).reshape(8, 128).T)
    W_in = np.asarray(inputs["L0_attn_W_in"], np.float32)
    W_x = np.asarray(inputs["L0_attn_W_x"], np.float32)
    W_dt = np.asarray(inputs["L0_attn_W_dt"], np.float32)
    conv_w = np.asarray(inputs["L0_attn_conv_w"], np.float32)
    conv_b = np.asarray(inputs["L0_attn_conv_b"], np.float32)
    dt_bias = np.asarray(inputs["L0_attn_dt_bias"], np.float32)
    A_log = np.asarray(inputs["L0_attn_A_log"], np.float32)
    Dv = np.asarray(inputs["L0_attn_D"], np.float32)
    W_out = np.asarray(inputs["L0_attn_W_out"], np.float32)

    nc = build_fused(T=T)
    CHW = 512
    nstr = T // CHW
    in_maps = []
    for c in range(NCORES):
        b = c // 4
        stripes = [(4 * j + c % 4) * CHW for j in range(nstr)]
        xTc = np.concatenate([x[b, s:s + CHW, :].T for s in stripes], axis=1)
        sl = slice(c * DSL, (c + 1) * DSL)
        rows = np.r_[c * DSL:(c + 1) * DSL, DIN + c * DSL:DIN + (c + 1) * DSL]
        in_maps.append(dict(
            xT=np.ascontiguousarray(xTc),
            k0=k0, k1=k1, k2=k2, s0r=s0, s1r=s1, s2r=s2, lng=lng, lnb=lnb,
            winT=np.ascontiguousarray(W_in[rows].T),
            wxT=np.ascontiguousarray(W_x[:, sl].T),
            wdtT=np.ascontiguousarray(W_dt[sl].T),
            cw=_tile2(conv_w[sl]), cb=_tile2(conv_b[sl]), dtb=_tile2(dt_bias[sl]),
            alog=_tile2(A_log[sl]), dsl=_tile2(Dv[sl]),
            woutTs=np.ascontiguousarray(W_out.T[sl]),
        ))
    res = run_bass_kernel_spmd(nc, in_maps, core_ids=list(range(NCORES))).results
    out = np.empty((BATCH, L, E), np.float32)
    for c in range(NCORES):
        b = c // 4
        o = np.asarray(res[c]["outT"])
        for j in range(nstr):
            s = (4 * j + c % 4) * CHW
            out[b, s:s + CHW, :] = o[:, j * CHW:(j + 1) * CHW].T
    return out



def _prep_common(inputs):
    s0 = np.ascontiguousarray(np.broadcast_to(np.asarray(inputs["L0_ffn_s0"], np.float32).reshape(1, 1), (128, 1)))
    s1 = np.ascontiguousarray(np.broadcast_to(np.asarray(inputs["L0_ffn_s1"], np.float32).reshape(1, 1), (128, 1)))
    s2 = np.ascontiguousarray(np.broadcast_to(np.asarray(inputs["L0_ffn_s2"], np.float32).reshape(1, 1), (128, 1)))
    k0 = np.ascontiguousarray(np.asarray(inputs["L0_ffn_k0"]).view(np.uint8))
    k1 = np.ascontiguousarray(np.asarray(inputs["L0_ffn_k1"]).view(np.uint8))
    k2 = np.ascontiguousarray(np.asarray(inputs["L0_ffn_k2"]).view(np.uint8))
    return s0, s1, s2, k0, k1, k2


def _tile2(v):
    v = np.asarray(v, np.float32)
    if v.ndim == 1:
        v = v[:, None]
    k = v.shape[1]
    return np.ascontiguousarray(v.reshape(2, 128, k).transpose(1, 0, 2).reshape(128, 2 * k))


def kernel(**inputs):
    return _forward_fused(inputs)


def _forward(inputs, collect=False):
    x = np.asarray(inputs["L0_attn_x"], np.float32)
    L = x.shape[1]
    T = L // 4
    s0, s1, s2, k0, k1, k2 = _prep_common(inputs)
    lng = np.ascontiguousarray(np.asarray(inputs["L0_attn_ln_g"], np.float32).reshape(8, 128).T)
    lnb = np.ascontiguousarray(np.asarray(inputs["L0_attn_ln_b"], np.float32).reshape(8, 128).T)

    nc1 = build_launch1(T=T)
    in1 = []
    for c in range(NCORES):
        b, lo = c // 4, (c % 4) * T
        xT = np.ascontiguousarray(x[b, lo:lo + T, :].T)
        in1.append(dict(xT=xT, k0=k0, k1=k1, k2=k2, s0r=s0, s1r=s1, s2r=s2, lng=lng, lnb=lnb))
    r1 = run_bass_kernel_spmd(nc1, in1, core_ids=list(range(NCORES))).results
    xn_full = np.empty((BATCH * E, L), dtype=BF)
    x2p = [np.ascontiguousarray(np.asarray(r1[c]["x2p_out"])) for c in range(NCORES)]
    for c in range(NCORES):
        b, lo = c // 4, (c % 4) * T
        xn_full[b * E:(b + 1) * E, lo:lo + T] = np.asarray(r1[c]["xn_out"])

    W_in = np.asarray(inputs["L0_attn_W_in"], np.float32)
    W_x = np.asarray(inputs["L0_attn_W_x"], np.float32)
    W_dt = np.asarray(inputs["L0_attn_W_dt"], np.float32)
    conv_w = np.asarray(inputs["L0_attn_conv_w"], np.float32)
    conv_b = np.asarray(inputs["L0_attn_conv_b"], np.float32)
    dt_bias = np.asarray(inputs["L0_attn_dt_bias"], np.float32)
    A_log = np.asarray(inputs["L0_attn_A_log"], np.float32)
    Dv = np.asarray(inputs["L0_attn_D"], np.float32)
    nc2 = build_launch2(Lb=L, SCH=1024)
    in2 = []
    for c in range(NCORES):
        sl = slice(c * DSL, (c + 1) * DSL)
        rows = np.r_[c * DSL:(c + 1) * DSL, DIN + c * DSL:DIN + (c + 1) * DSL]
        in2.append(dict(
            xn_in=xn_full,
            winT=np.ascontiguousarray(W_in[rows].T),
            wxT=np.ascontiguousarray(W_x[:, sl].T),
            wdtT=np.ascontiguousarray(W_dt[sl].T),
            cw=_tile2(conv_w[sl]),
            cb=_tile2(conv_b[sl]),
            dtb=_tile2(dt_bias[sl]),
            alog=_tile2(A_log[sl]),
            dsl=_tile2(Dv[sl]),
        ))
    r2 = run_bass_kernel_spmd(nc2, in2, core_ids=list(range(NCORES))).results
    yg_full = np.empty((BATCH, DIN, L), dtype=BF)
    for c in range(NCORES):
        ygc = np.asarray(r2[c]["yg_out"]).reshape(BATCH, DSL, L)
        yg_full[:, c * DSL:(c + 1) * DSL, :] = ygc

    W_out = np.asarray(inputs["L0_attn_W_out"], np.float32)
    nc3 = build_launch3(T=T)
    in3 = []
    for c in range(NCORES):
        b, lo = c // 4, (c % 4) * T
        in3.append(dict(
            yg_in=np.ascontiguousarray(yg_full[b, :, lo:lo + T]),
            x2p_in=x2p[c],
            woutT=np.ascontiguousarray(W_out.T),
            k0=k0, k1=k1, k2=k2, s0r=s0, s1r=s1, s2r=s2,
        ))
    r3 = run_bass_kernel_spmd(nc3, in3, core_ids=list(range(NCORES))).results
    out = np.empty((BATCH, L, E), np.float32)
    for c in range(NCORES):
        b, lo = c // 4, (c % 4) * T
        out[b, lo:lo + T, :] = np.asarray(r3[c]["outT"]).T
    inter = None
    if collect:
        x2p_full = np.empty((BATCH * E, L), np.float32)
        for c in range(NCORES):
            b, lo = c // 4, (c % 4) * T
            x2p_full[b * E:(b + 1) * E, lo:lo + T] = x2p[c]
        inter = dict(xn=xn_full, x2p=x2p_full, yg=yg_full)
    return out, inter


# version 20
# speedup vs baseline: 1.1562x; 1.0148x over previous
import contextlib

import numpy as np
import ml_dtypes

import concourse.bass as bass
import concourse.mybir as mybir
from concourse.bass_utils import run_bass_kernel_spmd

F32 = mybir.dt.float32
BF16 = mybir.dt.bfloat16
U8 = mybir.dt.uint8
AF = mybir.ActivationFunctionType
OP = mybir.AluOpType
BF = ml_dtypes.bfloat16

NCORES = 8
E = 1024
DIN = 2048
NST = 16
DCONV = 4
DTR = 64
BATCH = 2
SEQ = 16384
TPC = SEQ // 4
DSL = DIN // NCORES
CH = 512
EPS = 1e-5


class Prog:

    ENGINES = ("sp", "act", "dve", "pe", "pool")

    def __init__(self, nc, stack):
        self.nc = nc
        self.stack = stack
        self.sems = {}
        self.squeues = {}
        self.equeues = {e: [] for e in self.ENGINES}
        for e in self.ENGINES:
            self._get_sem(e)
        self.last_writer = {}
        self.readers = {}

    def _get_sem(self, key):
        if key not in self.sems:
            self.sems[key] = self.stack.enter_context(self.nc.semaphore("sem_" + str(key)))
            self.squeues[key] = []
        return self.sems[key]

    def _append(self, engine, fn, reads, writes, sem_key, inc, flagged):
        need = {}

        def add(src):
            if src is None:
                return
            sk, idx, eng2 = src
            if eng2 == engine and sk == "pe" and engine == "pe":
                return
            if need.get(sk, -1) < idx:
                need[sk] = idx
        for r in reads:
            add(self.last_writer.get(r))
        for w in writes:
            add(self.last_writer.get(w))
            for rr in self.readers.get(w, ()):
                add(rr)
        for sk, idx in need.items():
            self.squeues[sk][idx][3] = True
        rec = [need, fn, inc, flagged, sem_key]
        self.equeues[engine].append(rec)
        sidx = len(self.squeues[sem_key])
        self.squeues[sem_key].append(rec)
        src = (sem_key, sidx, engine)
        for r in reads:
            self.readers.setdefault(r, []).append(src)
        for w in writes:
            self.last_writer[w] = src
            self.readers[w] = []

    def op(self, engine, fn, reads=(), writes=(), inc=1):
        self._append(engine, fn, reads, writes, engine, inc, False)

    def dma(self, engine, out, in_, stream, reads=(), writes=(), **kw):
        self._get_sem(stream)
        self._append(engine, lambda e: e.dma_start(out=out, in_=in_, **kw),
                     reads, writes, stream, 16, True)

    def collective(self, fn, reads=(), writes=()):
        self._get_sem("cc")
        self._append("pool", fn, reads, writes, "cc", 1, True)

    def emit(self, block):
        decos = dict(sp=block.sync, act=block.scalar, dve=block.vector,
                     pe=block.tensor, pool=block.gpsimd)
        cums = {}
        for sk, q in self.squeues.items():
            c = 0
            cl = []
            for rec in q:
                if rec[3]:
                    c += rec[2]
                cl.append(c)
            cums[sk] = cl
        for ename in self.ENGINES:
            queue = self.equeues[ename]
            if not queue:
                continue

            def body(eng, queue=queue):
                floor = {}
                for (need, fn, inc, flagged, sem_key) in queue:
                    for sk, idx in need.items():
                        val = cums[sk][idx]
                        if floor.get(sk, 0) < val:
                            eng.wait_ge(self.sems[sk], val)
                            floor[sk] = val
                    instr = fn(eng)
                    if flagged:
                        instr.then_inc(self.sems[sem_key], inc)

            decos[ename](body)



def emit_ln(p, env, src, dst, ntiles, width, gb=None, gbkeys=(("s_gb",),)):
    xb, sq = env["lnxb"], env["lnsq"]
    pstatA, pstatB = env["pstatA"], env["pstatB"]
    rows = env["rows"]
    pbc1, pbc2 = env["pbc1"], env["pbc2"]
    onesb, ones1f, cst = env["onesb"], env["ones1f"], env["cst"]
    scr = env["lnscr"]
    inv_n = 1.0 / (128 * ntiles)

    for t in range(ntiles):
        a, k = src[t]
        s2_ = t % 4
        p.op("act", lambda e, a=a, s2_=s2_: e.activation(
            out=xb[:, s2_ * width:(s2_ + 1) * width], in_=a, func=AF.Copy),
             reads=[k], writes=[("lnxb", s2_)])
        p.op("act", lambda e, a=a, s2_=s2_: e.activation(
            out=sq[:, s2_ * width:(s2_ + 1) * width], in_=a, func=AF.Square),
             reads=[k], writes=[("lnsq", s2_)])
        p.op("pe", lambda e, s2_=s2_, t=t: e.matmul(out=pstatA[0:1, :], lhsT=onesb[:, 0:1],
                                                    rhs=xb[:, s2_ * width:(s2_ + 1) * width],
                                                    start=(t == 0), stop=(t == ntiles - 1)),
             reads=[("lnxb", s2_), ("onesb",)], writes=[("pstatA",)])
        p.op("pe", lambda e, s2_=s2_, t=t: e.matmul(out=pstatB[0:1, :], lhsT=onesb[:, 0:1],
                                                    rhs=sq[:, s2_ * width:(s2_ + 1) * width],
                                                    start=(t == 0), stop=(t == ntiles - 1)),
             reads=[("lnsq", s2_), ("onesb",)], writes=[("pstatB",)])
    m_row = rows[0:1, 0:width]
    var_row = rows[0:1, width:2 * width]
    rstd_row = rows[0:1, 2 * width:3 * width]
    tr_row = rows[0:1, 3 * width:4 * width]
    p.op("dve", lambda e: e.tensor_scalar(out=m_row, in0=pstatA[0:1, :], scalar1=inv_n,
                                          scalar2=0.0, op0=OP.mult, op1=OP.add),
         reads=[("pstatA",)], writes=[("m_row",)])
    p.op("act", lambda e: e.activation(out=var_row, in_=m_row, func=AF.Square),
         reads=[("m_row",)], writes=[("var_row",)])
    p.op("dve", lambda e: e.scalar_tensor_tensor(out=var_row, in0=pstatB[0:1, :], scalar=inv_n,
                                                 in1=var_row, op0=OP.mult, op1=OP.subtract),
         reads=[("pstatB",), ("var_row",)], writes=[("var_row",)])
    p.op("act", lambda e: e.activation(out=rstd_row, in_=var_row, func=AF.Ln, bias=cst[0:1, 2:3]),
         reads=[("var_row",), ("cst",)], writes=[("rstd_row",)])
    p.op("act", lambda e: e.activation(out=rstd_row, in_=rstd_row, func=AF.Exp, scale=cst[0:1, 3:4]),
         reads=[("rstd_row",), ("cst",)], writes=[("rstd_row",)])
    p.op("dve", lambda e: e.tensor_tensor(out=tr_row, in0=m_row, in1=rstd_row, op=OP.mult),
         reads=[("m_row",), ("rstd_row",)], writes=[("tr_row",)])
    p.op("pe", lambda e: e.matmul(out=pbc1[:, 0:width], lhsT=ones1f[0:1, :], rhs=tr_row,
                                  start=True, stop=True),
         reads=[("tr_row",), ("ones1f",)], writes=[("pbc1",)])
    p.op("pe", lambda e: e.matmul(out=pbc2[:, 0:width], lhsT=ones1f[0:1, :], rhs=rstd_row,
                                  start=True, stop=True),
         reads=[("rstd_row",), ("ones1f",)], writes=[("pbc2",)])
    for t in range(ntiles):
        a, k = src[t]
        d, dk = dst[t]
        s = scr[t % 2]
        sk = ("lnscr", t % 2)
        p.op("dve", lambda e, a=a, s=s: e.tensor_tensor(out=s[:], in0=a, in1=pbc2[:, 0:width], op=OP.mult),
             reads=[k, ("pbc2",)], writes=[sk])
        if gb is None:
            p.op("dve", lambda e, s=s, d=d: e.tensor_tensor(out=d, in0=s[:], in1=pbc1[:, 0:width], op=OP.subtract),
                 reads=[sk, ("pbc1",)], writes=[dk])
        else:
            p.op("dve", lambda e, s=s: e.tensor_tensor(out=s[:], in0=s[:], in1=pbc1[:, 0:width], op=OP.subtract),
                 reads=[sk, ("pbc1",)], writes=[sk])
            g_ap, b_ap = gb[t]
            p.op("dve", lambda e, s=s, d=d, g_ap=g_ap, b_ap=b_ap: e.tensor_scalar(
                out=d, in0=s[:], scalar1=g_ap, scalar2=b_ap, op0=OP.mult, op1=OP.add),
                 reads=[sk] + list(gbkeys), writes=[dk])


def emit_weight_convert(p, env, kdram, wf_ap, n_ktiles, n_mtiles, mode, tag):
    cva, cvb, cvw = env["cva"], env["cvb"], env["cvw"]
    width = n_mtiles * 128
    HW = min(width, 512)
    mth = HW // 128
    for kt in range(n_ktiles):
        for h in range(width // HW):
            c0 = h * HW
            if mode == "fold":
                p.dma("pool", out=cva[:, 0:HW], in_=kdram[kt * 128:(kt + 1) * 128, c0:c0 + HW],
                      stream="cva", reads=[("cva_rd",)], writes=[("cva",)])
                p.dma("pool", out=cvb[:, 0:HW], in_=kdram[E + kt * 128:E + (kt + 1) * 128, c0:c0 + HW],
                      stream="cvb", reads=[("cvb_rd",)], writes=[("cvb",)])
                p.op("dve", lambda e: e.tensor_tensor(out=cvw[:, 0:HW], in0=cva[:, 0:HW],
                                                      in1=cvb[:, 0:HW], op=OP.subtract),
                     reads=[("cva",), ("cvb",), ("cvw_rd",)], writes=[("cvw",), ("cva_rd",), ("cvb_rd",)])
            else:
                p.dma("pool", out=cva[:, 0:HW], in_=kdram[kt * 128:(kt + 1) * 128, c0:c0 + HW],
                      stream="cva", reads=[("cva_rd",)], writes=[("cva",)])
                p.op("dve", lambda e: e.tensor_scalar(out=cvw[:, 0:HW], in0=cva[:, 0:HW],
                                                      scalar1=2.0, scalar2=-1.0, op0=OP.mult, op1=OP.add),
                     reads=[("cva",), ("cvw_rd",)], writes=[("cvw",), ("cva_rd",)])
            for mtl in range(mth):
                mt = h * mth + mtl
                last = (mtl == mth - 1)
                p.dma("sp", out=wf_ap[mt, kt], in_=cvw[:, mtl * 128:(mtl + 1) * 128],
                      stream="wfst", reads=[("cvw",)],
                      writes=[(tag, mt, kt), ("cvw_rd",)] if last else [(tag, mt, kt)])


def emit_ffn(p, env, nxb_tiles, res_tiles, wf0, wf1, wf2, n1k, n1m, n2m, n3m):
    emit_ffn_l1(p, env, nxb_tiles, wf0, n1k, n1m)
    emit_ffn_l2(p, env, wf1, n1m, n2m)
    emit_ffn_l3(p, env, res_tiles, wf2, n2m, n3m)


def emit_ffn_l1(p, env, nxb_tiles, wf0, n1k, n1m):
    wbuf0 = env["wbuf0"]
    h1b = env["h1b"]
    pL = env["pL"]
    s0x4 = env["s0x4"]
    for mt in range(n1m):
        wb = wbuf0[mt % len(wbuf0)]
        wk = ("wbuf0", mt % len(wbuf0))
        p.dma("sp", out=wb[:].rearrange("p (k f) -> p k f", k=n1k), in_=wf0[mt],
              stream="wb0_%d" % (mt % len(wbuf0)), reads=[wk + ("free",), ("wfdone", 0)], writes=[wk])
        ps = pL[mt % 4]
        pk = ("pL", mt % 4)
        for k in range(n1k):
            a, ak = nxb_tiles[k]
            last = (k == n1k - 1)
            p.op("pe", lambda e, wb=wb, k=k, a=a, ps=ps, s=(k == 0), t=last: e.matmul(
                out=ps[:], lhsT=wb[:, k * 128:(k + 1) * 128], rhs=a, start=s, stop=t),
                 reads=[wk, ak], writes=[pk, wk + ("free",)] if last else [pk])
        p.op("act", lambda e, ps=ps, mt=mt: e.activation(
            out=h1b[:, mt * CH:(mt + 1) * CH], in_=ps[:], func=AF.Relu, scale=s0x4[:]),
             reads=[pk, ("s0x4",)], writes=[("h1b", mt)])


def emit_ffn_l2(p, env, wf1, n1m, n2m):
    wbuf1 = env["wbuf1"]
    h1b, h2b = env["h1b"], env["h2b"]
    pL = env["pL"]
    s1x2 = env["s1x2"]
    for mt in range(n2m):
        wb = wbuf1[mt % len(wbuf1)]
        wk = ("wbuf1", mt % len(wbuf1))
        p.dma("sp", out=wb[:].rearrange("p (k f) -> p k f", k=n1m), in_=wf1[mt],
              stream="wb1_%d" % (mt % len(wbuf1)), reads=[wk + ("free",), ("wfdone", 1)], writes=[wk])
        ps = pL[mt % 4]
        pk = ("pL", mt % 4)
        for k in range(n1m):
            last = (k == n1m - 1)
            p.op("pe", lambda e, wb=wb, k=k, ps=ps, s=(k == 0), t=last: e.matmul(
                out=ps[:], lhsT=wb[:, k * 128:(k + 1) * 128],
                rhs=h1b[:, k * CH:(k + 1) * CH], start=s, stop=t),
                 reads=[wk, ("h1b", k)], writes=[pk, wk + ("free",)] if last else [pk])
        p.op("act", lambda e, ps=ps, mt=mt: e.activation(
            out=h2b[:, mt * CH:(mt + 1) * CH], in_=ps[:], func=AF.Relu, scale=s1x2[:]),
             reads=[pk, ("s1x2",)], writes=[("h2b", mt)])


def emit_ffn_l3(p, env, res_tiles, wf2, n2m, n3m):
    wbuf2 = env["wbuf2"]
    h2b = env["h2b"]
    pL = env["pL"]
    halfs2 = env["halfs2"]
    for mt in range(n3m):
        wb = wbuf2[mt % len(wbuf2)]
        wk = ("wbuf2", mt % len(wbuf2))
        p.dma("sp", out=wb[:].rearrange("p (k f) -> p k f", k=n2m), in_=wf2[mt],
              stream="wb2_%d" % (mt % len(wbuf2)), reads=[wk + ("free",), ("wfdone", 2)], writes=[wk])
        ps = pL[mt % 4]
        pk = ("pL", mt % 4)
        for k in range(n2m):
            last = (k == n2m - 1)
            p.op("pe", lambda e, wb=wb, k=k, ps=ps, s=(k == 0), t=last: e.matmul(
                out=ps[:], lhsT=wb[:, k * 128:(k + 1) * 128],
                rhs=h2b[:, k * CH:(k + 1) * CH], start=s, stop=t),
                 reads=[wk, ("h2b", k)], writes=[pk, wk + ("free",)] if last else [pk])
        r, rk = res_tiles[mt]
        p.op("dve", lambda e, ps=ps, r=r: e.scalar_tensor_tensor(
            out=r, in0=ps[:], scalar=halfs2[:], in1=r, op0=OP.mult, op1=OP.add),
             reads=[pk, rk, ("halfs2",)], writes=[rk])


def make_ffn_env(nc, stack, env, nwb=2):
    sb = lambda name, shape, dt: stack.enter_context(nc.sbuf_tensor(name, shape, dt))
    ps = lambda name, shape: stack.enter_context(nc.psum_tensor(name, shape, F32))
    env["cst"] = sb("cst", [128, 8], F32)
    env["onesb"] = sb("onesb", [128, 1], BF16)
    env["ones1f"] = sb("ones1f", [1, 128], F32)
    env["cva"] = sb("cva", [128, 2048], BF16)
    env["cvb"] = sb("cvb", [128, 2048], BF16)
    env["cvw"] = sb("cvw", [128, 2048], BF16)
    env["wbuf0"] = [sb("wbuf0_%d" % i, [128, 8 * 128], BF16) for i in range(3)]
    env["wbuf1"] = [sb("wbuf1_%d" % i, [128, 32 * 128], BF16) for i in range(3)]
    env["wbuf2"] = [sb("wbuf2_%d" % i, [128, 32 * 128], BF16) for i in range(nwb)]
    env["h1b"] = sb("h1b", [128, 32 * CH], BF16)
    env["h2b"] = sb("h2b", [128, 32 * CH], BF16)
    env["pL"] = [ps("pL%d" % i, [128, CH]) for i in range(4)]
    env["s0x4"] = sb("s0x4", [128, 1], F32)
    env["s1x2"] = sb("s1x2", [128, 1], F32)
    env["halfs2"] = sb("halfs2", [128, 1], F32)
    env["lnxb"] = sb("lnxb", [128, 4 * CH], BF16)
    env["lnsq"] = sb("lnsq", [128, 4 * CH], BF16)
    env["lnscr"] = [sb("lnscr%d" % i, [128, CH], F32) for i in range(2)]
    env["rows"] = sb("rows", [1, 4 * CH], F32)
    env["pstatA"] = ps("pstatA", [1, CH])
    env["pstatB"] = ps("pstatB", [1, CH])
    env["pbc1"] = ps("pbc1", [128, CH])
    env["pbc2"] = ps("pbc2", [128, CH])


def emit_consts_scales(p, env, s_s0, s_s1, s_s2):
    cst, onesb, ones1f = env["cst"], env["onesb"], env["ones1f"]
    p.op("dve", lambda e: e.memset(cst[:, 0:1], 1.0), writes=[("cst",)])
    p.op("dve", lambda e: e.memset(cst[:, 1:2], -1.0), writes=[("cst",)])
    p.op("dve", lambda e: e.memset(cst[:, 2:3], EPS), writes=[("cst",)])
    p.op("dve", lambda e: e.memset(cst[:, 3:4], -0.5), writes=[("cst",)])
    p.op("dve", lambda e: e.memset(onesb[:], 1.0), writes=[("onesb",)])
    p.op("dve", lambda e: e.memset(ones1f[:], 1.0), writes=[("ones1f",)])
    for name, src, f in (("s0x4", s_s0, 4.0), ("s1x2", s_s1, 2.0), ("halfs2", s_s2, 0.5)):
        dst = env[name]
        p.op("dve", lambda e, dst=dst, src=src, f=f: e.tensor_scalar(
            out=dst[:], in0=src[:], scalar1=f, scalar2=0.0, op0=OP.mult, op1=OP.add),
             reads=[(name, "in")], writes=[(name,)])


def setup_ffn_weights(nc, p, env, k0, k1, k2):
    wf0d = nc.dram_tensor("wf0d", [32, 8, 128, 128], BF16).ap()
    wf1d = nc.dram_tensor("wf1d", [32, 32, 128, 128], BF16).ap()
    wf2d = nc.dram_tensor("wf2d", [8, 32, 128, 128], BF16).ap()
    emit_weight_convert(p, env, k0, wf0d, 8, 32, "fold", "wf0")
    p.op("dve", lambda e: e.memset(env["cst"][:, 4:5], 0.0),
         reads=[("wf0", 31, 7)], writes=[("wfdone", 0)])
    emit_weight_convert(p, env, k1, wf1d, 32, 32, "pm1", "wf1")
    p.op("dve", lambda e: e.memset(env["cst"][:, 5:6], 0.0),
         reads=[("wf1", 31, 31)], writes=[("wfdone", 1)])
    emit_weight_convert(p, env, k2, wf2d, 32, 8, "pm1", "wf2")
    p.op("dve", lambda e: e.memset(env["cst"][:, 6:7], 0.0),
         reads=[("wf2", 7, 31)], writes=[("wfdone", 2)])
    wf0 = [wf0d[mt].rearrange("k p f -> p k f") for mt in range(32)]
    wf1 = [wf1d[mt].rearrange("k p f -> p k f") for mt in range(32)]
    wf2 = [wf2d[mt].rearrange("k p f -> p k f") for mt in range(8)]
    return wf0, wf1, wf2



def build_launch1(T=TPC):
    nc = bass.Bass("TRN2", target_bir_lowering=False, debug=False)
    xT = nc.dram_tensor("xT", [E, T], F32, kind="ExternalInput").ap()
    k0 = nc.dram_tensor("k0", [2 * E, 4096], U8, kind="ExternalInput").ap()
    k1 = nc.dram_tensor("k1", [4096, 4096], U8, kind="ExternalInput").ap()
    k2 = nc.dram_tensor("k2", [4096, E], U8, kind="ExternalInput").ap()
    s0r = nc.dram_tensor("s0r", [128, 1], F32, kind="ExternalInput").ap()
    s1r = nc.dram_tensor("s1r", [128, 1], F32, kind="ExternalInput").ap()
    s2r = nc.dram_tensor("s2r", [128, 1], F32, kind="ExternalInput").ap()
    lng = nc.dram_tensor("lng", [128, 8], F32, kind="ExternalInput").ap()
    lnb = nc.dram_tensor("lnb", [128, 8], F32, kind="ExternalInput").ap()
    xn_out = nc.dram_tensor("xn_out", [E, T], BF16, kind="ExternalOutput").ap()
    x2p_out = nc.dram_tensor("x2p_out", [E, T], F32, kind="ExternalOutput").ap()

    with contextlib.ExitStack() as stack:
        env = {}
        sb = lambda name, shape, dt: stack.enter_context(nc.sbuf_tensor(name, shape, dt))
        make_ffn_env(nc, stack, env)
        xc = [sb("xc%d" % t, [128, CH], F32) for t in range(8)]
        xin = [sb("xin%d" % t, [128, CH], F32) for t in range(8)]
        nxb = [sb("nxb%d" % t, [128, CH], BF16) for t in range(8)]
        s_s0 = sb("s_s0", [128, 1], F32)
        s_s1 = sb("s_s1", [128, 1], F32)
        s_s2 = sb("s_s2", [128, 1], F32)
        s_g = sb("s_g", [128, 8], F32)
        s_b = sb("s_b", [128, 8], F32)

        p = Prog(nc, stack)
        p.dma("sp", out=s_s0[:], in_=s0r, stream="ss0", writes=[("s0x4", "in")])
        p.dma("sp", out=s_s1[:], in_=s1r, stream="ss1", writes=[("s1x2", "in")])
        p.dma("sp", out=s_s2[:], in_=s2r, stream="ss2", writes=[("halfs2", "in")])
        p.dma("sp", out=s_g[:], in_=lng, stream="sgv", writes=[("s_g",)])
        p.dma("sp", out=s_b[:], in_=lnb, stream="sbv", writes=[("s_gb",)])
        emit_consts_scales(p, env, s_s0, s_s1, s_s2)
        wf0, wf1, wf2 = setup_ffn_weights(nc, p, env, k0, k1, k2)

        nch = T // CH
        store_keys = []
        for c in range(nch):
            c0 = c * CH
            for t in range(8):
                p.dma("sp", out=xc[t][:], in_=xT[t * 128:(t + 1) * 128, c0:c0 + CH],
                      stream="xc%d" % t, writes=[("xc", t)])
            emit_ln(p, env, [(xc[t][:], ("xc", t)) for t in range(8)],
                    [(nxb[t][:], ("nxb", t)) for t in range(8)], 8, CH, gb=None)
            emit_ffn(p, env, [(nxb[t][:], ("nxb", t)) for t in range(8)],
                     [(xc[t][:], ("xc", t)) for t in range(8)],
                     wf0, wf1, wf2, 8, 32, 32, 8)
            emit_ln(p, env, [(xc[t][:], ("xc", t)) for t in range(8)],
                    [(xin[t][:], ("xin", t)) for t in range(8)], 8, CH, gb=None)
            for t in range(8):
                p.op("dve", lambda e, t=t: e.tensor_tensor(out=xc[t][:], in0=xc[t][:], in1=xin[t][:], op=OP.add),
                     reads=[("xc", t), ("xin", t)], writes=[("xc", t)])
                skey = ("x2p_st", c, t)
                p.dma("sp", out=x2p_out[t * 128:(t + 1) * 128, c0:c0 + CH], in_=xc[t][:],
                      stream="x2pst", reads=[("xc", t)], writes=[skey])
                store_keys.append(skey)
            emit_ln(p, env, [(xin[t][:], ("xin", t)) for t in range(8)],
                    [(nxb[t][:], ("nxb", t)) for t in range(8)], 8, CH,
                    gb=[(s_g[:, t:t + 1], s_b[:, t:t + 1]) for t in range(8)], gbkeys=[("s_g",), ("s_gb",)])
            for t in range(8):
                skey = ("xn_st", c, t)
                p.dma("sp", out=xn_out[t * 128:(t + 1) * 128, c0:c0 + CH], in_=nxb[t][:],
                      stream="xnst", reads=[("nxb", t)], writes=[skey])
                store_keys.append(skey)
        p.op("pool", lambda e: e.nop(nofuse=True), reads=store_keys[-16:])

        with nc.Block() as block:
            p.emit(block)
    return nc



def build_launch2(Lb=SEQ, SCH=1024):
    nc = bass.Bass("TRN2", target_bir_lowering=False, debug=False)
    xn_in = nc.dram_tensor("xn_in", [BATCH * E, Lb], BF16, kind="ExternalInput").ap()
    winT = nc.dram_tensor("winT", [E, 2 * DSL], F32, kind="ExternalInput").ap()
    wxT = nc.dram_tensor("wxT", [DSL, 96], F32, kind="ExternalInput").ap()
    wdtT = nc.dram_tensor("wdtT", [DTR, DSL], F32, kind="ExternalInput").ap()
    cw = nc.dram_tensor("cw", [128, 2 * DCONV], F32, kind="ExternalInput").ap()
    cb = nc.dram_tensor("cb", [128, 2], F32, kind="ExternalInput").ap()
    dtb = nc.dram_tensor("dtb", [128, 2], F32, kind="ExternalInput").ap()
    alog = nc.dram_tensor("alog", [128, 2 * NST], F32, kind="ExternalInput").ap()
    dsl_in = nc.dram_tensor("dsl", [128, 2], F32, kind="ExternalInput").ap()
    yg_out = nc.dram_tensor("yg_out", [BATCH * DSL, Lb], BF16, kind="ExternalOutput").ap()

    ar_in = nc.dram_tensor("ar_in", [96, SCH], F32).ap()
    ar_out = nc.dram_tensor("ar_out", [96, SCH], F32).ap()

    NT = DSL // 128
    NJ = SCH // 512

    with contextlib.ExitStack() as stack:
        sb = lambda name, shape, dt: stack.enter_context(nc.sbuf_tensor(name, shape, dt))
        psf = lambda name, shape: stack.enter_context(nc.psum_tensor(name, shape, F32))
        xnb = [sb("xnb%d" % t, [128, SCH], BF16) for t in range(8)]
        winsb = sb("winsb", [128, 8 * 4 * 128], BF16)
        wxsb = sb("wxsb", [128, 2 * 96], BF16)
        wdtsb = sb("wdtsb", [DTR, 2 * 128], BF16)
        s_cw = sb("s_cw", [128, 2 * DCONV], F32)
        s_cb = sb("s_cb", [128, 2], F32)
        s_dtb = sb("s_dtb", [128, 2], F32)
        s_aneg = sb("s_aneg", [128, 2 * NST], F32)
        s_d = sb("s_d", [128, 2], F32)
        cst = sb("cstl2", [128, 2], F32)
        u_raw = [sb("u_raw%d" % t, [128, DCONV - 1 + SCH], F32) for t in range(NT)]
        z = [sb("z%d" % t, [128, SCH], F32) for t in range(NT)]
        conv = [sb("conv%d" % t, [128, SCH], F32) for t in range(NT)]
        us = [sb("us%d" % t, [128, SCH], F32) for t in range(NT)]
        ub = [sb("ub%d" % t, [128, SCH], BF16) for t in range(NT)]
        dbcs = sb("dbcs", [96, SCH], F32)
        dtinb = sb("dtinb", [DTR, SCH], BF16)
        dt = [sb("dt%d" % t, [128, SCH], F32) for t in range(NT)]
        dtu = [sb("dtu%d" % t, [128, SCH], F32) for t in range(NT)]
        sp1 = sb("sp1", [128, 512], F32)
        sp2 = sb("sp2", [128, 512], F32)
        bcB = sb("bcB", [128, SCH], F32)
        bcC = sb("bcC", [128, SCH], F32)
        dA = sb("dA", [128, SCH], F32)
        dBu = sb("dBu", [128, SCH], F32)
        hb = sb("hb", [128, SCH], F32)
        hcb = sb("hcb", [128, SCH], F32)
        y = [sb("y%d" % t, [128, SCH], F32) for t in range(NT)]
        ygb = sb("ygb", [128, SCH], BF16)
        hcar = [sb("hcar%d" % t, [128, NST], F32) for t in range(NT)]
        pW = [psf("pW%d" % i, [128, 512]) for i in range(4)]

        p = Prog(nc, stack)
        for k in range(8):
            p.dma("pool", out=winsb[:, k * 512:(k + 1) * 512], in_=winT[k * 128:(k + 1) * 128, :],
                  stream="winl", writes=[("winsb",)])
        for kt in range(NT):
            p.dma("pool", out=wxsb[:, kt * 96:(kt + 1) * 96], in_=wxT[kt * 128:(kt + 1) * 128, :],
                  stream="wxl", writes=[("wxsb",)])
        p.dma("pool", out=wdtsb[:], in_=wdtT, stream="wdtl", writes=[("wdtsb",)])
        p.dma("sp", out=s_cw[:], in_=cw, stream="scw", writes=[("s_cw",)])
        p.dma("sp", out=s_cb[:], in_=cb, stream="scb", writes=[("s_cb",)])
        p.dma("sp", out=s_dtb[:], in_=dtb, stream="sdtb", writes=[("s_dtb",)])
        p.dma("sp", out=s_aneg[:], in_=alog, stream="saneg", writes=[("s_aneg",)])
        p.dma("sp", out=s_d[:], in_=dsl_in, stream="sdv", writes=[("s_d",)])
        p.op("dve", lambda e: e.memset(cst[:, 0:1], 1.0), writes=[("cst2",)])
        p.op("dve", lambda e: e.memset(cst[:, 1:2], -1.0), writes=[("cst2",)])
        p.op("act", lambda e: e.activation(out=s_aneg[:], in_=s_aneg[:], func=AF.Exp),
             reads=[("s_aneg",)], writes=[("s_aneg",)])
        p.op("dve", lambda e: e.tensor_scalar(out=s_aneg[:], in0=s_aneg[:], scalar1=-1.0,
                                              scalar2=0.0, op0=OP.mult, op1=OP.add),
             reads=[("s_aneg",)], writes=[("s_aneg",)])

        store_keys = []
        for b in range(BATCH):
            for t in range(NT):
                p.op("dve", lambda e, t=t: e.memset(hcar[t][:], 0.0),
                     reads=[("hcar_rd", t)],
                     writes=[("hcar", t, n) for n in range(NST)])
            for ci in range(Lb // SCH):
                c0 = ci * SCH
                for t8 in range(8):
                    p.dma("sp", out=xnb[t8][:], in_=xn_in[b * E + t8 * 128:b * E + (t8 + 1) * 128, c0:c0 + SCH],
                          stream="xnb%d" % t8, writes=[("xnb", t8)])
                for t in range(NT):
                    if ci == 0:
                        p.op("dve", lambda e, t=t: e.memset(u_raw[t][:, 0:DCONV - 1], 0.0),
                             reads=[("u_raw", t)], writes=[("ucar", t)])
                    else:
                        p.op("dve", lambda e, t=t: e.tensor_copy(
                            out=u_raw[t][:, 0:DCONV - 1], in_=u_raw[t][:, SCH:SCH + DCONV - 1]),
                             reads=[("u_raw", t)], writes=[("ucar", t)])
                for mt in range(4):
                    for j in range(NJ):
                        pp = pW[(mt * NJ + j) % 4]
                        pk = ("pW", (mt * NJ + j) % 4)
                        for k in range(8):
                            p.op("pe", lambda e, pp=pp, k=k, mt=mt, j=j, s=(k == 0), t_=(k == 7): e.matmul(
                                out=pp[:], lhsT=winsb[:, k * 512 + mt * 128:k * 512 + (mt + 1) * 128],
                                rhs=xnb[k][:, j * 512:(j + 1) * 512], start=s, stop=t_),
                                 reads=[("winsb",), ("xnb", k)], writes=[pk])
                        if mt < NT:
                            t = mt
                            p.op("act", lambda e, pp=pp, t=t, j=j: e.activation(
                                out=u_raw[t][:, DCONV - 1 + j * 512:DCONV - 1 + (j + 1) * 512],
                                in_=pp[:], func=AF.Copy),
                                 reads=[pk, ("ucar", t)], writes=[("u_raw", t)])
                        else:
                            t = mt - NT
                            p.op("act", lambda e, pp=pp, t=t, j=j: e.activation(
                                out=z[t][:, j * 512:(j + 1) * 512], in_=pp[:], func=AF.Copy),
                                 reads=[pk, ("z_free", t)], writes=[("z", t)])
                for t in range(NT):
                    p.op("dve", lambda e, t=t: e.tensor_scalar(
                        out=conv[t][:], in0=u_raw[t][:, 0:SCH], scalar1=s_cw[:, t * DCONV:t * DCONV + 1],
                        scalar2=0.0, op0=OP.mult, op1=OP.add),
                         reads=[("u_raw", t), ("ucar", t), ("s_cw",), ("conv_rd", t)], writes=[("conv", t)])
                    for kk in range(1, DCONV):
                        p.op("dve", lambda e, t=t, kk=kk: e.scalar_tensor_tensor(
                            out=conv[t][:], in0=u_raw[t][:, kk:kk + SCH],
                            scalar=s_cw[:, t * DCONV + kk:t * DCONV + kk + 1],
                            in1=conv[t][:], op0=OP.mult, op1=OP.add),
                             reads=[("u_raw", t), ("conv", t), ("s_cw",)], writes=[("conv", t)])
                    p.op("act", lambda e, t=t: e.activation(
                        out=us[t][:], in_=conv[t][:], func=AF.Silu, bias=s_cb[:, t:t + 1]),
                         reads=[("conv", t), ("s_cb",), ("us_free", t)], writes=[("us", t), ("conv_rd", t)])
                    p.op("act", lambda e, t=t: e.activation(out=ub[t][:], in_=us[t][:], func=AF.Copy),
                         reads=[("us", t)], writes=[("ub", t)])
                for j in range(NJ):
                    pp = pW[j % 2]
                    pk = ("pW", j % 2)
                    for kt in range(NT):
                        p.op("pe", lambda e, pp=pp, kt=kt, j=j, s=(kt == 0), t_=(kt == NT - 1): e.matmul(
                            out=pp[0:96, :], lhsT=wxsb[:, kt * 96:(kt + 1) * 96],
                            rhs=ub[kt][:, j * 512:(j + 1) * 512], start=s, stop=t_),
                             reads=[("wxsb",), ("ub", kt)], writes=[pk])
                    p.op("act", lambda e, pp=pp, j=j: e.activation(
                        out=dbcs[0:96, j * 512:(j + 1) * 512], in_=pp[0:96, :], func=AF.Copy),
                         reads=[pk, ("dbcs_rd",)], writes=[("dbcs",)])
                p.dma("sp", out=ar_in, in_=dbcs[0:96, :], stream="arin", reads=[("dbcs",)], writes=[("ar_in",), ("dbcs_rd",)])
                p.collective(lambda e: e.collective_compute(
                    "AllReduce", OP.add, replica_groups=[list(range(NCORES))],
                    ins=[ar_in.opt()], outs=[ar_out.opt()]),
                     reads=[("ar_in",)], writes=[("ar_out",)])
                p.dma("pool", out=dtinb[:], in_=ar_out[0:DTR, :], stream="dtinb",
                      reads=[("ar_out",)], writes=[("dtinb",)])
                for mt in range(NT):
                    for j in range(NJ):
                        pp = pW[2 + (j % 2)]
                        pk = ("pW", 2 + (j % 2))
                        p.op("pe", lambda e, pp=pp, mt=mt, j=j: e.matmul(
                            out=pp[:], lhsT=wdtsb[:, mt * 128:(mt + 1) * 128],
                            rhs=dtinb[:, j * 512:(j + 1) * 512], start=True, stop=True),
                             reads=[("wdtsb",), ("dtinb",)], writes=[pk])
                        p.op("act", lambda e, pp=pp, mt=mt, j=j: e.activation(
                            out=dt[mt][:, j * 512:(j + 1) * 512], in_=pp[:], func=AF.Relu,
                            bias=s_dtb[:, mt:mt + 1]),
                             reads=[pk, ("s_dtb",)], writes=[("dt", mt)])
                        p.op("act", lambda e, pp=pp, mt=mt: e.activation(
                            out=sp1[:], in_=pp[:], func=AF.Abs, bias=s_dtb[:, mt:mt + 1]),
                             reads=[pk, ("s_dtb",), ("sp1_rd",)], writes=[("sp1",)])
                        p.op("act", lambda e: e.activation(out=sp2[:], in_=sp1[:], func=AF.Exp,
                                                           scale=cst[:, 1:2]),
                             reads=[("sp1",), ("cst2",), ("sp2_rd",)], writes=[("sp2",), ("sp1_rd",)])
                        p.op("act", lambda e: e.activation(out=sp1[:], in_=sp2[:], func=AF.Ln,
                                                           bias=cst[:, 0:1]),
                             reads=[("sp2",), ("cst2",), ("sp1_rd",)], writes=[("sp1",), ("sp2_rd",)])
                        p.op("dve", lambda e, mt=mt, j=j: e.tensor_tensor(
                            out=dt[mt][:, j * 512:(j + 1) * 512],
                            in0=dt[mt][:, j * 512:(j + 1) * 512], in1=sp1[:], op=OP.add),
                             reads=[("dt", mt), ("sp1",)], writes=[("dt", mt), ("sp1_rd",)])
                for t in range(NT):
                    p.op("dve", lambda e, t=t: e.tensor_tensor(out=dtu[t][:], in0=dt[t][:], in1=us[t][:], op=OP.mult),
                         reads=[("dt", t), ("us", t)], writes=[("dtu", t)])
                for n in range(NST):
                    p.dma("sp", out=bcB[:], in_=ar_out[DTR + n, :].partition_broadcast(128),
                          stream="bcB", reads=[("ar_out",)], writes=[("bcB",)])
                    p.dma("sp", out=bcC[:], in_=ar_out[DTR + NST + n, :].partition_broadcast(128),
                          stream="bcC", reads=[("ar_out",)], writes=[("bcC",)])
                    for t in range(NT):
                        p.op("act", lambda e, t=t, n=n: e.activation(
                            out=dA[:], in_=dt[t][:], func=AF.Exp, scale=s_aneg[:, t * NST + n:t * NST + n + 1]),
                             reads=[("dt", t), ("s_aneg",), ("dA_rd",)], writes=[("dA",)])
                        p.op("dve", lambda e, t=t: e.tensor_tensor(out=dBu[:], in0=dtu[t][:], in1=bcB[:], op=OP.mult),
                             reads=[("dtu", t), ("bcB",), ("dBu_rd",)], writes=[("dBu",)])
                        p.op("dve", lambda e, t=t, n=n: e.tensor_tensor_scan(
                            out=hb[:], data0=dA[:], data1=dBu[:], initial=hcar[t][:, n:n + 1],
                            op0=OP.mult, op1=OP.add),
                             reads=[("dA",), ("dBu",), ("hcar", t, n), ("hb_rd",)],
                             writes=[("hb",), ("dA_rd",), ("dBu_rd",)])
                        p.op("act", lambda e, t=t, n=n: e.activation(
                            out=hcar[t][:, n:n + 1], in_=hb[:, SCH - 1:SCH], func=AF.Copy),
                             reads=[("hb",)], writes=[("hcar", t, n), ("hcar_rd", t)])
                        if n == 0:
                            p.op("dve", lambda e, t=t: e.tensor_tensor(out=y[t][:], in0=hb[:], in1=bcC[:], op=OP.mult),
                                 reads=[("hb",), ("bcC",), ("y_free", t)], writes=[("y", t), ("hb_rd",)])
                        else:
                            p.op("dve", lambda e, t=t: e.tensor_tensor(out=hcb[:], in0=hb[:], in1=bcC[:], op=OP.mult),
                                 reads=[("hb",), ("bcC",), ("hcb_rd",)], writes=[("hcb",), ("hb_rd",)])
                            p.op("dve", lambda e, t=t: e.tensor_tensor(out=y[t][:], in0=y[t][:], in1=hcb[:], op=OP.add),
                                 reads=[("y", t), ("hcb",)], writes=[("y", t), ("hcb_rd",)])

                for t in range(NT):
                    p.op("dve", lambda e, t=t: e.scalar_tensor_tensor(
                        out=y[t][:], in0=us[t][:], scalar=s_d[:, t:t + 1], in1=y[t][:],
                        op0=OP.mult, op1=OP.add),
                         reads=[("us", t), ("y", t), ("s_d",)], writes=[("y", t), ("us_free", t)])
                    p.op("act", lambda e, t=t: e.activation(out=z[t][:], in_=z[t][:], func=AF.Silu),
                         reads=[("z", t)], writes=[("z", t)])
                    p.op("dve", lambda e, t=t: e.tensor_tensor(out=ygb[:], in0=y[t][:], in1=z[t][:], op=OP.mult),
                         reads=[("y", t), ("z", t), ("ygb_rd",)], writes=[("ygb",), ("y_free", t), ("z_free", t)])
                    skey = ("yg_st", b, ci, t)
                    p.dma("sp", out=yg_out[b * DSL + t * 128:b * DSL + (t + 1) * 128, c0:c0 + SCH],
                          in_=ygb[:], stream="ygst", reads=[("ygb",)], writes=[skey, ("ygb_rd",)])
                    store_keys.append(skey)

        p.op("pool", lambda e: e.nop(nofuse=True), reads=store_keys[-2 * NT:])
        with nc.Block() as block:
            p.emit(block)
    return nc



def build_launch3(T=TPC):
    nc = bass.Bass("TRN2", target_bir_lowering=False, debug=False)
    yg_in = nc.dram_tensor("yg_in", [DIN, T], BF16, kind="ExternalInput").ap()
    x2p_in = nc.dram_tensor("x2p_in", [E, T], F32, kind="ExternalInput").ap()
    woutT = nc.dram_tensor("woutT", [DIN, E], F32, kind="ExternalInput").ap()
    k0 = nc.dram_tensor("k0", [2 * E, 4096], U8, kind="ExternalInput").ap()
    k1 = nc.dram_tensor("k1", [4096, 4096], U8, kind="ExternalInput").ap()
    k2 = nc.dram_tensor("k2", [4096, E], U8, kind="ExternalInput").ap()
    s0r = nc.dram_tensor("s0r", [128, 1], F32, kind="ExternalInput").ap()
    s1r = nc.dram_tensor("s1r", [128, 1], F32, kind="ExternalInput").ap()
    s2r = nc.dram_tensor("s2r", [128, 1], F32, kind="ExternalInput").ap()
    out_d = nc.dram_tensor("outT", [E, T], F32, kind="ExternalOutput").ap()

    with contextlib.ExitStack() as stack:
        env = {}
        sb = lambda name, shape, dt: stack.enter_context(nc.sbuf_tensor(name, shape, dt))
        make_ffn_env(nc, stack, env, nwb=1)
        woutsb = sb("woutsb", [128, 16 * 1024], BF16)
        ygt = [sb("ygt%d" % t, [128, CH], BF16) for t in range(16)]
        x3 = [sb("x3_%d" % t, [128, CH], F32) for t in range(8)]
        nxb = [sb("nxb%d" % t, [128, CH], BF16) for t in range(8)]
        s_s0 = sb("s_s0", [128, 1], F32)
        s_s1 = sb("s_s1", [128, 1], F32)
        s_s2 = sb("s_s2", [128, 1], F32)

        p = Prog(nc, stack)
        p.dma("sp", out=s_s0[:], in_=s0r, stream="ss0", writes=[("s0x4", "in")])
        p.dma("sp", out=s_s1[:], in_=s1r, stream="ss1", writes=[("s1x2", "in")])
        p.dma("sp", out=s_s2[:], in_=s2r, stream="ss2", writes=[("halfs2", "in")])
        emit_consts_scales(p, env, s_s0, s_s1, s_s2)
        for k in range(16):
            p.dma("pool", out=woutsb[:, k * 1024:(k + 1) * 1024], in_=woutT[k * 128:(k + 1) * 128, :],
                  stream="woutl", writes=[("woutsb",)])
        wf0, wf1, wf2 = setup_ffn_weights(nc, p, env, k0, k1, k2)

        nch = T // CH
        store_keys = []
        for c in range(nch):
            c0 = c * CH
            for t in range(16):
                p.dma("sp", out=ygt[t][:], in_=yg_in[t * 128:(t + 1) * 128, c0:c0 + CH],
                      stream="ygt%d" % t, writes=[("ygt", t)])
            for t in range(8):
                p.dma("sp", out=x3[t][:], in_=x2p_in[t * 128:(t + 1) * 128, c0:c0 + CH],
                      stream="x3l%d" % t, writes=[("x3", t)])
            for mt in range(8):
                pp = env["pL"][mt % 4]
                pk = ("pL", mt % 4)
                for k in range(16):
                    p.op("pe", lambda e, pp=pp, k=k, mt=mt, s=(k == 0), t_=(k == 15): e.matmul(
                        out=pp[:], lhsT=woutsb[:, k * 1024 + mt * 128:k * 1024 + (mt + 1) * 128],
                        rhs=ygt[k][:], start=s, stop=t_),
                         reads=[("woutsb",), ("ygt", k)], writes=[pk])
                p.op("dve", lambda e, pp=pp, mt=mt: e.scalar_tensor_tensor(
                    out=x3[mt][:], in0=pp[:], scalar=env["cst"][:, 0:1], in1=x3[mt][:],
                    op0=OP.mult, op1=OP.add),
                     reads=[pk, ("x3", mt), ("cst",)], writes=[("x3", mt)])

            emit_ln(p, env, [(x3[t][:], ("x3", t)) for t in range(8)],
                    [(nxb[t][:], ("nxb", t)) for t in range(8)], 8, CH, gb=None)
            emit_ffn(p, env, [(nxb[t][:], ("nxb", t)) for t in range(8)],
                     [(x3[t][:], ("x3", t)) for t in range(8)],
                     wf0, wf1, wf2, 8, 32, 32, 8)
            for t in range(8):
                skey = ("out_st", c, t)
                p.dma("sp", out=out_d[t * 128:(t + 1) * 128, c0:c0 + CH], in_=x3[t][:],
                      stream="outst", reads=[("x3", t)], writes=[skey])
                store_keys.append(skey)
        p.op("pool", lambda e: e.nop(nofuse=True), reads=store_keys[-8:])
        with nc.Block() as block:
            p.emit(block)
    return nc




def make_persist_env(nc, stack, env):
    sb = lambda name, shape, dt: stack.enter_context(nc.sbuf_tensor(name, shape, dt))
    env["cst"] = sb("cst", [128, 8], F32)
    env["onesb"] = sb("onesb", [128, 1], BF16)
    env["ones1f"] = sb("ones1f", [1, 128], F32)
    env["s0x4"] = sb("s0x4", [128, 1], F32)
    env["s1x2"] = sb("s1x2", [128, 1], F32)
    env["halfs2"] = sb("halfs2", [128, 1], F32)


def make_ffn_phase_env(nc, stack, env, nwb=2, tag=""):
    sb = lambda name, shape, dt: stack.enter_context(nc.sbuf_tensor(tag + name, shape, dt))
    ps = lambda name, shape: stack.enter_context(nc.psum_tensor(tag + name, shape, F32))
    env["wbuf0"] = [sb("wbuf0_%d" % i, [128, 8 * 128], BF16) for i in range(3)]
    env["wbuf1"] = [sb("wbuf1_%d" % i, [128, 32 * 128], BF16) for i in range(3)]
    env["wbuf2"] = [sb("wbuf2_%d" % i, [128, 32 * 128], BF16) for i in range(nwb)]
    env["h1b"] = sb("h1b", [128, 32 * CH], BF16)
    env["h2b"] = sb("h2b", [128, 32 * CH], BF16)
    env["pL"] = [ps("pL%d" % i, [128, CH]) for i in range(4)]
    env["lnxb"] = sb("lnxb", [128, 4 * CH], BF16)
    env["lnsq"] = sb("lnsq", [128, 4 * CH], BF16)
    env["lnscr"] = [sb("lnscr%d" % i, [128, CH], F32) for i in range(2)]
    env["rows"] = sb("rows", [1, 4 * CH], F32)
    env["pstatA"] = ps("pstatA", [1, CH])
    env["pstatB"] = ps("pstatB", [1, CH])
    env["pbc1"] = ps("pbc1", [128, CH])
    env["pbc2"] = ps("pbc2", [128, CH])


def phase_barrier(p, keys):
    for e in Prog.ENGINES:
        p.op(e, lambda en: en.nop(nofuse=True), reads=list(keys))


def build_fused(T=TPC, SCH=512):
    SCH = min(SCH, T)
    Lb = 4 * T
    NT = DSL // 128
    NJ = SCH // 512
    NCH1 = T // CH
    NQ = T // SCH
    nc = bass.Bass("TRN2", target_bir_lowering=False, debug=False)
    xT = nc.dram_tensor("xT", [E, T], F32, kind="ExternalInput").ap()
    k0 = nc.dram_tensor("k0", [2 * E, 4096], U8, kind="ExternalInput").ap()
    k1 = nc.dram_tensor("k1", [4096, 4096], U8, kind="ExternalInput").ap()
    k2 = nc.dram_tensor("k2", [4096, E], U8, kind="ExternalInput").ap()
    s0r = nc.dram_tensor("s0r", [128, 1], F32, kind="ExternalInput").ap()
    s1r = nc.dram_tensor("s1r", [128, 1], F32, kind="ExternalInput").ap()
    s2r = nc.dram_tensor("s2r", [128, 1], F32, kind="ExternalInput").ap()
    lng = nc.dram_tensor("lng", [128, 8], F32, kind="ExternalInput").ap()
    lnb = nc.dram_tensor("lnb", [128, 8], F32, kind="ExternalInput").ap()
    winT = nc.dram_tensor("winT", [E, 2 * DSL], F32, kind="ExternalInput").ap()
    wxT = nc.dram_tensor("wxT", [DSL, 96], F32, kind="ExternalInput").ap()
    wdtT = nc.dram_tensor("wdtT", [DTR, DSL], F32, kind="ExternalInput").ap()
    cw = nc.dram_tensor("cw", [128, 2 * DCONV], F32, kind="ExternalInput").ap()
    cb = nc.dram_tensor("cb", [128, 2], F32, kind="ExternalInput").ap()
    dtb = nc.dram_tensor("dtb", [128, 2], F32, kind="ExternalInput").ap()
    alog = nc.dram_tensor("alog", [128, 2 * NST], F32, kind="ExternalInput").ap()
    dsl_in = nc.dram_tensor("dsl", [128, 2], F32, kind="ExternalInput").ap()
    woutTs = nc.dram_tensor("woutTs", [DSL, E], F32, kind="ExternalInput").ap()
    out_d = nc.dram_tensor("outT", [E, T], F32, kind="ExternalOutput").ap()

    x2p_d = nc.dram_tensor("x2p_d", [E, T], F32).ap()
    ag_in = [nc.dram_tensor("ag_in%d" % k, [E, CH], BF16).ap() for k in range(NCH1)]
    ag_out = [nc.dram_tensor("ag_out%d" % k, [NCORES * E, CH], BF16).ap() for k in range(NCH1)]
    rs_in = [nc.dram_tensor("rs_in%d" % q, [NCORES * E, SCH], F32).ap() for q in range(NQ)]
    rs_out = [nc.dram_tensor("rs_out%d" % q, [E, SCH], F32).ap() for q in range(NQ)]
    ar_in = [[nc.dram_tensor("ar_in%d_%d" % (b, i), [96, SCH], F32).ap() for i in range(3)]
             for b in range(BATCH)]
    ar_out = [[nc.dram_tensor("ar_out%d_%d" % (b, i), [96, SCH], F32).ap() for i in range(3)]
              for b in range(BATCH)]

    with contextlib.ExitStack() as stack:
        env = {}
        make_persist_env(nc, stack, env)
        sbp = lambda name, shape, dt: stack.enter_context(nc.sbuf_tensor(name, shape, dt))
        s_s0 = sbp("s_s0", [128, 1], F32)
        s_s1 = sbp("s_s1", [128, 1], F32)
        s_s2 = sbp("s_s2", [128, 1], F32)
        p = Prog(nc, stack)
        p.dma("sp", out=s_s0[:], in_=s0r, stream="ss0", writes=[("s0x4", "in")])
        p.dma("sp", out=s_s1[:], in_=s1r, stream="ss1", writes=[("s1x2", "in")])
        p.dma("sp", out=s_s2[:], in_=s2r, stream="ss2", writes=[("halfs2", "in")])
        emit_consts_scales(p, env, s_s0, s_s1, s_s2)

        ph1 = contextlib.ExitStack()
        env1 = dict(env)
        make_ffn_phase_env(nc, ph1, env1, nwb=2, tag="p1")
        sb1 = lambda name, shape, dt: ph1.enter_context(nc.sbuf_tensor(name, shape, dt))
        env1["cva"] = sb1("cva", [128, 512], BF16)
        env1["cvb"] = sb1("cvb", [128, 512], BF16)
        env1["cvw"] = sb1("cvw", [128, 512], BF16)
        xc = [[sb1("xc%d_%d" % (i, t), [128, CH], F32) for t in range(8)] for i in range(2)]
        xin = [sb1("xin%d" % t, [128, CH], F32) for t in range(8)]
        nxb = [[sb1("nxb%d_%d" % (i, t), [128, CH], BF16) for t in range(8)] for i in range(2)]

        xnout = [sb1("xnout%d" % t, [128, CH], BF16) for t in range(8)]
        s_g = sb1("s_g", [128, 8], F32)
        s_b = sb1("s_b", [128, 8], F32)
        p.dma("sp", out=s_g[:], in_=lng, stream="sgv", writes=[("s_g",)])
        p.dma("sp", out=s_b[:], in_=lnb, stream="sbv", writes=[("s_gb",)])

        def p1_load(c):
            i = c % 2
            for t in range(8):
                p.dma("pool", out=xc[i][t][:], in_=xT[t * 128:(t + 1) * 128, c * CH:(c + 1) * CH],
                      stream="xc%d" % t, writes=[("xc", i, t)])

        def p1_ln1(c):
            i = c % 2
            emit_ln(p, env1, [(xc[i][t][:], ("xc", i, t)) for t in range(8)],
                    [(nxb[i][t][:], ("nxb", i, t)) for t in range(8)], 8, CH, gb=None)

        def p1_tail1(c):
            i = c % 2
            emit_ln(p, env1, [(xc[i][t][:], ("xc", i, t)) for t in range(8)],
                    [(xin[t][:], ("xin", t)) for t in range(8)], 8, CH, gb=None)
            for t in range(8):
                p.op("dve", lambda e, i=i, t=t: e.tensor_tensor(
                    out=xc[i][t][:], in0=xc[i][t][:], in1=xin[t][:], op=OP.add),
                     reads=[("xc", i, t), ("xin", t)], writes=[("xc", i, t)])
                p.dma("act", out=x2p_d[t * 128:(t + 1) * 128, c * CH:(c + 1) * CH], in_=xc[i][t][:],
                      stream="x2pst", reads=[("xc", i, t)], writes=[("x2p_st", c, t)])

        def p1_tail2(c):
            i = c % 2
            emit_ln(p, env1, [(xin[t][:], ("xin", t)) for t in range(8)],
                    [(xnout[t][:], ("xnout", t)) for t in range(8)], 8, CH,
                    gb=[(s_g[:, t:t + 1], s_b[:, t:t + 1]) for t in range(8)],
                    gbkeys=[("s_g",), ("s_gb",)])
            for t in range(8):
                p.dma("act", out=ag_in[c][t * 128:(t + 1) * 128, :], in_=xnout[t][:],
                      stream="xnst", reads=[("xnout", t)], writes=[("xn_st", c, t)])
            p.collective(lambda e, c=c: e.collective_compute(
                "AllGather", OP.bypass, replica_groups=[list(range(NCORES))],
                ins=[ag_in[c].opt()], outs=[ag_out[c].opt()]),
                 reads=[("xn_st", c, t) for t in range(8)], writes=[("ag", c)])

        p1_load(0)
        p1_ln1(0)
        wf0, wf1, wf2 = setup_ffn_weights(nc, p, env1, k0, k1, k2)
        for c in range(NCH1):
            i = c % 2
            if c + 1 < NCH1:
                p1_load(c + 1)
            emit_ffn_l1(p, env1, [(nxb[i][t][:], ("nxb", i, t)) for t in range(8)], wf0, 8, 32)
            if c + 1 < NCH1:
                p1_ln1(c + 1)
            if c >= 1:
                p1_tail2(c - 1)
            emit_ffn_l2(p, env1, wf1, 32, 32)
            emit_ffn_l3(p, env1, [(xc[i][t][:], ("xc", i, t)) for t in range(8)], wf2, 32, 8)
            p1_tail1(c)
        p1_tail2(NCH1 - 1)
        ph1.close()
        phase_barrier(p, [("x2p_st", NCH1 - 1, 7), ("wfdone", 2), ("cvw",), ("cva",), ("cvb",),
                          ("ag", NCH1 - 1)])

        ph2 = contextlib.ExitStack()
        sb2 = lambda name, shape, dt: ph2.enter_context(nc.sbuf_tensor(name, shape, dt))
        ps2 = lambda name, shape: ph2.enter_context(nc.psum_tensor(name, shape, F32))
        xnb = [sb2("xnb%d" % t, [128, SCH], BF16) for t in range(8)]
        winsb = sb2("winsb", [128, 8 * 4 * 128], BF16)
        wxsb = sb2("wxsb", [128, 2 * 96], BF16)
        wdtsb = sb2("wdtsb", [DTR, 2 * 128], BF16)
        woutsb2 = sb2("woutsb2", [128, 2 * 1024], BF16)
        s_cw = sb2("s_cw", [128, 2 * DCONV], F32)
        s_cb = sb2("s_cb", [128, 2], F32)
        s_dtb = sb2("s_dtb", [128, 2], F32)
        s_aneg = sb2("s_aneg", [128, 2 * NST], F32)
        s_d = sb2("s_d", [128, 2], F32)
        cst2 = sb2("cstl2", [128, 2], F32)
        u_raw = [[[sb2("ur%d_%d_%d" % (b, i, t), [128, DCONV - 1 + SCH], F32) for t in range(NT)]
                  for i in range(3)] for b in range(BATCH)]
        zt = [[[sb2("z%d_%d_%d" % (b, i, t), [128, SCH], F32) for t in range(NT)]
               for i in range(3)] for b in range(BATCH)]
        ust = [[[sb2("us%d_%d_%d" % (b, i, t), [128, SCH], F32) for t in range(NT)]
                for i in range(3)] for b in range(BATCH)]
        conv = [sb2("conv%d" % t, [128, SCH], F32) for t in range(NT)]
        ub = [sb2("ub%d" % t, [128, SCH], BF16) for t in range(NT)]
        dbcs = sb2("dbcs", [96, SCH], F32)
        dtinb = sb2("dtinb", [DTR, SCH], BF16)
        dt = [[sb2("dt%d_%d" % (b, t), [128, SCH], F32) for t in range(NT)] for b in range(BATCH)]
        dtu = [sb2("dtu%d" % t, [128, SCH], F32) for t in range(NT)]
        sp1 = sb2("sp1", [128, 512], F32)
        sp2 = sb2("sp2", [128, 512], F32)
        bcB = [sb2("bcB%d" % i, [128, SCH], F32) for i in range(3)]
        bcC = [sb2("bcC%d" % i, [128, SCH], F32) for i in range(3)]
        dA = [[sb2("dA%d_%d" % (t, i), [128, SCH], F32) for i in range(2)] for t in range(NT)]
        dBu = [sb2("dBu%d" % t, [128, SCH], F32) for t in range(NT)]
        hb = [sb2("hb%d" % i, [128, SCH], F32) for i in range(4)]
        hcb = [sb2("hcb%d" % t, [128, SCH], F32) for t in range(NT)]
        y = [sb2("y%d" % t, [128, SCH], F32) for t in range(NT)]
        ygb = [[sb2("ygb%d_%d" % (b, t), [128, SCH], BF16) for t in range(NT)] for b in range(BATCH)]
        motile = [sb2("motile%d" % i, [128, 512], F32) for i in range(2)]
        hcar = [[sb2("hcar%d_%d" % (b, t), [128, NST], F32) for t in range(NT)] for b in range(BATCH)]
        pW = [ps2("pW%d" % i, [128, 512]) for i in range(4)]

        for k in range(8):
            p.dma("pool", out=winsb[:, k * 512:(k + 1) * 512], in_=winT[k * 128:(k + 1) * 128, :],
                  stream="winl", writes=[("winsb",)])
        for kt in range(NT):
            p.dma("pool", out=wxsb[:, kt * 96:(kt + 1) * 96], in_=wxT[kt * 128:(kt + 1) * 128, :],
                  stream="wxl", writes=[("wxsb",)])
        p.dma("pool", out=wdtsb[:], in_=wdtT, stream="wdtl", writes=[("wdtsb",)])
        for kt in range(NT):
            p.dma("pool", out=woutsb2[:, kt * 1024:(kt + 1) * 1024], in_=woutTs[kt * 128:(kt + 1) * 128, :],
                  stream="wo2l", writes=[("woutsb2",)])
        p.dma("sp", out=s_cw[:], in_=cw, stream="scw", writes=[("s_cw",)])
        p.dma("sp", out=s_cb[:], in_=cb, stream="scb", writes=[("s_cb",)])
        p.dma("sp", out=s_dtb[:], in_=dtb, stream="sdtb", writes=[("s_dtb",)])
        p.dma("sp", out=s_aneg[:], in_=alog, stream="saneg", writes=[("s_aneg",)])
        p.dma("sp", out=s_d[:], in_=dsl_in, stream="sdv", writes=[("s_d",)])
        p.op("dve", lambda e: e.memset(cst2[:, 0:1], 1.0), writes=[("cst2",)])
        p.op("dve", lambda e: e.memset(cst2[:, 1:2], -1.0), writes=[("cst2",)])
        p.op("act", lambda e: e.activation(out=s_aneg[:], in_=s_aneg[:], func=AF.Exp),
             reads=[("s_aneg",)], writes=[("s_aneg",)])
        p.op("dve", lambda e: e.tensor_scalar(out=s_aneg[:], in0=s_aneg[:], scalar1=-1.0,
                                              scalar2=0.0, op0=OP.mult, op1=OP.add),
             reads=[("s_aneg",)], writes=[("s_aneg",)])
        for b in range(BATCH):
            for t in range(NT):
                p.op("dve", lambda e, b=b, t=t: e.memset(hcar[b][t][:], 0.0),
                     writes=[("hcar", b, t, n) for n in range(NST)])

        def rro(b, ci):
            r = b * 4 + ci % 4
            o = (ci // 4) * SCH
            return r, o

        def pre_phase(b, ci):
            par = ci % 3
            r, o = rro(b, ci)
            kc = o // CH
            ur = u_raw[b][par]
            z = zt[b][par]
            us = ust[b][par]
            for t8 in range(8):
                p.dma("pool", out=xnb[t8][:],
                      in_=ag_out[kc][r * E + t8 * 128:r * E + (t8 + 1) * 128, :],
                      stream="xnb%d" % t8, reads=[("ag", kc)], writes=[("xnb", t8)])
            for t in range(NT):
                if ci == 0:
                    p.op("dve", lambda e, b=b, par=par, t=t: e.memset(ur[t][:, 0:DCONV - 1], 0.0),
                         reads=[("ur", b, par, t)], writes=[("ucar", b, par, t)])
                else:
                    pprev = (ci - 1) % 3
                    p.op("dve", lambda e, b=b, par=par, t=t, pprev=pprev: e.tensor_copy(
                        out=u_raw[b][par][t][:, 0:DCONV - 1],
                        in_=u_raw[b][pprev][t][:, SCH:SCH + DCONV - 1]),
                         reads=[("ur", b, pprev, t), ("ur", b, par, t)],
                         writes=[("ucar", b, par, t)])
            for mt in range(4):
                pp = pW[mt % 4]
                pk = ("pW", mt % 4)
                for k in range(8):
                    p.op("pe", lambda e, pp=pp, k=k, mt=mt, s=(k == 0), t_=(k == 7): e.matmul(
                        out=pp[:], lhsT=winsb[:, k * 512 + mt * 128:k * 512 + (mt + 1) * 128],
                        rhs=xnb[k][:], start=s, stop=t_),
                         reads=[("winsb",), ("xnb", k)], writes=[pk])
                if mt < NT:
                    t = mt
                    p.op("act", lambda e, pp=pp, t=t: e.activation(
                        out=ur[t][:, DCONV - 1:DCONV - 1 + SCH], in_=pp[:], func=AF.Copy),
                         reads=[pk, ("ucar", b, par, t)], writes=[("ur", b, par, t)])
                else:
                    t = mt - NT
                    p.op("act", lambda e, pp=pp, t=t: e.activation(
                        out=z[t][:], in_=pp[:], func=AF.Copy),
                         reads=[pk, ("z", b, par, t)], writes=[("z", b, par, t)])
            for t in range(NT):
                p.op("dve", lambda e, t=t, ur=ur: e.tensor_scalar(
                    out=conv[t][:], in0=ur[t][:, 0:SCH], scalar1=s_cw[:, t * DCONV:t * DCONV + 1],
                    scalar2=0.0, op0=OP.mult, op1=OP.add),
                     reads=[("ur", b, par, t), ("ucar", b, par, t), ("s_cw",), ("conv", t)],
                     writes=[("conv", t)])
                for kk in range(1, DCONV):
                    p.op("dve", lambda e, t=t, kk=kk, ur=ur: e.scalar_tensor_tensor(
                        out=conv[t][:], in0=ur[t][:, kk:kk + SCH],
                        scalar=s_cw[:, t * DCONV + kk:t * DCONV + kk + 1],
                        in1=conv[t][:], op0=OP.mult, op1=OP.add),
                         reads=[("ur", b, par, t), ("conv", t), ("s_cw",)], writes=[("conv", t)])
                p.op("act", lambda e, t=t, us=us: e.activation(
                    out=us[t][:], in_=conv[t][:], func=AF.Silu, bias=s_cb[:, t:t + 1]),
                     reads=[("conv", t), ("s_cb",), ("us", b, par, t)], writes=[("us", b, par, t)])
                p.op("act", lambda e, t=t, us=us: e.activation(out=ub[t][:], in_=us[t][:], func=AF.Copy),
                     reads=[("us", b, par, t), ("ub", t)], writes=[("ub", t)])
            pp = pW[0]
            pk = ("pW", 0)
            for kt in range(NT):
                p.op("pe", lambda e, pp=pp, kt=kt, s=(kt == 0), t_=(kt == NT - 1): e.matmul(
                    out=pp[0:96, :], lhsT=wxsb[:, kt * 96:(kt + 1) * 96],
                    rhs=ub[kt][:], start=s, stop=t_),
                     reads=[("wxsb",), ("ub", kt)], writes=[pk])
            p.op("act", lambda e, pp=pp: e.activation(
                out=dbcs[0:96, :], in_=pp[0:96, :], func=AF.Copy),
                 reads=[pk, ("dbcs",)], writes=[("dbcs",)])
            p.dma("sp", out=ar_in[b][par], in_=dbcs[0:96, :], stream="arin%d%d" % (b, par),
                  reads=[("dbcs",)], writes=[("ar_in", b, par)])
            p.collective(lambda e, b=b, par=par: e.collective_compute(
                "AllReduce", OP.add, replica_groups=[list(range(NCORES))],
                ins=[ar_in[b][par].opt()], outs=[ar_out[b][par].opt()]),
                 reads=[("ar_in", b, par), ("ar_out", b, par)], writes=[("ar_out", b, par)])

        def post_dt(b, ci):
            par = ci % 3
            p.dma("pool", out=dtinb[:], in_=ar_out[b][par][0:DTR, :], stream="dtinb",
                  reads=[("ar_out", b, par), ("dtinb",)], writes=[("dtinb",)])
            for mt in range(NT):
                pp = pW[2 + mt % 2]
                pk = ("pW", 2 + mt % 2)
                p.op("pe", lambda e, pp=pp, mt=mt: e.matmul(
                    out=pp[:], lhsT=wdtsb[:, mt * 128:(mt + 1) * 128],
                    rhs=dtinb[:], start=True, stop=True),
                     reads=[("wdtsb",), ("dtinb",)], writes=[pk])
                p.op("act", lambda e, pp=pp, b=b, mt=mt: e.activation(
                    out=dt[b][mt][:], in_=pp[:], func=AF.Relu, bias=s_dtb[:, mt:mt + 1]),
                     reads=[pk, ("s_dtb",), ("dt", b, mt)], writes=[("dt", b, mt)])
                p.op("act", lambda e, pp=pp, mt=mt: e.activation(
                    out=sp1[:], in_=pp[:], func=AF.Abs, bias=s_dtb[:, mt:mt + 1]),
                     reads=[pk, ("s_dtb",), ("sp1",)], writes=[("sp1",)])
                p.op("act", lambda e: e.activation(out=sp2[:], in_=sp1[:], func=AF.Exp,
                                                   scale=cst2[:, 1:2]),
                     reads=[("sp1",), ("cst2",), ("sp2",)], writes=[("sp2",)])
                p.op("act", lambda e: e.activation(out=sp1[:], in_=sp2[:], func=AF.Ln,
                                                   bias=cst2[:, 0:1]),
                     reads=[("sp2",), ("cst2",), ("sp1",)], writes=[("sp1",)])
                p.op("dve", lambda e, b=b, mt=mt: e.tensor_tensor(
                    out=dt[b][mt][:], in0=dt[b][mt][:], in1=sp1[:], op=OP.add),
                     reads=[("dt", b, mt), ("sp1",)], writes=[("dt", b, mt)])

        def post_scan(b, ci, mid=None):
            par = ci % 3
            r, o = rro(b, ci)
            q = o // SCH
            us = ust[b][par]
            z = zt[b][par]
            for t in range(NT):
                p.op("dve", lambda e, b=b, t=t, us=us: e.tensor_tensor(
                    out=dtu[t][:], in0=dt[b][t][:], in1=us[t][:], op=OP.mult),
                     reads=[("dt", b, t), ("us", b, par, t), ("dtu", t)], writes=[("dtu", t)])
                p.op("act", lambda e, t=t, z=z: e.activation(out=z[t][:], in_=z[t][:], func=AF.Silu),
                     reads=[("z", b, par, t)], writes=[("z", b, par, t)])
            for n in range(NST):
                if n == 8 and mid is not None:
                    mid()
                i2 = n % 3
                iA = n % 2
                p.dma("sp", out=bcB[i2][:], in_=ar_out[b][par][DTR + n, :].partition_broadcast(128),
                      stream="bcB%d" % i2, reads=[("ar_out", b, par), ("bcB", i2)], writes=[("bcB", i2)])
                p.dma("pool", out=bcC[i2][:], in_=ar_out[b][par][DTR + NST + n, :].partition_broadcast(128),
                      stream="bcC%d" % i2, reads=[("ar_out", b, par), ("bcC", i2)], writes=[("bcC", i2)])
                for t in range(NT):
                    hi = (n * NT + t) % 4
                    p.op("act", lambda e, b=b, t=t, n=n, iA=iA: e.activation(
                        out=dA[t][iA][:], in_=dt[b][t][:], func=AF.Exp,
                        scale=s_aneg[:, t * NST + n:t * NST + n + 1]),
                         reads=[("dt", b, t), ("s_aneg",), ("dA", t, iA)], writes=[("dA", t, iA)])
                    p.op("dve", lambda e, t=t, i2=i2: e.tensor_tensor(
                        out=dBu[t][:], in0=dtu[t][:], in1=bcB[i2][:], op=OP.mult),
                         reads=[("dtu", t), ("bcB", i2), ("dBu", t)], writes=[("dBu", t)])
                    p.op("dve", lambda e, b=b, t=t, n=n, hi=hi, iA=iA: e.tensor_tensor_scan(
                        out=hb[hi][:], data0=dA[t][iA][:], data1=dBu[t][:],
                        initial=hcar[b][t][:, n:n + 1], op0=OP.mult, op1=OP.add),
                         reads=[("dA", t, iA), ("dBu", t), ("hcar", b, t, n), ("hb", hi)],
                         writes=[("hb", hi)])
                    p.op("act", lambda e, b=b, t=t, n=n, hi=hi: e.activation(
                        out=hcar[b][t][:, n:n + 1], in_=hb[hi][:, SCH - 1:SCH], func=AF.Copy),
                         reads=[("hb", hi)], writes=[("hcar", b, t, n)])
                    if n == 0:
                        p.op("dve", lambda e, t=t, i2=i2, hi=hi: e.tensor_tensor(
                            out=y[t][:], in0=hb[hi][:], in1=bcC[i2][:], op=OP.mult),
                             reads=[("hb", hi), ("bcC", i2), ("y", t)], writes=[("y", t)])
                    else:
                        p.op("dve", lambda e, t=t, i2=i2, hi=hi: e.tensor_tensor(
                            out=hcb[t][:], in0=hb[hi][:], in1=bcC[i2][:], op=OP.mult),
                             reads=[("hb", hi), ("bcC", i2), ("hcb", t)], writes=[("hcb", t)])
                        p.op("dve", lambda e, t=t: e.tensor_tensor(
                            out=y[t][:], in0=y[t][:], in1=hcb[t][:], op=OP.add),
                             reads=[("y", t), ("hcb", t)], writes=[("y", t)])
            for t in range(NT):
                p.op("dve", lambda e, b=b, t=t, us=us: e.scalar_tensor_tensor(
                    out=y[t][:], in0=us[t][:], scalar=s_d[:, t:t + 1], in1=y[t][:],
                    op0=OP.mult, op1=OP.add),
                     reads=[("us", b, par, t), ("y", t), ("s_d",)], writes=[("y", t)])
                p.op("dve", lambda e, b=b, t=t, z=z: e.tensor_tensor(
                    out=ygb[b][t][:], in0=y[t][:], in1=z[t][:], op=OP.mult),
                     reads=[("y", t), ("z", b, par, t), ("ygb", b, t)], writes=[("ygb", b, t)])

        def post_wout(b, ci):
            r, o = rro(b, ci)
            q = o // SCH
            for mt in range(8):
                pp = pW[mt % 2]
                pk = ("pW", mt % 2)
                for kt in range(NT):
                    p.op("pe", lambda e, pp=pp, b=b, kt=kt, mt=mt, s=(kt == 0), t_=(kt == NT - 1): e.matmul(
                        out=pp[:], lhsT=woutsb2[:, kt * 1024 + mt * 128:kt * 1024 + (mt + 1) * 128],
                        rhs=ygb[b][kt][:], start=s, stop=t_),
                         reads=[("woutsb2",), ("ygb", b, kt)], writes=[pk])
                mo = motile[mt % 2]
                mok = ("motile", mt % 2)
                p.op("act", lambda e, pp=pp, mo=mo: e.activation(out=mo[:], in_=pp[:], func=AF.Copy),
                     reads=[pk, mok], writes=[mok])
                p.dma("act", out=rs_in[q][r * E + mt * 128:r * E + (mt + 1) * 128, :],
                      in_=mo[:], stream="rsst", reads=[mok], writes=[("rs_st", q)])

        NCI = Lb // SCH
        for b in range(BATCH):
            pre_phase(b, 0)
        for b in range(BATCH):
            if NCI > 1:
                pre_phase(b, 1)
        for b in range(BATCH):
            post_dt(b, 0)
        def fire_rs(q):
            p.collective(lambda e, q=q: e.collective_compute(
                "ReduceScatter", OP.add, replica_groups=[list(range(NCORES))],
                ins=[rs_in[q].opt()], outs=[rs_out[q].opt()]),
                 reads=[("rs_st", q)], writes=[("rs_done", q)])

        for ci in range(NCI):
            for b in range(BATCH):
                def mid(b=b, ci=ci):
                    if ci + 1 < NCI:
                        post_dt(b, ci + 1)
                    if b == 0:
                        if ci >= 1:
                            post_wout(1, ci - 1)
                            if (ci - 1) % 4 == 3:
                                fire_rs((ci - 1) // 4)
                    else:
                        post_wout(0, ci)
                post_scan(b, ci, mid=mid)
                if ci + 2 < NCI:
                    pre_phase(b, ci + 2)
        post_wout(1, NCI - 1)
        fire_rs(NQ - 1)
        ph2.close()
        hkeys = [("hcar", b, t, n) for b in range(BATCH) for t in range(NT) for n in range(NST)]
        phase_barrier(p, [("rs_done", q) for q in range(NQ)] + hkeys
                      + [("ucar", b, par, t) for b in range(BATCH) for par in range(3) for t in range(NT)])

        ph3 = contextlib.ExitStack()
        env3 = dict(env)
        make_ffn_phase_env(nc, ph3, env3, nwb=2, tag="p3")
        sb3 = lambda name, shape, dt: ph3.enter_context(nc.sbuf_tensor(name, shape, dt))
        x3 = [[sb3("x3_%d_%d" % (i, t), [128, CH], F32) for t in range(8)] for i in range(2)]
        mo3 = [sb3("mo3_%d" % t, [128, CH], F32) for t in range(8)]
        nxb3 = [[sb3("nxb3_%d_%d" % (i, t), [128, CH], BF16) for t in range(8)] for i in range(2)]
        store_keys = []

        def p3_load(c):
            i = c % 2
            c0 = c * CH
            q = c0 // SCH
            qo = c0 - q * SCH
            for t in range(8):
                p.dma("pool", out=x3[i][t][:], in_=x2p_d[t * 128:(t + 1) * 128, c0:c0 + CH],
                      stream="x3l%d" % t, writes=[("x3", i, t)])
                p.dma("pool", out=mo3[t][:], in_=rs_out[q][t * 128:(t + 1) * 128, qo:qo + CH],
                      stream="mo%d" % t, reads=[("rs_done", q)], writes=[("mo3", t)])

        def p3_ln(c):
            i = c % 2
            for t in range(8):
                p.op("dve", lambda e, i=i, t=t: e.tensor_tensor(
                    out=x3[i][t][:], in0=x3[i][t][:], in1=mo3[t][:], op=OP.add),
                     reads=[("x3", i, t), ("mo3", t)], writes=[("x3", i, t)])
            emit_ln(p, env3, [(x3[i][t][:], ("x3", i, t)) for t in range(8)],
                    [(nxb3[i][t][:], ("nxb3", i, t)) for t in range(8)], 8, CH, gb=None)

        p3_load(0)
        p3_ln(0)
        for c in range(NCH1):
            i = c % 2
            if c + 1 < NCH1:
                p3_load(c + 1)
            emit_ffn_l1(p, env3, [(nxb3[i][t][:], ("nxb3", i, t)) for t in range(8)], wf0, 8, 32)
            if c + 1 < NCH1:
                p3_ln(c + 1)
            emit_ffn_l2(p, env3, wf1, 32, 32)
            emit_ffn_l3(p, env3, [(x3[i][t][:], ("x3", i, t)) for t in range(8)], wf2, 32, 8)
            for t in range(8):
                skey = ("out_st", c, t)
                p.dma("act", out=out_d[t * 128:(t + 1) * 128, c * CH:(c + 1) * CH], in_=x3[i][t][:],
                      stream="outst", reads=[("x3", i, t)], writes=[skey])
                store_keys.append(skey)
        ph3.close()
        p.op("pool", lambda e: e.nop(nofuse=True), reads=store_keys[-8:])
        with nc.Block() as block:
            p.emit(block)
    return nc


def _forward_fused(inputs):
    x = np.asarray(inputs["L0_attn_x"], np.float32)
    L = x.shape[1]
    T = L // 4
    s0, s1, s2, k0, k1, k2 = _prep_common(inputs)
    lng = np.ascontiguousarray(np.asarray(inputs["L0_attn_ln_g"], np.float32).reshape(8, 128).T)
    lnb = np.ascontiguousarray(np.asarray(inputs["L0_attn_ln_b"], np.float32).reshape(8, 128).T)
    W_in = np.asarray(inputs["L0_attn_W_in"], np.float32)
    W_x = np.asarray(inputs["L0_attn_W_x"], np.float32)
    W_dt = np.asarray(inputs["L0_attn_W_dt"], np.float32)
    conv_w = np.asarray(inputs["L0_attn_conv_w"], np.float32)
    conv_b = np.asarray(inputs["L0_attn_conv_b"], np.float32)
    dt_bias = np.asarray(inputs["L0_attn_dt_bias"], np.float32)
    A_log = np.asarray(inputs["L0_attn_A_log"], np.float32)
    Dv = np.asarray(inputs["L0_attn_D"], np.float32)
    W_out = np.asarray(inputs["L0_attn_W_out"], np.float32)

    nc = build_fused(T=T)
    CHW = 512
    nstr = T // CHW
    in_maps = []
    for c in range(NCORES):
        b = c // 4
        stripes = [(4 * j + c % 4) * CHW for j in range(nstr)]
        xTc = np.concatenate([x[b, s:s + CHW, :].T for s in stripes], axis=1)
        sl = slice(c * DSL, (c + 1) * DSL)
        rows = np.r_[c * DSL:(c + 1) * DSL, DIN + c * DSL:DIN + (c + 1) * DSL]
        in_maps.append(dict(
            xT=np.ascontiguousarray(xTc),
            k0=k0, k1=k1, k2=k2, s0r=s0, s1r=s1, s2r=s2, lng=lng, lnb=lnb,
            winT=np.ascontiguousarray(W_in[rows].T),
            wxT=np.ascontiguousarray(W_x[:, sl].T),
            wdtT=np.ascontiguousarray(W_dt[sl].T),
            cw=_tile2(conv_w[sl]), cb=_tile2(conv_b[sl]), dtb=_tile2(dt_bias[sl]),
            alog=_tile2(A_log[sl]), dsl=_tile2(Dv[sl]),
            woutTs=np.ascontiguousarray(W_out.T[sl]),
        ))
    res = run_bass_kernel_spmd(nc, in_maps, core_ids=list(range(NCORES))).results
    out = np.empty((BATCH, L, E), np.float32)
    for c in range(NCORES):
        b = c // 4
        o = np.asarray(res[c]["outT"])
        for j in range(nstr):
            s = (4 * j + c % 4) * CHW
            out[b, s:s + CHW, :] = o[:, j * CHW:(j + 1) * CHW].T
    return out



def _prep_common(inputs):
    s0 = np.ascontiguousarray(np.broadcast_to(np.asarray(inputs["L0_ffn_s0"], np.float32).reshape(1, 1), (128, 1)))
    s1 = np.ascontiguousarray(np.broadcast_to(np.asarray(inputs["L0_ffn_s1"], np.float32).reshape(1, 1), (128, 1)))
    s2 = np.ascontiguousarray(np.broadcast_to(np.asarray(inputs["L0_ffn_s2"], np.float32).reshape(1, 1), (128, 1)))
    k0 = np.ascontiguousarray(np.asarray(inputs["L0_ffn_k0"]).view(np.uint8))
    k1 = np.ascontiguousarray(np.asarray(inputs["L0_ffn_k1"]).view(np.uint8))
    k2 = np.ascontiguousarray(np.asarray(inputs["L0_ffn_k2"]).view(np.uint8))
    return s0, s1, s2, k0, k1, k2


def _tile2(v):
    v = np.asarray(v, np.float32)
    if v.ndim == 1:
        v = v[:, None]
    k = v.shape[1]
    return np.ascontiguousarray(v.reshape(2, 128, k).transpose(1, 0, 2).reshape(128, 2 * k))


def kernel(**inputs):
    return _forward_fused(inputs)


def _forward(inputs, collect=False):
    x = np.asarray(inputs["L0_attn_x"], np.float32)
    L = x.shape[1]
    T = L // 4
    s0, s1, s2, k0, k1, k2 = _prep_common(inputs)
    lng = np.ascontiguousarray(np.asarray(inputs["L0_attn_ln_g"], np.float32).reshape(8, 128).T)
    lnb = np.ascontiguousarray(np.asarray(inputs["L0_attn_ln_b"], np.float32).reshape(8, 128).T)

    nc1 = build_launch1(T=T)
    in1 = []
    for c in range(NCORES):
        b, lo = c // 4, (c % 4) * T
        xT = np.ascontiguousarray(x[b, lo:lo + T, :].T)
        in1.append(dict(xT=xT, k0=k0, k1=k1, k2=k2, s0r=s0, s1r=s1, s2r=s2, lng=lng, lnb=lnb))
    r1 = run_bass_kernel_spmd(nc1, in1, core_ids=list(range(NCORES))).results
    xn_full = np.empty((BATCH * E, L), dtype=BF)
    x2p = [np.ascontiguousarray(np.asarray(r1[c]["x2p_out"])) for c in range(NCORES)]
    for c in range(NCORES):
        b, lo = c // 4, (c % 4) * T
        xn_full[b * E:(b + 1) * E, lo:lo + T] = np.asarray(r1[c]["xn_out"])

    W_in = np.asarray(inputs["L0_attn_W_in"], np.float32)
    W_x = np.asarray(inputs["L0_attn_W_x"], np.float32)
    W_dt = np.asarray(inputs["L0_attn_W_dt"], np.float32)
    conv_w = np.asarray(inputs["L0_attn_conv_w"], np.float32)
    conv_b = np.asarray(inputs["L0_attn_conv_b"], np.float32)
    dt_bias = np.asarray(inputs["L0_attn_dt_bias"], np.float32)
    A_log = np.asarray(inputs["L0_attn_A_log"], np.float32)
    Dv = np.asarray(inputs["L0_attn_D"], np.float32)
    nc2 = build_launch2(Lb=L, SCH=1024)
    in2 = []
    for c in range(NCORES):
        sl = slice(c * DSL, (c + 1) * DSL)
        rows = np.r_[c * DSL:(c + 1) * DSL, DIN + c * DSL:DIN + (c + 1) * DSL]
        in2.append(dict(
            xn_in=xn_full,
            winT=np.ascontiguousarray(W_in[rows].T),
            wxT=np.ascontiguousarray(W_x[:, sl].T),
            wdtT=np.ascontiguousarray(W_dt[sl].T),
            cw=_tile2(conv_w[sl]),
            cb=_tile2(conv_b[sl]),
            dtb=_tile2(dt_bias[sl]),
            alog=_tile2(A_log[sl]),
            dsl=_tile2(Dv[sl]),
        ))
    r2 = run_bass_kernel_spmd(nc2, in2, core_ids=list(range(NCORES))).results
    yg_full = np.empty((BATCH, DIN, L), dtype=BF)
    for c in range(NCORES):
        ygc = np.asarray(r2[c]["yg_out"]).reshape(BATCH, DSL, L)
        yg_full[:, c * DSL:(c + 1) * DSL, :] = ygc

    W_out = np.asarray(inputs["L0_attn_W_out"], np.float32)
    nc3 = build_launch3(T=T)
    in3 = []
    for c in range(NCORES):
        b, lo = c // 4, (c % 4) * T
        in3.append(dict(
            yg_in=np.ascontiguousarray(yg_full[b, :, lo:lo + T]),
            x2p_in=x2p[c],
            woutT=np.ascontiguousarray(W_out.T),
            k0=k0, k1=k1, k2=k2, s0r=s0, s1r=s1, s2r=s2,
        ))
    r3 = run_bass_kernel_spmd(nc3, in3, core_ids=list(range(NCORES))).results
    out = np.empty((BATCH, L, E), np.float32)
    for c in range(NCORES):
        b, lo = c // 4, (c % 4) * T
        out[b, lo:lo + T, :] = np.asarray(r3[c]["outT"]).T
    inter = None
    if collect:
        x2p_full = np.empty((BATCH * E, L), np.float32)
        for c in range(NCORES):
            b, lo = c // 4, (c % 4) * T
            x2p_full[b * E:(b + 1) * E, lo:lo + T] = x2p[c]
        inter = dict(xn=xn_full, x2p=x2p_full, yg=yg_full)
    return out, inter


# version 21
# speedup vs baseline: 1.1564x; 1.0002x over previous
import contextlib

import numpy as np
import ml_dtypes

import concourse.bass as bass
import concourse.mybir as mybir
from concourse.bass_utils import run_bass_kernel_spmd

F32 = mybir.dt.float32
BF16 = mybir.dt.bfloat16
U8 = mybir.dt.uint8
AF = mybir.ActivationFunctionType
OP = mybir.AluOpType
BF = ml_dtypes.bfloat16

NCORES = 8
E = 1024
DIN = 2048
NST = 16
DCONV = 4
DTR = 64
BATCH = 2
SEQ = 16384
TPC = SEQ // 4
DSL = DIN // NCORES
CH = 512
EPS = 1e-5


class Prog:

    ENGINES = ("sp", "act", "dve", "pe", "pool")

    def __init__(self, nc, stack):
        self.nc = nc
        self.stack = stack
        self.sems = {}
        self.squeues = {}
        self.equeues = {e: [] for e in self.ENGINES}
        for e in self.ENGINES:
            self._get_sem(e)
        self.last_writer = {}
        self.readers = {}

    def _get_sem(self, key):
        if key not in self.sems:
            self.sems[key] = self.stack.enter_context(self.nc.semaphore("sem_" + str(key)))
            self.squeues[key] = []
        return self.sems[key]

    def _append(self, engine, fn, reads, writes, sem_key, inc, flagged):
        need = {}

        def add(src):
            if src is None:
                return
            sk, idx, eng2 = src
            if eng2 == engine and sk == "pe" and engine == "pe":
                return
            if need.get(sk, -1) < idx:
                need[sk] = idx
        for r in reads:
            add(self.last_writer.get(r))
        for w in writes:
            add(self.last_writer.get(w))
            for rr in self.readers.get(w, ()):
                add(rr)
        for sk, idx in need.items():
            self.squeues[sk][idx][3] = True
        rec = [need, fn, inc, flagged, sem_key]
        self.equeues[engine].append(rec)
        sidx = len(self.squeues[sem_key])
        self.squeues[sem_key].append(rec)
        src = (sem_key, sidx, engine)
        for r in reads:
            self.readers.setdefault(r, []).append(src)
        for w in writes:
            self.last_writer[w] = src
            self.readers[w] = []

    def op(self, engine, fn, reads=(), writes=(), inc=1):
        self._append(engine, fn, reads, writes, engine, inc, False)

    def dma(self, engine, out, in_, stream, reads=(), writes=(), **kw):
        self._get_sem(stream)
        self._append(engine, lambda e: e.dma_start(out=out, in_=in_, **kw),
                     reads, writes, stream, 16, True)

    def collective(self, fn, reads=(), writes=()):
        self._get_sem("cc")
        self._append("pool", fn, reads, writes, "cc", 1, True)

    def emit(self, block):
        decos = dict(sp=block.sync, act=block.scalar, dve=block.vector,
                     pe=block.tensor, pool=block.gpsimd)
        cums = {}
        for sk, q in self.squeues.items():
            c = 0
            cl = []
            for rec in q:
                if rec[3]:
                    c += rec[2]
                cl.append(c)
            cums[sk] = cl
        for ename in self.ENGINES:
            queue = self.equeues[ename]
            if not queue:
                continue

            def body(eng, queue=queue):
                floor = {}
                for (need, fn, inc, flagged, sem_key) in queue:
                    for sk, idx in need.items():
                        val = cums[sk][idx]
                        if floor.get(sk, 0) < val:
                            eng.wait_ge(self.sems[sk], val)
                            floor[sk] = val
                    instr = fn(eng)
                    if flagged:
                        instr.then_inc(self.sems[sem_key], inc)

            decos[ename](body)



def emit_ln(p, env, src, dst, ntiles, width, gb=None, gbkeys=(("s_gb",),)):
    xb, sq = env["lnxb"], env["lnsq"]
    pstatA, pstatB = env["pstatA"], env["pstatB"]
    rows = env["rows"]
    pbc1, pbc2 = env["pbc1"], env["pbc2"]
    onesb, ones1f, cst = env["onesb"], env["ones1f"], env["cst"]
    scr = env["lnscr"]
    inv_n = 1.0 / (128 * ntiles)

    for t in range(ntiles):
        a, k = src[t]
        s2_ = t % 4
        p.op("act", lambda e, a=a, s2_=s2_: e.activation(
            out=xb[:, s2_ * width:(s2_ + 1) * width], in_=a, func=AF.Copy),
             reads=[k], writes=[("lnxb", s2_)])
        p.op("act", lambda e, a=a, s2_=s2_: e.activation(
            out=sq[:, s2_ * width:(s2_ + 1) * width], in_=a, func=AF.Square),
             reads=[k], writes=[("lnsq", s2_)])
        p.op("pe", lambda e, s2_=s2_, t=t: e.matmul(out=pstatA[0:1, :], lhsT=onesb[:, 0:1],
                                                    rhs=xb[:, s2_ * width:(s2_ + 1) * width],
                                                    start=(t == 0), stop=(t == ntiles - 1)),
             reads=[("lnxb", s2_), ("onesb",)], writes=[("pstatA",)])
        p.op("pe", lambda e, s2_=s2_, t=t: e.matmul(out=pstatB[0:1, :], lhsT=onesb[:, 0:1],
                                                    rhs=sq[:, s2_ * width:(s2_ + 1) * width],
                                                    start=(t == 0), stop=(t == ntiles - 1)),
             reads=[("lnsq", s2_), ("onesb",)], writes=[("pstatB",)])
    m_row = rows[0:1, 0:width]
    var_row = rows[0:1, width:2 * width]
    rstd_row = rows[0:1, 2 * width:3 * width]
    tr_row = rows[0:1, 3 * width:4 * width]
    p.op("dve", lambda e: e.tensor_scalar(out=m_row, in0=pstatA[0:1, :], scalar1=inv_n,
                                          scalar2=0.0, op0=OP.mult, op1=OP.add),
         reads=[("pstatA",)], writes=[("m_row",)])
    p.op("act", lambda e: e.activation(out=var_row, in_=m_row, func=AF.Square),
         reads=[("m_row",)], writes=[("var_row",)])
    p.op("dve", lambda e: e.scalar_tensor_tensor(out=var_row, in0=pstatB[0:1, :], scalar=inv_n,
                                                 in1=var_row, op0=OP.mult, op1=OP.subtract),
         reads=[("pstatB",), ("var_row",)], writes=[("var_row",)])
    p.op("act", lambda e: e.activation(out=rstd_row, in_=var_row, func=AF.Ln, bias=cst[0:1, 2:3]),
         reads=[("var_row",), ("cst",)], writes=[("rstd_row",)])
    p.op("act", lambda e: e.activation(out=rstd_row, in_=rstd_row, func=AF.Exp, scale=cst[0:1, 3:4]),
         reads=[("rstd_row",), ("cst",)], writes=[("rstd_row",)])
    p.op("dve", lambda e: e.tensor_tensor(out=tr_row, in0=m_row, in1=rstd_row, op=OP.mult),
         reads=[("m_row",), ("rstd_row",)], writes=[("tr_row",)])
    p.op("pe", lambda e: e.matmul(out=pbc1[:, 0:width], lhsT=ones1f[0:1, :], rhs=tr_row,
                                  start=True, stop=True),
         reads=[("tr_row",), ("ones1f",)], writes=[("pbc1",)])
    p.op("pe", lambda e: e.matmul(out=pbc2[:, 0:width], lhsT=ones1f[0:1, :], rhs=rstd_row,
                                  start=True, stop=True),
         reads=[("rstd_row",), ("ones1f",)], writes=[("pbc2",)])
    for t in range(ntiles):
        a, k = src[t]
        d, dk = dst[t]
        s = scr[t % 2]
        sk = ("lnscr", t % 2)
        p.op("dve", lambda e, a=a, s=s: e.tensor_tensor(out=s[:], in0=a, in1=pbc2[:, 0:width], op=OP.mult),
             reads=[k, ("pbc2",)], writes=[sk])
        if gb is None:
            p.op("dve", lambda e, s=s, d=d: e.tensor_tensor(out=d, in0=s[:], in1=pbc1[:, 0:width], op=OP.subtract),
                 reads=[sk, ("pbc1",)], writes=[dk])
        else:
            p.op("dve", lambda e, s=s: e.tensor_tensor(out=s[:], in0=s[:], in1=pbc1[:, 0:width], op=OP.subtract),
                 reads=[sk, ("pbc1",)], writes=[sk])
            g_ap, b_ap = gb[t]
            p.op("dve", lambda e, s=s, d=d, g_ap=g_ap, b_ap=b_ap: e.tensor_scalar(
                out=d, in0=s[:], scalar1=g_ap, scalar2=b_ap, op0=OP.mult, op1=OP.add),
                 reads=[sk] + list(gbkeys), writes=[dk])


def emit_weight_convert(p, env, kdram, wf_ap, n_ktiles, n_mtiles, mode, tag):
    cva, cvb, cvw = env["cva"], env["cvb"], env["cvw"]
    width = n_mtiles * 128
    HW = min(width, 512)
    mth = HW // 128
    for kt in range(n_ktiles):
        for h in range(width // HW):
            c0 = h * HW
            if mode == "fold":
                p.dma("pool", out=cva[:, 0:HW], in_=kdram[kt * 128:(kt + 1) * 128, c0:c0 + HW],
                      stream="cva", reads=[("cva_rd",)], writes=[("cva",)])
                p.dma("pool", out=cvb[:, 0:HW], in_=kdram[E + kt * 128:E + (kt + 1) * 128, c0:c0 + HW],
                      stream="cvb", reads=[("cvb_rd",)], writes=[("cvb",)])
                p.op("dve", lambda e: e.tensor_tensor(out=cvw[:, 0:HW], in0=cva[:, 0:HW],
                                                      in1=cvb[:, 0:HW], op=OP.subtract),
                     reads=[("cva",), ("cvb",), ("cvw_rd",)], writes=[("cvw",), ("cva_rd",), ("cvb_rd",)])
            else:
                p.dma("pool", out=cva[:, 0:HW], in_=kdram[kt * 128:(kt + 1) * 128, c0:c0 + HW],
                      stream="cva", reads=[("cva_rd",)], writes=[("cva",)])
                p.op("dve", lambda e: e.tensor_scalar(out=cvw[:, 0:HW], in0=cva[:, 0:HW],
                                                      scalar1=2.0, scalar2=-1.0, op0=OP.mult, op1=OP.add),
                     reads=[("cva",), ("cvw_rd",)], writes=[("cvw",), ("cva_rd",)])
            for mtl in range(mth):
                mt = h * mth + mtl
                last = (mtl == mth - 1)
                p.dma("sp", out=wf_ap[mt, kt], in_=cvw[:, mtl * 128:(mtl + 1) * 128],
                      stream="wfst", reads=[("cvw",)],
                      writes=[(tag, mt, kt), ("cvw_rd",)] if last else [(tag, mt, kt)])


def emit_ffn(p, env, nxb_tiles, res_tiles, wf0, wf1, wf2, n1k, n1m, n2m, n3m):
    emit_ffn_l1(p, env, nxb_tiles, wf0, n1k, n1m)
    emit_ffn_l2(p, env, wf1, n1m, n2m)
    emit_ffn_l3(p, env, res_tiles, wf2, n2m, n3m)


def emit_ffn_l1(p, env, nxb_tiles, wf0, n1k, n1m):
    wbuf0 = env["wbuf0"]
    h1b = env["h1b"]
    pL = env["pL"]
    s0x4 = env["s0x4"]
    for mt in range(n1m):
        wb = wbuf0[mt % len(wbuf0)]
        wk = ("wbuf0", mt % len(wbuf0))
        p.dma("sp", out=wb[:].rearrange("p (k f) -> p k f", k=n1k), in_=wf0[mt],
              stream="wb0_%d" % (mt % len(wbuf0)), reads=[wk + ("free",), ("wfdone", 0)], writes=[wk])
        ps = pL[mt % 4]
        pk = ("pL", mt % 4)
        for k in range(n1k):
            a, ak = nxb_tiles[k]
            last = (k == n1k - 1)
            p.op("pe", lambda e, wb=wb, k=k, a=a, ps=ps, s=(k == 0), t=last: e.matmul(
                out=ps[:], lhsT=wb[:, k * 128:(k + 1) * 128], rhs=a, start=s, stop=t),
                 reads=[wk, ak], writes=[pk, wk + ("free",)] if last else [pk])
        p.op("act", lambda e, ps=ps, mt=mt: e.activation(
            out=h1b[:, mt * CH:(mt + 1) * CH], in_=ps[:], func=AF.Relu, scale=s0x4[:]),
             reads=[pk, ("s0x4",)], writes=[("h1b", mt)])


def emit_ffn_l2(p, env, wf1, n1m, n2m):
    wbuf1 = env["wbuf1"]
    h1b, h2b = env["h1b"], env["h2b"]
    pL = env["pL"]
    s1x2 = env["s1x2"]
    for mt in range(n2m):
        wb = wbuf1[mt % len(wbuf1)]
        wk = ("wbuf1", mt % len(wbuf1))
        p.dma("sp", out=wb[:].rearrange("p (k f) -> p k f", k=n1m), in_=wf1[mt],
              stream="wb1_%d" % (mt % len(wbuf1)), reads=[wk + ("free",), ("wfdone", 1)], writes=[wk])
        ps = pL[mt % 4]
        pk = ("pL", mt % 4)
        for k in range(n1m):
            last = (k == n1m - 1)
            p.op("pe", lambda e, wb=wb, k=k, ps=ps, s=(k == 0), t=last: e.matmul(
                out=ps[:], lhsT=wb[:, k * 128:(k + 1) * 128],
                rhs=h1b[:, k * CH:(k + 1) * CH], start=s, stop=t),
                 reads=[wk, ("h1b", k)], writes=[pk, wk + ("free",)] if last else [pk])
        p.op("act", lambda e, ps=ps, mt=mt: e.activation(
            out=h2b[:, mt * CH:(mt + 1) * CH], in_=ps[:], func=AF.Relu, scale=s1x2[:]),
             reads=[pk, ("s1x2",)], writes=[("h2b", mt)])


def emit_ffn_l3(p, env, res_tiles, wf2, n2m, n3m):
    wbuf2 = env["wbuf2"]
    h2b = env["h2b"]
    pL = env["pL"]
    halfs2 = env["halfs2"]
    for mt in range(n3m):
        wb = wbuf2[mt % len(wbuf2)]
        wk = ("wbuf2", mt % len(wbuf2))
        p.dma("sp", out=wb[:].rearrange("p (k f) -> p k f", k=n2m), in_=wf2[mt],
              stream="wb2_%d" % (mt % len(wbuf2)), reads=[wk + ("free",), ("wfdone", 2)], writes=[wk])
        ps = pL[mt % 4]
        pk = ("pL", mt % 4)
        for k in range(n2m):
            last = (k == n2m - 1)
            p.op("pe", lambda e, wb=wb, k=k, ps=ps, s=(k == 0), t=last: e.matmul(
                out=ps[:], lhsT=wb[:, k * 128:(k + 1) * 128],
                rhs=h2b[:, k * CH:(k + 1) * CH], start=s, stop=t),
                 reads=[wk, ("h2b", k)], writes=[pk, wk + ("free",)] if last else [pk])
        r, rk = res_tiles[mt]
        p.op("dve", lambda e, ps=ps, r=r: e.scalar_tensor_tensor(
            out=r, in0=ps[:], scalar=halfs2[:], in1=r, op0=OP.mult, op1=OP.add),
             reads=[pk, rk, ("halfs2",)], writes=[rk])


def make_ffn_env(nc, stack, env, nwb=2):
    sb = lambda name, shape, dt: stack.enter_context(nc.sbuf_tensor(name, shape, dt))
    ps = lambda name, shape: stack.enter_context(nc.psum_tensor(name, shape, F32))
    env["cst"] = sb("cst", [128, 8], F32)
    env["onesb"] = sb("onesb", [128, 1], BF16)
    env["ones1f"] = sb("ones1f", [1, 128], F32)
    env["cva"] = sb("cva", [128, 2048], BF16)
    env["cvb"] = sb("cvb", [128, 2048], BF16)
    env["cvw"] = sb("cvw", [128, 2048], BF16)
    env["wbuf0"] = [sb("wbuf0_%d" % i, [128, 8 * 128], BF16) for i in range(3)]
    env["wbuf1"] = [sb("wbuf1_%d" % i, [128, 32 * 128], BF16) for i in range(3)]
    env["wbuf2"] = [sb("wbuf2_%d" % i, [128, 32 * 128], BF16) for i in range(nwb)]
    env["h1b"] = sb("h1b", [128, 32 * CH], BF16)
    env["h2b"] = sb("h2b", [128, 32 * CH], BF16)
    env["pL"] = [ps("pL%d" % i, [128, CH]) for i in range(4)]
    env["s0x4"] = sb("s0x4", [128, 1], F32)
    env["s1x2"] = sb("s1x2", [128, 1], F32)
    env["halfs2"] = sb("halfs2", [128, 1], F32)
    env["lnxb"] = sb("lnxb", [128, 4 * CH], BF16)
    env["lnsq"] = sb("lnsq", [128, 4 * CH], BF16)
    env["lnscr"] = [sb("lnscr%d" % i, [128, CH], F32) for i in range(2)]
    env["rows"] = sb("rows", [1, 4 * CH], F32)
    env["pstatA"] = ps("pstatA", [1, CH])
    env["pstatB"] = ps("pstatB", [1, CH])
    env["pbc1"] = ps("pbc1", [128, CH])
    env["pbc2"] = ps("pbc2", [128, CH])


def emit_consts_scales(p, env, s_s0, s_s1, s_s2):
    cst, onesb, ones1f = env["cst"], env["onesb"], env["ones1f"]
    p.op("dve", lambda e: e.memset(cst[:, 0:1], 1.0), writes=[("cst",)])
    p.op("dve", lambda e: e.memset(cst[:, 1:2], -1.0), writes=[("cst",)])
    p.op("dve", lambda e: e.memset(cst[:, 2:3], EPS), writes=[("cst",)])
    p.op("dve", lambda e: e.memset(cst[:, 3:4], -0.5), writes=[("cst",)])
    p.op("dve", lambda e: e.memset(onesb[:], 1.0), writes=[("onesb",)])
    p.op("dve", lambda e: e.memset(ones1f[:], 1.0), writes=[("ones1f",)])
    for name, src, f in (("s0x4", s_s0, 4.0), ("s1x2", s_s1, 2.0), ("halfs2", s_s2, 0.5)):
        dst = env[name]
        p.op("dve", lambda e, dst=dst, src=src, f=f: e.tensor_scalar(
            out=dst[:], in0=src[:], scalar1=f, scalar2=0.0, op0=OP.mult, op1=OP.add),
             reads=[(name, "in")], writes=[(name,)])


def setup_ffn_weights(nc, p, env, k0, k1, k2):
    wf0d = nc.dram_tensor("wf0d", [32, 8, 128, 128], BF16).ap()
    wf1d = nc.dram_tensor("wf1d", [32, 32, 128, 128], BF16).ap()
    wf2d = nc.dram_tensor("wf2d", [8, 32, 128, 128], BF16).ap()
    emit_weight_convert(p, env, k0, wf0d, 8, 32, "fold", "wf0")
    p.op("dve", lambda e: e.memset(env["cst"][:, 4:5], 0.0),
         reads=[("wf0", 31, 7)], writes=[("wfdone", 0)])
    emit_weight_convert(p, env, k1, wf1d, 32, 32, "pm1", "wf1")
    p.op("dve", lambda e: e.memset(env["cst"][:, 5:6], 0.0),
         reads=[("wf1", 31, 31)], writes=[("wfdone", 1)])
    emit_weight_convert(p, env, k2, wf2d, 32, 8, "pm1", "wf2")
    p.op("dve", lambda e: e.memset(env["cst"][:, 6:7], 0.0),
         reads=[("wf2", 7, 31)], writes=[("wfdone", 2)])
    wf0 = [wf0d[mt].rearrange("k p f -> p k f") for mt in range(32)]
    wf1 = [wf1d[mt].rearrange("k p f -> p k f") for mt in range(32)]
    wf2 = [wf2d[mt].rearrange("k p f -> p k f") for mt in range(8)]
    return wf0, wf1, wf2



def build_launch1(T=TPC):
    nc = bass.Bass("TRN2", target_bir_lowering=False, debug=False)
    xT = nc.dram_tensor("xT", [E, T], F32, kind="ExternalInput").ap()
    k0 = nc.dram_tensor("k0", [2 * E, 4096], U8, kind="ExternalInput").ap()
    k1 = nc.dram_tensor("k1", [4096, 4096], U8, kind="ExternalInput").ap()
    k2 = nc.dram_tensor("k2", [4096, E], U8, kind="ExternalInput").ap()
    s0r = nc.dram_tensor("s0r", [128, 1], F32, kind="ExternalInput").ap()
    s1r = nc.dram_tensor("s1r", [128, 1], F32, kind="ExternalInput").ap()
    s2r = nc.dram_tensor("s2r", [128, 1], F32, kind="ExternalInput").ap()
    lng = nc.dram_tensor("lng", [128, 8], F32, kind="ExternalInput").ap()
    lnb = nc.dram_tensor("lnb", [128, 8], F32, kind="ExternalInput").ap()
    xn_out = nc.dram_tensor("xn_out", [E, T], BF16, kind="ExternalOutput").ap()
    x2p_out = nc.dram_tensor("x2p_out", [E, T], F32, kind="ExternalOutput").ap()

    with contextlib.ExitStack() as stack:
        env = {}
        sb = lambda name, shape, dt: stack.enter_context(nc.sbuf_tensor(name, shape, dt))
        make_ffn_env(nc, stack, env)
        xc = [sb("xc%d" % t, [128, CH], F32) for t in range(8)]
        xin = [sb("xin%d" % t, [128, CH], F32) for t in range(8)]
        nxb = [sb("nxb%d" % t, [128, CH], BF16) for t in range(8)]
        s_s0 = sb("s_s0", [128, 1], F32)
        s_s1 = sb("s_s1", [128, 1], F32)
        s_s2 = sb("s_s2", [128, 1], F32)
        s_g = sb("s_g", [128, 8], F32)
        s_b = sb("s_b", [128, 8], F32)

        p = Prog(nc, stack)
        p.dma("sp", out=s_s0[:], in_=s0r, stream="ss0", writes=[("s0x4", "in")])
        p.dma("sp", out=s_s1[:], in_=s1r, stream="ss1", writes=[("s1x2", "in")])
        p.dma("sp", out=s_s2[:], in_=s2r, stream="ss2", writes=[("halfs2", "in")])
        p.dma("sp", out=s_g[:], in_=lng, stream="sgv", writes=[("s_g",)])
        p.dma("sp", out=s_b[:], in_=lnb, stream="sbv", writes=[("s_gb",)])
        emit_consts_scales(p, env, s_s0, s_s1, s_s2)
        wf0, wf1, wf2 = setup_ffn_weights(nc, p, env, k0, k1, k2)

        nch = T // CH
        store_keys = []
        for c in range(nch):
            c0 = c * CH
            for t in range(8):
                p.dma("sp", out=xc[t][:], in_=xT[t * 128:(t + 1) * 128, c0:c0 + CH],
                      stream="xc%d" % t, writes=[("xc", t)])
            emit_ln(p, env, [(xc[t][:], ("xc", t)) for t in range(8)],
                    [(nxb[t][:], ("nxb", t)) for t in range(8)], 8, CH, gb=None)
            emit_ffn(p, env, [(nxb[t][:], ("nxb", t)) for t in range(8)],
                     [(xc[t][:], ("xc", t)) for t in range(8)],
                     wf0, wf1, wf2, 8, 32, 32, 8)
            emit_ln(p, env, [(xc[t][:], ("xc", t)) for t in range(8)],
                    [(xin[t][:], ("xin", t)) for t in range(8)], 8, CH, gb=None)
            for t in range(8):
                p.op("dve", lambda e, t=t: e.tensor_tensor(out=xc[t][:], in0=xc[t][:], in1=xin[t][:], op=OP.add),
                     reads=[("xc", t), ("xin", t)], writes=[("xc", t)])
                skey = ("x2p_st", c, t)
                p.dma("sp", out=x2p_out[t * 128:(t + 1) * 128, c0:c0 + CH], in_=xc[t][:],
                      stream="x2pst", reads=[("xc", t)], writes=[skey])
                store_keys.append(skey)
            emit_ln(p, env, [(xin[t][:], ("xin", t)) for t in range(8)],
                    [(nxb[t][:], ("nxb", t)) for t in range(8)], 8, CH,
                    gb=[(s_g[:, t:t + 1], s_b[:, t:t + 1]) for t in range(8)], gbkeys=[("s_g",), ("s_gb",)])
            for t in range(8):
                skey = ("xn_st", c, t)
                p.dma("sp", out=xn_out[t * 128:(t + 1) * 128, c0:c0 + CH], in_=nxb[t][:],
                      stream="xnst", reads=[("nxb", t)], writes=[skey])
                store_keys.append(skey)
        p.op("pool", lambda e: e.nop(nofuse=True), reads=store_keys[-16:])

        with nc.Block() as block:
            p.emit(block)
    return nc



def build_launch2(Lb=SEQ, SCH=1024):
    nc = bass.Bass("TRN2", target_bir_lowering=False, debug=False)
    xn_in = nc.dram_tensor("xn_in", [BATCH * E, Lb], BF16, kind="ExternalInput").ap()
    winT = nc.dram_tensor("winT", [E, 2 * DSL], F32, kind="ExternalInput").ap()
    wxT = nc.dram_tensor("wxT", [DSL, 96], F32, kind="ExternalInput").ap()
    wdtT = nc.dram_tensor("wdtT", [DTR, DSL], F32, kind="ExternalInput").ap()
    cw = nc.dram_tensor("cw", [128, 2 * DCONV], F32, kind="ExternalInput").ap()
    cb = nc.dram_tensor("cb", [128, 2], F32, kind="ExternalInput").ap()
    dtb = nc.dram_tensor("dtb", [128, 2], F32, kind="ExternalInput").ap()
    alog = nc.dram_tensor("alog", [128, 2 * NST], F32, kind="ExternalInput").ap()
    dsl_in = nc.dram_tensor("dsl", [128, 2], F32, kind="ExternalInput").ap()
    yg_out = nc.dram_tensor("yg_out", [BATCH * DSL, Lb], BF16, kind="ExternalOutput").ap()

    ar_in = nc.dram_tensor("ar_in", [96, SCH], F32).ap()
    ar_out = nc.dram_tensor("ar_out", [96, SCH], F32).ap()

    NT = DSL // 128
    NJ = SCH // 512

    with contextlib.ExitStack() as stack:
        sb = lambda name, shape, dt: stack.enter_context(nc.sbuf_tensor(name, shape, dt))
        psf = lambda name, shape: stack.enter_context(nc.psum_tensor(name, shape, F32))
        xnb = [sb("xnb%d" % t, [128, SCH], BF16) for t in range(8)]
        winsb = sb("winsb", [128, 8 * 4 * 128], BF16)
        wxsb = sb("wxsb", [128, 2 * 96], BF16)
        wdtsb = sb("wdtsb", [DTR, 2 * 128], BF16)
        s_cw = sb("s_cw", [128, 2 * DCONV], F32)
        s_cb = sb("s_cb", [128, 2], F32)
        s_dtb = sb("s_dtb", [128, 2], F32)
        s_aneg = sb("s_aneg", [128, 2 * NST], F32)
        s_d = sb("s_d", [128, 2], F32)
        cst = sb("cstl2", [128, 2], F32)
        u_raw = [sb("u_raw%d" % t, [128, DCONV - 1 + SCH], F32) for t in range(NT)]
        z = [sb("z%d" % t, [128, SCH], F32) for t in range(NT)]
        conv = [sb("conv%d" % t, [128, SCH], F32) for t in range(NT)]
        us = [sb("us%d" % t, [128, SCH], F32) for t in range(NT)]
        ub = [sb("ub%d" % t, [128, SCH], BF16) for t in range(NT)]
        dbcs = sb("dbcs", [96, SCH], F32)
        dtinb = sb("dtinb", [DTR, SCH], BF16)
        dt = [sb("dt%d" % t, [128, SCH], F32) for t in range(NT)]
        dtu = [sb("dtu%d" % t, [128, SCH], F32) for t in range(NT)]
        sp1 = sb("sp1", [128, 512], F32)
        sp2 = sb("sp2", [128, 512], F32)
        bcB = sb("bcB", [128, SCH], F32)
        bcC = sb("bcC", [128, SCH], F32)
        dA = sb("dA", [128, SCH], F32)
        dBu = sb("dBu", [128, SCH], F32)
        hb = sb("hb", [128, SCH], F32)
        hcb = sb("hcb", [128, SCH], F32)
        y = [sb("y%d" % t, [128, SCH], F32) for t in range(NT)]
        ygb = sb("ygb", [128, SCH], BF16)
        hcar = [sb("hcar%d" % t, [128, NST], F32) for t in range(NT)]
        pW = [psf("pW%d" % i, [128, 512]) for i in range(4)]

        p = Prog(nc, stack)
        for k in range(8):
            p.dma("pool", out=winsb[:, k * 512:(k + 1) * 512], in_=winT[k * 128:(k + 1) * 128, :],
                  stream="winl", writes=[("winsb",)])
        for kt in range(NT):
            p.dma("pool", out=wxsb[:, kt * 96:(kt + 1) * 96], in_=wxT[kt * 128:(kt + 1) * 128, :],
                  stream="wxl", writes=[("wxsb",)])
        p.dma("pool", out=wdtsb[:], in_=wdtT, stream="wdtl", writes=[("wdtsb",)])
        p.dma("sp", out=s_cw[:], in_=cw, stream="scw", writes=[("s_cw",)])
        p.dma("sp", out=s_cb[:], in_=cb, stream="scb", writes=[("s_cb",)])
        p.dma("sp", out=s_dtb[:], in_=dtb, stream="sdtb", writes=[("s_dtb",)])
        p.dma("sp", out=s_aneg[:], in_=alog, stream="saneg", writes=[("s_aneg",)])
        p.dma("sp", out=s_d[:], in_=dsl_in, stream="sdv", writes=[("s_d",)])
        p.op("dve", lambda e: e.memset(cst[:, 0:1], 1.0), writes=[("cst2",)])
        p.op("dve", lambda e: e.memset(cst[:, 1:2], -1.0), writes=[("cst2",)])
        p.op("act", lambda e: e.activation(out=s_aneg[:], in_=s_aneg[:], func=AF.Exp),
             reads=[("s_aneg",)], writes=[("s_aneg",)])
        p.op("dve", lambda e: e.tensor_scalar(out=s_aneg[:], in0=s_aneg[:], scalar1=-1.0,
                                              scalar2=0.0, op0=OP.mult, op1=OP.add),
             reads=[("s_aneg",)], writes=[("s_aneg",)])

        store_keys = []
        for b in range(BATCH):
            for t in range(NT):
                p.op("dve", lambda e, t=t: e.memset(hcar[t][:], 0.0),
                     reads=[("hcar_rd", t)],
                     writes=[("hcar", t, n) for n in range(NST)])
            for ci in range(Lb // SCH):
                c0 = ci * SCH
                for t8 in range(8):
                    p.dma("sp", out=xnb[t8][:], in_=xn_in[b * E + t8 * 128:b * E + (t8 + 1) * 128, c0:c0 + SCH],
                          stream="xnb%d" % t8, writes=[("xnb", t8)])
                for t in range(NT):
                    if ci == 0:
                        p.op("dve", lambda e, t=t: e.memset(u_raw[t][:, 0:DCONV - 1], 0.0),
                             reads=[("u_raw", t)], writes=[("ucar", t)])
                    else:
                        p.op("dve", lambda e, t=t: e.tensor_copy(
                            out=u_raw[t][:, 0:DCONV - 1], in_=u_raw[t][:, SCH:SCH + DCONV - 1]),
                             reads=[("u_raw", t)], writes=[("ucar", t)])
                for mt in range(4):
                    for j in range(NJ):
                        pp = pW[(mt * NJ + j) % 4]
                        pk = ("pW", (mt * NJ + j) % 4)
                        for k in range(8):
                            p.op("pe", lambda e, pp=pp, k=k, mt=mt, j=j, s=(k == 0), t_=(k == 7): e.matmul(
                                out=pp[:], lhsT=winsb[:, k * 512 + mt * 128:k * 512 + (mt + 1) * 128],
                                rhs=xnb[k][:, j * 512:(j + 1) * 512], start=s, stop=t_),
                                 reads=[("winsb",), ("xnb", k)], writes=[pk])
                        if mt < NT:
                            t = mt
                            p.op("act", lambda e, pp=pp, t=t, j=j: e.activation(
                                out=u_raw[t][:, DCONV - 1 + j * 512:DCONV - 1 + (j + 1) * 512],
                                in_=pp[:], func=AF.Copy),
                                 reads=[pk, ("ucar", t)], writes=[("u_raw", t)])
                        else:
                            t = mt - NT
                            p.op("act", lambda e, pp=pp, t=t, j=j: e.activation(
                                out=z[t][:, j * 512:(j + 1) * 512], in_=pp[:], func=AF.Copy),
                                 reads=[pk, ("z_free", t)], writes=[("z", t)])
                for t in range(NT):
                    p.op("dve", lambda e, t=t: e.tensor_scalar(
                        out=conv[t][:], in0=u_raw[t][:, 0:SCH], scalar1=s_cw[:, t * DCONV:t * DCONV + 1],
                        scalar2=0.0, op0=OP.mult, op1=OP.add),
                         reads=[("u_raw", t), ("ucar", t), ("s_cw",), ("conv_rd", t)], writes=[("conv", t)])
                    for kk in range(1, DCONV):
                        p.op("dve", lambda e, t=t, kk=kk: e.scalar_tensor_tensor(
                            out=conv[t][:], in0=u_raw[t][:, kk:kk + SCH],
                            scalar=s_cw[:, t * DCONV + kk:t * DCONV + kk + 1],
                            in1=conv[t][:], op0=OP.mult, op1=OP.add),
                             reads=[("u_raw", t), ("conv", t), ("s_cw",)], writes=[("conv", t)])
                    p.op("act", lambda e, t=t: e.activation(
                        out=us[t][:], in_=conv[t][:], func=AF.Silu, bias=s_cb[:, t:t + 1]),
                         reads=[("conv", t), ("s_cb",), ("us_free", t)], writes=[("us", t), ("conv_rd", t)])
                    p.op("act", lambda e, t=t: e.activation(out=ub[t][:], in_=us[t][:], func=AF.Copy),
                         reads=[("us", t)], writes=[("ub", t)])
                for j in range(NJ):
                    pp = pW[j % 2]
                    pk = ("pW", j % 2)
                    for kt in range(NT):
                        p.op("pe", lambda e, pp=pp, kt=kt, j=j, s=(kt == 0), t_=(kt == NT - 1): e.matmul(
                            out=pp[0:96, :], lhsT=wxsb[:, kt * 96:(kt + 1) * 96],
                            rhs=ub[kt][:, j * 512:(j + 1) * 512], start=s, stop=t_),
                             reads=[("wxsb",), ("ub", kt)], writes=[pk])
                    p.op("act", lambda e, pp=pp, j=j: e.activation(
                        out=dbcs[0:96, j * 512:(j + 1) * 512], in_=pp[0:96, :], func=AF.Copy),
                         reads=[pk, ("dbcs_rd",)], writes=[("dbcs",)])
                p.dma("sp", out=ar_in, in_=dbcs[0:96, :], stream="arin", reads=[("dbcs",)], writes=[("ar_in",), ("dbcs_rd",)])
                p.collective(lambda e: e.collective_compute(
                    "AllReduce", OP.add, replica_groups=[list(range(NCORES))],
                    ins=[ar_in.opt()], outs=[ar_out.opt()]),
                     reads=[("ar_in",)], writes=[("ar_out",)])
                p.dma("pool", out=dtinb[:], in_=ar_out[0:DTR, :], stream="dtinb",
                      reads=[("ar_out",)], writes=[("dtinb",)])
                for mt in range(NT):
                    for j in range(NJ):
                        pp = pW[2 + (j % 2)]
                        pk = ("pW", 2 + (j % 2))
                        p.op("pe", lambda e, pp=pp, mt=mt, j=j: e.matmul(
                            out=pp[:], lhsT=wdtsb[:, mt * 128:(mt + 1) * 128],
                            rhs=dtinb[:, j * 512:(j + 1) * 512], start=True, stop=True),
                             reads=[("wdtsb",), ("dtinb",)], writes=[pk])
                        p.op("act", lambda e, pp=pp, mt=mt, j=j: e.activation(
                            out=dt[mt][:, j * 512:(j + 1) * 512], in_=pp[:], func=AF.Relu,
                            bias=s_dtb[:, mt:mt + 1]),
                             reads=[pk, ("s_dtb",)], writes=[("dt", mt)])
                        p.op("act", lambda e, pp=pp, mt=mt: e.activation(
                            out=sp1[:], in_=pp[:], func=AF.Abs, bias=s_dtb[:, mt:mt + 1]),
                             reads=[pk, ("s_dtb",), ("sp1_rd",)], writes=[("sp1",)])
                        p.op("act", lambda e: e.activation(out=sp2[:], in_=sp1[:], func=AF.Exp,
                                                           scale=cst[:, 1:2]),
                             reads=[("sp1",), ("cst2",), ("sp2_rd",)], writes=[("sp2",), ("sp1_rd",)])
                        p.op("act", lambda e: e.activation(out=sp1[:], in_=sp2[:], func=AF.Ln,
                                                           bias=cst[:, 0:1]),
                             reads=[("sp2",), ("cst2",), ("sp1_rd",)], writes=[("sp1",), ("sp2_rd",)])
                        p.op("dve", lambda e, mt=mt, j=j: e.tensor_tensor(
                            out=dt[mt][:, j * 512:(j + 1) * 512],
                            in0=dt[mt][:, j * 512:(j + 1) * 512], in1=sp1[:], op=OP.add),
                             reads=[("dt", mt), ("sp1",)], writes=[("dt", mt), ("sp1_rd",)])
                for t in range(NT):
                    p.op("dve", lambda e, t=t: e.tensor_tensor(out=dtu[t][:], in0=dt[t][:], in1=us[t][:], op=OP.mult),
                         reads=[("dt", t), ("us", t)], writes=[("dtu", t)])
                for n in range(NST):
                    p.dma("sp", out=bcB[:], in_=ar_out[DTR + n, :].partition_broadcast(128),
                          stream="bcB", reads=[("ar_out",)], writes=[("bcB",)])
                    p.dma("sp", out=bcC[:], in_=ar_out[DTR + NST + n, :].partition_broadcast(128),
                          stream="bcC", reads=[("ar_out",)], writes=[("bcC",)])
                    for t in range(NT):
                        p.op("act", lambda e, t=t, n=n: e.activation(
                            out=dA[:], in_=dt[t][:], func=AF.Exp, scale=s_aneg[:, t * NST + n:t * NST + n + 1]),
                             reads=[("dt", t), ("s_aneg",), ("dA_rd",)], writes=[("dA",)])
                        p.op("dve", lambda e, t=t: e.tensor_tensor(out=dBu[:], in0=dtu[t][:], in1=bcB[:], op=OP.mult),
                             reads=[("dtu", t), ("bcB",), ("dBu_rd",)], writes=[("dBu",)])
                        p.op("dve", lambda e, t=t, n=n: e.tensor_tensor_scan(
                            out=hb[:], data0=dA[:], data1=dBu[:], initial=hcar[t][:, n:n + 1],
                            op0=OP.mult, op1=OP.add),
                             reads=[("dA",), ("dBu",), ("hcar", t, n), ("hb_rd",)],
                             writes=[("hb",), ("dA_rd",), ("dBu_rd",)])
                        p.op("act", lambda e, t=t, n=n: e.activation(
                            out=hcar[t][:, n:n + 1], in_=hb[:, SCH - 1:SCH], func=AF.Copy),
                             reads=[("hb",)], writes=[("hcar", t, n), ("hcar_rd", t)])
                        if n == 0:
                            p.op("dve", lambda e, t=t: e.tensor_tensor(out=y[t][:], in0=hb[:], in1=bcC[:], op=OP.mult),
                                 reads=[("hb",), ("bcC",), ("y_free", t)], writes=[("y", t), ("hb_rd",)])
                        else:
                            p.op("dve", lambda e, t=t: e.tensor_tensor(out=hcb[:], in0=hb[:], in1=bcC[:], op=OP.mult),
                                 reads=[("hb",), ("bcC",), ("hcb_rd",)], writes=[("hcb",), ("hb_rd",)])
                            p.op("dve", lambda e, t=t: e.tensor_tensor(out=y[t][:], in0=y[t][:], in1=hcb[:], op=OP.add),
                                 reads=[("y", t), ("hcb",)], writes=[("y", t), ("hcb_rd",)])

                for t in range(NT):
                    p.op("dve", lambda e, t=t: e.scalar_tensor_tensor(
                        out=y[t][:], in0=us[t][:], scalar=s_d[:, t:t + 1], in1=y[t][:],
                        op0=OP.mult, op1=OP.add),
                         reads=[("us", t), ("y", t), ("s_d",)], writes=[("y", t), ("us_free", t)])
                    p.op("act", lambda e, t=t: e.activation(out=z[t][:], in_=z[t][:], func=AF.Silu),
                         reads=[("z", t)], writes=[("z", t)])
                    p.op("dve", lambda e, t=t: e.tensor_tensor(out=ygb[:], in0=y[t][:], in1=z[t][:], op=OP.mult),
                         reads=[("y", t), ("z", t), ("ygb_rd",)], writes=[("ygb",), ("y_free", t), ("z_free", t)])
                    skey = ("yg_st", b, ci, t)
                    p.dma("sp", out=yg_out[b * DSL + t * 128:b * DSL + (t + 1) * 128, c0:c0 + SCH],
                          in_=ygb[:], stream="ygst", reads=[("ygb",)], writes=[skey, ("ygb_rd",)])
                    store_keys.append(skey)

        p.op("pool", lambda e: e.nop(nofuse=True), reads=store_keys[-2 * NT:])
        with nc.Block() as block:
            p.emit(block)
    return nc



def build_launch3(T=TPC):
    nc = bass.Bass("TRN2", target_bir_lowering=False, debug=False)
    yg_in = nc.dram_tensor("yg_in", [DIN, T], BF16, kind="ExternalInput").ap()
    x2p_in = nc.dram_tensor("x2p_in", [E, T], F32, kind="ExternalInput").ap()
    woutT = nc.dram_tensor("woutT", [DIN, E], F32, kind="ExternalInput").ap()
    k0 = nc.dram_tensor("k0", [2 * E, 4096], U8, kind="ExternalInput").ap()
    k1 = nc.dram_tensor("k1", [4096, 4096], U8, kind="ExternalInput").ap()
    k2 = nc.dram_tensor("k2", [4096, E], U8, kind="ExternalInput").ap()
    s0r = nc.dram_tensor("s0r", [128, 1], F32, kind="ExternalInput").ap()
    s1r = nc.dram_tensor("s1r", [128, 1], F32, kind="ExternalInput").ap()
    s2r = nc.dram_tensor("s2r", [128, 1], F32, kind="ExternalInput").ap()
    out_d = nc.dram_tensor("outT", [E, T], F32, kind="ExternalOutput").ap()

    with contextlib.ExitStack() as stack:
        env = {}
        sb = lambda name, shape, dt: stack.enter_context(nc.sbuf_tensor(name, shape, dt))
        make_ffn_env(nc, stack, env, nwb=1)
        woutsb = sb("woutsb", [128, 16 * 1024], BF16)
        ygt = [sb("ygt%d" % t, [128, CH], BF16) for t in range(16)]
        x3 = [sb("x3_%d" % t, [128, CH], F32) for t in range(8)]
        nxb = [sb("nxb%d" % t, [128, CH], BF16) for t in range(8)]
        s_s0 = sb("s_s0", [128, 1], F32)
        s_s1 = sb("s_s1", [128, 1], F32)
        s_s2 = sb("s_s2", [128, 1], F32)

        p = Prog(nc, stack)
        p.dma("sp", out=s_s0[:], in_=s0r, stream="ss0", writes=[("s0x4", "in")])
        p.dma("sp", out=s_s1[:], in_=s1r, stream="ss1", writes=[("s1x2", "in")])
        p.dma("sp", out=s_s2[:], in_=s2r, stream="ss2", writes=[("halfs2", "in")])
        emit_consts_scales(p, env, s_s0, s_s1, s_s2)
        for k in range(16):
            p.dma("pool", out=woutsb[:, k * 1024:(k + 1) * 1024], in_=woutT[k * 128:(k + 1) * 128, :],
                  stream="woutl", writes=[("woutsb",)])
        wf0, wf1, wf2 = setup_ffn_weights(nc, p, env, k0, k1, k2)

        nch = T // CH
        store_keys = []
        for c in range(nch):
            c0 = c * CH
            for t in range(16):
                p.dma("sp", out=ygt[t][:], in_=yg_in[t * 128:(t + 1) * 128, c0:c0 + CH],
                      stream="ygt%d" % t, writes=[("ygt", t)])
            for t in range(8):
                p.dma("sp", out=x3[t][:], in_=x2p_in[t * 128:(t + 1) * 128, c0:c0 + CH],
                      stream="x3l%d" % t, writes=[("x3", t)])
            for mt in range(8):
                pp = env["pL"][mt % 4]
                pk = ("pL", mt % 4)
                for k in range(16):
                    p.op("pe", lambda e, pp=pp, k=k, mt=mt, s=(k == 0), t_=(k == 15): e.matmul(
                        out=pp[:], lhsT=woutsb[:, k * 1024 + mt * 128:k * 1024 + (mt + 1) * 128],
                        rhs=ygt[k][:], start=s, stop=t_),
                         reads=[("woutsb",), ("ygt", k)], writes=[pk])
                p.op("dve", lambda e, pp=pp, mt=mt: e.scalar_tensor_tensor(
                    out=x3[mt][:], in0=pp[:], scalar=env["cst"][:, 0:1], in1=x3[mt][:],
                    op0=OP.mult, op1=OP.add),
                     reads=[pk, ("x3", mt), ("cst",)], writes=[("x3", mt)])

            emit_ln(p, env, [(x3[t][:], ("x3", t)) for t in range(8)],
                    [(nxb[t][:], ("nxb", t)) for t in range(8)], 8, CH, gb=None)
            emit_ffn(p, env, [(nxb[t][:], ("nxb", t)) for t in range(8)],
                     [(x3[t][:], ("x3", t)) for t in range(8)],
                     wf0, wf1, wf2, 8, 32, 32, 8)
            for t in range(8):
                skey = ("out_st", c, t)
                p.dma("sp", out=out_d[t * 128:(t + 1) * 128, c0:c0 + CH], in_=x3[t][:],
                      stream="outst", reads=[("x3", t)], writes=[skey])
                store_keys.append(skey)
        p.op("pool", lambda e: e.nop(nofuse=True), reads=store_keys[-8:])
        with nc.Block() as block:
            p.emit(block)
    return nc




def make_persist_env(nc, stack, env):
    sb = lambda name, shape, dt: stack.enter_context(nc.sbuf_tensor(name, shape, dt))
    env["cst"] = sb("cst", [128, 8], F32)
    env["onesb"] = sb("onesb", [128, 1], BF16)
    env["ones1f"] = sb("ones1f", [1, 128], F32)
    env["s0x4"] = sb("s0x4", [128, 1], F32)
    env["s1x2"] = sb("s1x2", [128, 1], F32)
    env["halfs2"] = sb("halfs2", [128, 1], F32)


def make_ffn_phase_env(nc, stack, env, nwb=2, tag=""):
    sb = lambda name, shape, dt: stack.enter_context(nc.sbuf_tensor(tag + name, shape, dt))
    ps = lambda name, shape: stack.enter_context(nc.psum_tensor(tag + name, shape, F32))
    env["wbuf0"] = [sb("wbuf0_%d" % i, [128, 8 * 128], BF16) for i in range(3)]
    env["wbuf1"] = [sb("wbuf1_%d" % i, [128, 32 * 128], BF16) for i in range(3)]
    env["wbuf2"] = [sb("wbuf2_%d" % i, [128, 32 * 128], BF16) for i in range(nwb)]
    env["h1b"] = sb("h1b", [128, 32 * CH], BF16)
    env["h2b"] = sb("h2b", [128, 32 * CH], BF16)
    env["pL"] = [ps("pL%d" % i, [128, CH]) for i in range(4)]
    env["lnxb"] = sb("lnxb", [128, 4 * CH], BF16)
    env["lnsq"] = sb("lnsq", [128, 4 * CH], BF16)
    env["lnscr"] = [sb("lnscr%d" % i, [128, CH], F32) for i in range(2)]
    env["rows"] = sb("rows", [1, 4 * CH], F32)
    env["pstatA"] = ps("pstatA", [1, CH])
    env["pstatB"] = ps("pstatB", [1, CH])
    env["pbc1"] = ps("pbc1", [128, CH])
    env["pbc2"] = ps("pbc2", [128, CH])


def phase_barrier(p, keys):
    for e in Prog.ENGINES:
        p.op(e, lambda en: en.nop(nofuse=True), reads=list(keys))


def build_fused(T=TPC, SCH=512):
    SCH = min(SCH, T)
    Lb = 4 * T
    NT = DSL // 128
    NJ = SCH // 512
    NCH1 = T // CH
    NQ = T // SCH
    nc = bass.Bass("TRN2", target_bir_lowering=False, debug=False)
    xT = nc.dram_tensor("xT", [E, T], F32, kind="ExternalInput").ap()
    k0 = nc.dram_tensor("k0", [2 * E, 4096], U8, kind="ExternalInput").ap()
    k1 = nc.dram_tensor("k1", [4096, 4096], U8, kind="ExternalInput").ap()
    k2 = nc.dram_tensor("k2", [4096, E], U8, kind="ExternalInput").ap()
    s0r = nc.dram_tensor("s0r", [128, 1], F32, kind="ExternalInput").ap()
    s1r = nc.dram_tensor("s1r", [128, 1], F32, kind="ExternalInput").ap()
    s2r = nc.dram_tensor("s2r", [128, 1], F32, kind="ExternalInput").ap()
    lng = nc.dram_tensor("lng", [128, 8], F32, kind="ExternalInput").ap()
    lnb = nc.dram_tensor("lnb", [128, 8], F32, kind="ExternalInput").ap()
    winT = nc.dram_tensor("winT", [E, 2 * DSL], F32, kind="ExternalInput").ap()
    wxT = nc.dram_tensor("wxT", [DSL, 96], F32, kind="ExternalInput").ap()
    wdtT = nc.dram_tensor("wdtT", [DTR, DSL], F32, kind="ExternalInput").ap()
    cw = nc.dram_tensor("cw", [128, 2 * DCONV], F32, kind="ExternalInput").ap()
    cb = nc.dram_tensor("cb", [128, 2], F32, kind="ExternalInput").ap()
    dtb = nc.dram_tensor("dtb", [128, 2], F32, kind="ExternalInput").ap()
    alog = nc.dram_tensor("alog", [128, 2 * NST], F32, kind="ExternalInput").ap()
    dsl_in = nc.dram_tensor("dsl", [128, 2], F32, kind="ExternalInput").ap()
    woutTs = nc.dram_tensor("woutTs", [DSL, E], F32, kind="ExternalInput").ap()
    out_d = nc.dram_tensor("outT", [E, T], F32, kind="ExternalOutput").ap()

    x2p_d = nc.dram_tensor("x2p_d", [E, T], F32).ap()
    ag_in = [nc.dram_tensor("ag_in%d" % k, [E, CH], BF16).ap() for k in range(NCH1)]
    ag_out = [nc.dram_tensor("ag_out%d" % k, [NCORES * E, CH], BF16).ap() for k in range(NCH1)]
    rs_in = [nc.dram_tensor("rs_in%d" % q, [NCORES * E, SCH], F32).ap() for q in range(NQ)]
    rs_out = [nc.dram_tensor("rs_out%d" % q, [E, SCH], F32).ap() for q in range(NQ)]
    ar_in = [[nc.dram_tensor("ar_in%d_%d" % (b, i), [96, SCH], F32).ap() for i in range(3)]
             for b in range(BATCH)]
    ar_out = [[nc.dram_tensor("ar_out%d_%d" % (b, i), [96, SCH], F32).ap() for i in range(3)]
              for b in range(BATCH)]

    with contextlib.ExitStack() as stack:
        env = {}
        make_persist_env(nc, stack, env)
        sbp = lambda name, shape, dt: stack.enter_context(nc.sbuf_tensor(name, shape, dt))
        s_s0 = sbp("s_s0", [128, 1], F32)
        s_s1 = sbp("s_s1", [128, 1], F32)
        s_s2 = sbp("s_s2", [128, 1], F32)
        p = Prog(nc, stack)
        p.dma("sp", out=s_s0[:], in_=s0r, stream="ss0", writes=[("s0x4", "in")])
        p.dma("sp", out=s_s1[:], in_=s1r, stream="ss1", writes=[("s1x2", "in")])
        p.dma("sp", out=s_s2[:], in_=s2r, stream="ss2", writes=[("halfs2", "in")])
        emit_consts_scales(p, env, s_s0, s_s1, s_s2)

        ph1 = contextlib.ExitStack()
        env1 = dict(env)
        make_ffn_phase_env(nc, ph1, env1, nwb=2, tag="p1")
        sb1 = lambda name, shape, dt: ph1.enter_context(nc.sbuf_tensor(name, shape, dt))
        env1["cva"] = sb1("cva", [128, 512], BF16)
        env1["cvb"] = sb1("cvb", [128, 512], BF16)
        env1["cvw"] = sb1("cvw", [128, 512], BF16)
        xc = [[sb1("xc%d_%d" % (i, t), [128, CH], F32) for t in range(8)] for i in range(2)]
        xin = [sb1("xin%d" % t, [128, CH], F32) for t in range(8)]
        nxb = [[sb1("nxb%d_%d" % (i, t), [128, CH], BF16) for t in range(8)] for i in range(2)]

        xnout = [sb1("xnout%d" % t, [128, CH], BF16) for t in range(8)]
        s_g = sb1("s_g", [128, 8], F32)
        s_b = sb1("s_b", [128, 8], F32)
        p.dma("sp", out=s_g[:], in_=lng, stream="sgv", writes=[("s_g",)])
        p.dma("sp", out=s_b[:], in_=lnb, stream="sbv", writes=[("s_gb",)])

        def p1_load(c):
            i = c % 2
            for t in range(8):
                p.dma("pool", out=xc[i][t][:], in_=xT[t * 128:(t + 1) * 128, c * CH:(c + 1) * CH],
                      stream="xc%d" % t, writes=[("xc", i, t)])

        def p1_ln1(c):
            i = c % 2
            emit_ln(p, env1, [(xc[i][t][:], ("xc", i, t)) for t in range(8)],
                    [(nxb[i][t][:], ("nxb", i, t)) for t in range(8)], 8, CH, gb=None)

        def p1_tail1(c):
            i = c % 2
            emit_ln(p, env1, [(xc[i][t][:], ("xc", i, t)) for t in range(8)],
                    [(xin[t][:], ("xin", t)) for t in range(8)], 8, CH, gb=None)
            for t in range(8):
                p.op("dve", lambda e, i=i, t=t: e.tensor_tensor(
                    out=xc[i][t][:], in0=xc[i][t][:], in1=xin[t][:], op=OP.add),
                     reads=[("xc", i, t), ("xin", t)], writes=[("xc", i, t)])
                p.dma("act", out=x2p_d[t * 128:(t + 1) * 128, c * CH:(c + 1) * CH], in_=xc[i][t][:],
                      stream="x2pst", reads=[("xc", i, t)], writes=[("x2p_st", c, t)])

        def p1_tail2(c):
            i = c % 2
            emit_ln(p, env1, [(xin[t][:], ("xin", t)) for t in range(8)],
                    [(xnout[t][:], ("xnout", t)) for t in range(8)], 8, CH,
                    gb=[(s_g[:, t:t + 1], s_b[:, t:t + 1]) for t in range(8)],
                    gbkeys=[("s_g",), ("s_gb",)])
            for t in range(8):
                p.dma("act", out=ag_in[c][t * 128:(t + 1) * 128, :], in_=xnout[t][:],
                      stream="xnst", reads=[("xnout", t)], writes=[("xn_st", c, t)])
            p.collective(lambda e, c=c: e.collective_compute(
                "AllGather", OP.bypass, replica_groups=[list(range(NCORES))],
                ins=[ag_in[c].opt()], outs=[ag_out[c].opt()]),
                 reads=[("xn_st", c, t) for t in range(8)], writes=[("ag", c)])

        p1_load(0)
        p1_ln1(0)
        wf0, wf1, wf2 = setup_ffn_weights(nc, p, env1, k0, k1, k2)
        for c in range(NCH1):
            i = c % 2
            if c + 1 < NCH1:
                p1_load(c + 1)
            emit_ffn_l1(p, env1, [(nxb[i][t][:], ("nxb", i, t)) for t in range(8)], wf0, 8, 32)
            if c + 1 < NCH1:
                p1_ln1(c + 1)
            if c >= 1:
                p1_tail2(c - 1)
            emit_ffn_l2(p, env1, wf1, 32, 32)
            emit_ffn_l3(p, env1, [(xc[i][t][:], ("xc", i, t)) for t in range(8)], wf2, 32, 8)
            p1_tail1(c)
        p1_tail2(NCH1 - 1)
        ph1.close()
        phase_barrier(p, [("x2p_st", NCH1 - 1, 7), ("wfdone", 2), ("cvw",), ("cva",), ("cvb",),
                          ("ag", NCH1 - 1)])

        ph2 = contextlib.ExitStack()
        sb2 = lambda name, shape, dt: ph2.enter_context(nc.sbuf_tensor(name, shape, dt))
        ps2 = lambda name, shape: ph2.enter_context(nc.psum_tensor(name, shape, F32))
        xnb = [sb2("xnb%d" % t, [128, SCH], BF16) for t in range(8)]
        winsb = sb2("winsb", [128, 8 * 4 * 128], BF16)
        wxsb = sb2("wxsb", [128, 2 * 96], BF16)
        wdtsb = sb2("wdtsb", [DTR, 2 * 128], BF16)
        woutsb2 = sb2("woutsb2", [128, 2 * 1024], BF16)
        s_cw = sb2("s_cw", [128, 2 * DCONV], F32)
        s_cb = sb2("s_cb", [128, 2], F32)
        s_dtb = sb2("s_dtb", [128, 2], F32)
        s_aneg = sb2("s_aneg", [128, 2 * NST], F32)
        s_d = sb2("s_d", [128, 2], F32)
        cst2 = sb2("cstl2", [128, 2], F32)
        u_raw = [[[sb2("ur%d_%d_%d" % (b, i, t), [128, DCONV - 1 + SCH], F32) for t in range(NT)]
                  for i in range(3)] for b in range(BATCH)]
        zt = [[[sb2("z%d_%d_%d" % (b, i, t), [128, SCH], F32) for t in range(NT)]
               for i in range(3)] for b in range(BATCH)]
        ust = [[[sb2("us%d_%d_%d" % (b, i, t), [128, SCH], F32) for t in range(NT)]
                for i in range(3)] for b in range(BATCH)]
        conv = [sb2("conv%d" % t, [128, SCH], F32) for t in range(NT)]
        ub = [sb2("ub%d" % t, [128, SCH], BF16) for t in range(NT)]
        dbcs = sb2("dbcs", [96, SCH], F32)
        dtinb = sb2("dtinb", [DTR, SCH], BF16)
        dt = [[sb2("dt%d_%d" % (b, t), [128, SCH], F32) for t in range(NT)] for b in range(BATCH)]
        dtu = [sb2("dtu%d" % t, [128, SCH], F32) for t in range(NT)]
        sp1 = sb2("sp1", [128, 512], F32)
        sp2 = sb2("sp2", [128, 512], F32)
        bcB = [sb2("bcB%d" % i, [128, SCH], F32) for i in range(3)]
        bcC = [sb2("bcC%d" % i, [128, SCH], F32) for i in range(3)]
        dA = [[sb2("dA%d_%d" % (t, i), [128, SCH], F32) for i in range(2)] for t in range(NT)]
        dBu = [sb2("dBu%d" % t, [128, SCH], F32) for t in range(NT)]
        hb = [sb2("hb%d" % i, [128, SCH], F32) for i in range(4)]
        hcb = [sb2("hcb%d" % t, [128, SCH], F32) for t in range(NT)]
        y = [sb2("y%d" % t, [128, SCH], F32) for t in range(NT)]
        ygb = [[sb2("ygb%d_%d" % (b, t), [128, SCH], BF16) for t in range(NT)] for b in range(BATCH)]
        motile = [sb2("motile%d" % i, [128, 512], F32) for i in range(2)]
        hcar = [[sb2("hcar%d_%d" % (b, t), [128, NST], F32) for t in range(NT)] for b in range(BATCH)]
        pW = [ps2("pW%d" % i, [128, 512]) for i in range(4)]

        for k in range(8):
            p.dma("pool", out=winsb[:, k * 512:(k + 1) * 512], in_=winT[k * 128:(k + 1) * 128, :],
                  stream="winl", writes=[("winsb",)])
        for kt in range(NT):
            p.dma("pool", out=wxsb[:, kt * 96:(kt + 1) * 96], in_=wxT[kt * 128:(kt + 1) * 128, :],
                  stream="wxl", writes=[("wxsb",)])
        p.dma("pool", out=wdtsb[:], in_=wdtT, stream="wdtl", writes=[("wdtsb",)])
        for kt in range(NT):
            p.dma("pool", out=woutsb2[:, kt * 1024:(kt + 1) * 1024], in_=woutTs[kt * 128:(kt + 1) * 128, :],
                  stream="wo2l", writes=[("woutsb2",)])
        p.dma("sp", out=s_cw[:], in_=cw, stream="scw", writes=[("s_cw",)])
        p.dma("sp", out=s_cb[:], in_=cb, stream="scb", writes=[("s_cb",)])
        p.dma("sp", out=s_dtb[:], in_=dtb, stream="sdtb", writes=[("s_dtb",)])
        p.dma("sp", out=s_aneg[:], in_=alog, stream="saneg", writes=[("s_aneg",)])
        p.dma("sp", out=s_d[:], in_=dsl_in, stream="sdv", writes=[("s_d",)])
        p.op("dve", lambda e: e.memset(cst2[:, 0:1], 1.0), writes=[("cst2",)])
        p.op("dve", lambda e: e.memset(cst2[:, 1:2], -1.0), writes=[("cst2",)])
        p.op("act", lambda e: e.activation(out=s_aneg[:], in_=s_aneg[:], func=AF.Exp),
             reads=[("s_aneg",)], writes=[("s_aneg",)])
        p.op("dve", lambda e: e.tensor_scalar(out=s_aneg[:], in0=s_aneg[:], scalar1=-1.0,
                                              scalar2=0.0, op0=OP.mult, op1=OP.add),
             reads=[("s_aneg",)], writes=[("s_aneg",)])
        for b in range(BATCH):
            for t in range(NT):
                p.op("dve", lambda e, b=b, t=t: e.memset(hcar[b][t][:], 0.0),
                     writes=[("hcar", b, t, n) for n in range(NST)])

        def rro(b, ci):
            r = b * 4 + ci % 4
            o = (ci // 4) * SCH
            return r, o

        def pre_phase(b, ci):
            par = ci % 3
            r, o = rro(b, ci)
            kc = o // CH
            ur = u_raw[b][par]
            z = zt[b][par]
            us = ust[b][par]
            for t8 in range(8):
                p.dma("pool", out=xnb[t8][:],
                      in_=ag_out[kc][r * E + t8 * 128:r * E + (t8 + 1) * 128, :],
                      stream="xnb%d" % t8, reads=[("ag", kc)], writes=[("xnb", t8)])
            for t in range(NT):
                if ci == 0:
                    p.op("dve", lambda e, b=b, par=par, t=t: e.memset(ur[t][:, 0:DCONV - 1], 0.0),
                         reads=[("ur", b, par, t)], writes=[("ucar", b, par, t)])
                else:
                    pprev = (ci - 1) % 3
                    p.op("dve", lambda e, b=b, par=par, t=t, pprev=pprev: e.tensor_copy(
                        out=u_raw[b][par][t][:, 0:DCONV - 1],
                        in_=u_raw[b][pprev][t][:, SCH:SCH + DCONV - 1]),
                         reads=[("ur", b, pprev, t), ("ur", b, par, t)],
                         writes=[("ucar", b, par, t)])
            for mt in range(4):
                pp = pW[mt % 4]
                pk = ("pW", mt % 4)
                for k in range(8):
                    p.op("pe", lambda e, pp=pp, k=k, mt=mt, s=(k == 0), t_=(k == 7): e.matmul(
                        out=pp[:], lhsT=winsb[:, k * 512 + mt * 128:k * 512 + (mt + 1) * 128],
                        rhs=xnb[k][:], start=s, stop=t_),
                         reads=[("winsb",), ("xnb", k)], writes=[pk])
                if mt < NT:
                    t = mt
                    p.op("act", lambda e, pp=pp, t=t: e.activation(
                        out=ur[t][:, DCONV - 1:DCONV - 1 + SCH], in_=pp[:], func=AF.Copy),
                         reads=[pk, ("ucar", b, par, t)], writes=[("ur", b, par, t)])
                else:
                    t = mt - NT
                    p.op("act", lambda e, pp=pp, t=t: e.activation(
                        out=z[t][:], in_=pp[:], func=AF.Copy),
                         reads=[pk, ("z", b, par, t)], writes=[("z", b, par, t)])
            for t in range(NT):
                p.op("dve", lambda e, t=t, ur=ur: e.tensor_scalar(
                    out=conv[t][:], in0=ur[t][:, 0:SCH], scalar1=s_cw[:, t * DCONV:t * DCONV + 1],
                    scalar2=0.0, op0=OP.mult, op1=OP.add),
                     reads=[("ur", b, par, t), ("ucar", b, par, t), ("s_cw",), ("conv", t)],
                     writes=[("conv", t)])
                for kk in range(1, DCONV):
                    p.op("dve", lambda e, t=t, kk=kk, ur=ur: e.scalar_tensor_tensor(
                        out=conv[t][:], in0=ur[t][:, kk:kk + SCH],
                        scalar=s_cw[:, t * DCONV + kk:t * DCONV + kk + 1],
                        in1=conv[t][:], op0=OP.mult, op1=OP.add),
                         reads=[("ur", b, par, t), ("conv", t), ("s_cw",)], writes=[("conv", t)])
                p.op("act", lambda e, t=t, us=us: e.activation(
                    out=us[t][:], in_=conv[t][:], func=AF.Silu, bias=s_cb[:, t:t + 1]),
                     reads=[("conv", t), ("s_cb",), ("us", b, par, t)], writes=[("us", b, par, t)])
                p.op("act", lambda e, t=t, us=us: e.activation(out=ub[t][:], in_=us[t][:], func=AF.Copy),
                     reads=[("us", b, par, t), ("ub", t)], writes=[("ub", t)])
            pp = pW[0]
            pk = ("pW", 0)
            for kt in range(NT):
                p.op("pe", lambda e, pp=pp, kt=kt, s=(kt == 0), t_=(kt == NT - 1): e.matmul(
                    out=pp[0:96, :], lhsT=wxsb[:, kt * 96:(kt + 1) * 96],
                    rhs=ub[kt][:], start=s, stop=t_),
                     reads=[("wxsb",), ("ub", kt)], writes=[pk])
            p.op("act", lambda e, pp=pp: e.activation(
                out=dbcs[0:96, :], in_=pp[0:96, :], func=AF.Copy),
                 reads=[pk, ("dbcs",)], writes=[("dbcs",)])
            p.dma("sp", out=ar_in[b][par], in_=dbcs[0:96, :], stream="arin%d%d" % (b, par),
                  reads=[("dbcs",)], writes=[("ar_in", b, par)])
            p.collective(lambda e, b=b, par=par: e.collective_compute(
                "AllReduce", OP.add, replica_groups=[list(range(NCORES))],
                ins=[ar_in[b][par].opt()], outs=[ar_out[b][par].opt()]),
                 reads=[("ar_in", b, par), ("ar_out", b, par)], writes=[("ar_out", b, par)])

        def post_dt(b, ci):
            par = ci % 3
            p.dma("pool", out=dtinb[:], in_=ar_out[b][par][0:DTR, :], stream="dtinb",
                  reads=[("ar_out", b, par), ("dtinb",)], writes=[("dtinb",)])
            for mt in range(NT):
                pp = pW[2 + mt % 2]
                pk = ("pW", 2 + mt % 2)
                p.op("pe", lambda e, pp=pp, mt=mt: e.matmul(
                    out=pp[:], lhsT=wdtsb[:, mt * 128:(mt + 1) * 128],
                    rhs=dtinb[:], start=True, stop=True),
                     reads=[("wdtsb",), ("dtinb",)], writes=[pk])
                p.op("act", lambda e, pp=pp, b=b, mt=mt: e.activation(
                    out=dt[b][mt][:], in_=pp[:], func=AF.Relu, bias=s_dtb[:, mt:mt + 1]),
                     reads=[pk, ("s_dtb",), ("dt", b, mt)], writes=[("dt", b, mt)])
                p.op("act", lambda e, pp=pp, mt=mt: e.activation(
                    out=sp1[:], in_=pp[:], func=AF.Abs, bias=s_dtb[:, mt:mt + 1]),
                     reads=[pk, ("s_dtb",), ("sp1",)], writes=[("sp1",)])
                p.op("act", lambda e: e.activation(out=sp2[:], in_=sp1[:], func=AF.Exp,
                                                   scale=cst2[:, 1:2]),
                     reads=[("sp1",), ("cst2",), ("sp2",)], writes=[("sp2",)])
                p.op("act", lambda e: e.activation(out=sp1[:], in_=sp2[:], func=AF.Ln,
                                                   bias=cst2[:, 0:1]),
                     reads=[("sp2",), ("cst2",), ("sp1",)], writes=[("sp1",)])
                p.op("dve", lambda e, b=b, mt=mt: e.tensor_tensor(
                    out=dt[b][mt][:], in0=dt[b][mt][:], in1=sp1[:], op=OP.add),
                     reads=[("dt", b, mt), ("sp1",)], writes=[("dt", b, mt)])

        def post_scan(b, ci, mid=None):
            par = ci % 3
            r, o = rro(b, ci)
            q = o // SCH
            us = ust[b][par]
            z = zt[b][par]
            for t in range(NT):
                p.op("dve", lambda e, b=b, t=t, us=us: e.tensor_tensor(
                    out=dtu[t][:], in0=dt[b][t][:], in1=us[t][:], op=OP.mult),
                     reads=[("dt", b, t), ("us", b, par, t), ("dtu", t)], writes=[("dtu", t)])
                p.op("act", lambda e, t=t, z=z: e.activation(out=z[t][:], in_=z[t][:], func=AF.Silu),
                     reads=[("z", b, par, t)], writes=[("z", b, par, t)])
            for np_ in range(0, NST, 2):
                if np_ == 8 and mid is not None:
                    mid()
                for n in (np_, np_ + 1):
                    i2 = n % 3
                    p.dma("sp", out=bcB[i2][:], in_=ar_out[b][par][DTR + n, :].partition_broadcast(128),
                          stream="bcB%d" % i2, reads=[("ar_out", b, par), ("bcB", i2)], writes=[("bcB", i2)])
                    p.dma("pool", out=bcC[i2][:], in_=ar_out[b][par][DTR + NST + n, :].partition_broadcast(128),
                          stream="bcC%d" % i2, reads=[("ar_out", b, par), ("bcC", i2)], writes=[("bcC", i2)])
                    for t in range(NT):
                        iA = n % 2
                        p.op("act", lambda e, b=b, t=t, n=n, iA=iA: e.activation(
                            out=dA[t][iA][:], in_=dt[b][t][:], func=AF.Exp,
                            scale=s_aneg[:, t * NST + n:t * NST + n + 1]),
                             reads=[("dt", b, t), ("s_aneg",), ("dA", t, iA)], writes=[("dA", t, iA)])
                for n in (np_, np_ + 1):
                    i2 = n % 3
                    iA = n % 2
                    for t in range(NT):
                        hi = (n * NT + t) % 4
                        p.op("dve", lambda e, t=t, i2=i2: e.tensor_tensor(
                            out=dBu[t][:], in0=dtu[t][:], in1=bcB[i2][:], op=OP.mult),
                             reads=[("dtu", t), ("bcB", i2), ("dBu", t)], writes=[("dBu", t)])
                        p.op("dve", lambda e, b=b, t=t, n=n, hi=hi, iA=iA: e.tensor_tensor_scan(
                            out=hb[hi][:], data0=dA[t][iA][:], data1=dBu[t][:],
                            initial=hcar[b][t][:, n:n + 1], op0=OP.mult, op1=OP.add),
                             reads=[("dA", t, iA), ("dBu", t), ("hcar", b, t, n), ("hb", hi)],
                             writes=[("hb", hi)])
                        p.op("dve", lambda e, b=b, t=t, n=n, hi=hi: e.tensor_copy(
                            out=hcar[b][t][:, n:n + 1], in_=hb[hi][:, SCH - 1:SCH]),
                             reads=[("hb", hi)], writes=[("hcar", b, t, n)])
                        if n == 0:
                            p.op("dve", lambda e, t=t, i2=i2, hi=hi: e.tensor_tensor(
                                out=y[t][:], in0=hb[hi][:], in1=bcC[i2][:], op=OP.mult),
                                 reads=[("hb", hi), ("bcC", i2), ("y", t)], writes=[("y", t)])
                        else:
                            p.op("dve", lambda e, t=t, i2=i2, hi=hi: e.tensor_tensor(
                                out=hcb[t][:], in0=hb[hi][:], in1=bcC[i2][:], op=OP.mult),
                                 reads=[("hb", hi), ("bcC", i2), ("hcb", t)], writes=[("hcb", t)])
                            p.op("dve", lambda e, t=t: e.tensor_tensor(
                                out=y[t][:], in0=y[t][:], in1=hcb[t][:], op=OP.add),
                                 reads=[("y", t), ("hcb", t)], writes=[("y", t)])
            for t in range(NT):
                p.op("dve", lambda e, b=b, t=t, us=us: e.scalar_tensor_tensor(
                    out=y[t][:], in0=us[t][:], scalar=s_d[:, t:t + 1], in1=y[t][:],
                    op0=OP.mult, op1=OP.add),
                     reads=[("us", b, par, t), ("y", t), ("s_d",)], writes=[("y", t)])
                p.op("dve", lambda e, b=b, t=t, z=z: e.tensor_tensor(
                    out=ygb[b][t][:], in0=y[t][:], in1=z[t][:], op=OP.mult),
                     reads=[("y", t), ("z", b, par, t), ("ygb", b, t)], writes=[("ygb", b, t)])

        def post_wout(b, ci):
            r, o = rro(b, ci)
            q = o // SCH
            for mt in range(8):
                pp = pW[mt % 2]
                pk = ("pW", mt % 2)
                for kt in range(NT):
                    p.op("pe", lambda e, pp=pp, b=b, kt=kt, mt=mt, s=(kt == 0), t_=(kt == NT - 1): e.matmul(
                        out=pp[:], lhsT=woutsb2[:, kt * 1024 + mt * 128:kt * 1024 + (mt + 1) * 128],
                        rhs=ygb[b][kt][:], start=s, stop=t_),
                         reads=[("woutsb2",), ("ygb", b, kt)], writes=[pk])
                mo = motile[mt % 2]
                mok = ("motile", mt % 2)
                p.op("act", lambda e, pp=pp, mo=mo: e.activation(out=mo[:], in_=pp[:], func=AF.Copy),
                     reads=[pk, mok], writes=[mok])
                p.dma("act", out=rs_in[q][r * E + mt * 128:r * E + (mt + 1) * 128, :],
                      in_=mo[:], stream="rsst", reads=[mok], writes=[("rs_st", q)])

        NCI = Lb // SCH
        for b in range(BATCH):
            pre_phase(b, 0)
        for b in range(BATCH):
            if NCI > 1:
                pre_phase(b, 1)
        for b in range(BATCH):
            post_dt(b, 0)
        def fire_rs(q):
            p.collective(lambda e, q=q: e.collective_compute(
                "ReduceScatter", OP.add, replica_groups=[list(range(NCORES))],
                ins=[rs_in[q].opt()], outs=[rs_out[q].opt()]),
                 reads=[("rs_st", q)], writes=[("rs_done", q)])

        for ci in range(NCI):
            for b in range(BATCH):
                def mid(b=b, ci=ci):
                    if ci + 1 < NCI:
                        post_dt(b, ci + 1)
                    if b == 0:
                        if ci >= 1:
                            post_wout(1, ci - 1)
                            if (ci - 1) % 4 == 3:
                                fire_rs((ci - 1) // 4)
                    else:
                        post_wout(0, ci)
                post_scan(b, ci, mid=mid)
                if ci + 2 < NCI:
                    pre_phase(b, ci + 2)
        post_wout(1, NCI - 1)
        fire_rs(NQ - 1)
        ph2.close()
        hkeys = [("hcar", b, t, n) for b in range(BATCH) for t in range(NT) for n in range(NST)]
        phase_barrier(p, [("rs_done", q) for q in range(NQ)] + hkeys
                      + [("ucar", b, par, t) for b in range(BATCH) for par in range(3) for t in range(NT)])

        ph3 = contextlib.ExitStack()
        env3 = dict(env)
        make_ffn_phase_env(nc, ph3, env3, nwb=2, tag="p3")
        sb3 = lambda name, shape, dt: ph3.enter_context(nc.sbuf_tensor(name, shape, dt))
        x3 = [[sb3("x3_%d_%d" % (i, t), [128, CH], F32) for t in range(8)] for i in range(2)]
        mo3 = [sb3("mo3_%d" % t, [128, CH], F32) for t in range(8)]
        nxb3 = [[sb3("nxb3_%d_%d" % (i, t), [128, CH], BF16) for t in range(8)] for i in range(2)]
        store_keys = []

        def p3_load(c):
            i = c % 2
            c0 = c * CH
            q = c0 // SCH
            qo = c0 - q * SCH
            for t in range(8):
                p.dma("pool", out=x3[i][t][:], in_=x2p_d[t * 128:(t + 1) * 128, c0:c0 + CH],
                      stream="x3l%d" % t, writes=[("x3", i, t)])
                p.dma("pool", out=mo3[t][:], in_=rs_out[q][t * 128:(t + 1) * 128, qo:qo + CH],
                      stream="mo%d" % t, reads=[("rs_done", q)], writes=[("mo3", t)])

        def p3_ln(c):
            i = c % 2
            for t in range(8):
                p.op("dve", lambda e, i=i, t=t: e.tensor_tensor(
                    out=x3[i][t][:], in0=x3[i][t][:], in1=mo3[t][:], op=OP.add),
                     reads=[("x3", i, t), ("mo3", t)], writes=[("x3", i, t)])
            emit_ln(p, env3, [(x3[i][t][:], ("x3", i, t)) for t in range(8)],
                    [(nxb3[i][t][:], ("nxb3", i, t)) for t in range(8)], 8, CH, gb=None)

        p3_load(0)
        p3_ln(0)
        for c in range(NCH1):
            i = c % 2
            if c + 1 < NCH1:
                p3_load(c + 1)
            emit_ffn_l1(p, env3, [(nxb3[i][t][:], ("nxb3", i, t)) for t in range(8)], wf0, 8, 32)
            if c + 1 < NCH1:
                p3_ln(c + 1)
            emit_ffn_l2(p, env3, wf1, 32, 32)
            emit_ffn_l3(p, env3, [(x3[i][t][:], ("x3", i, t)) for t in range(8)], wf2, 32, 8)
            for t in range(8):
                skey = ("out_st", c, t)
                p.dma("act", out=out_d[t * 128:(t + 1) * 128, c * CH:(c + 1) * CH], in_=x3[i][t][:],
                      stream="outst", reads=[("x3", i, t)], writes=[skey])
                store_keys.append(skey)
        ph3.close()
        p.op("pool", lambda e: e.nop(nofuse=True), reads=store_keys[-8:])
        with nc.Block() as block:
            p.emit(block)
    return nc


def _forward_fused(inputs):
    x = np.asarray(inputs["L0_attn_x"], np.float32)
    L = x.shape[1]
    T = L // 4
    s0, s1, s2, k0, k1, k2 = _prep_common(inputs)
    lng = np.ascontiguousarray(np.asarray(inputs["L0_attn_ln_g"], np.float32).reshape(8, 128).T)
    lnb = np.ascontiguousarray(np.asarray(inputs["L0_attn_ln_b"], np.float32).reshape(8, 128).T)
    W_in = np.asarray(inputs["L0_attn_W_in"], np.float32)
    W_x = np.asarray(inputs["L0_attn_W_x"], np.float32)
    W_dt = np.asarray(inputs["L0_attn_W_dt"], np.float32)
    conv_w = np.asarray(inputs["L0_attn_conv_w"], np.float32)
    conv_b = np.asarray(inputs["L0_attn_conv_b"], np.float32)
    dt_bias = np.asarray(inputs["L0_attn_dt_bias"], np.float32)
    A_log = np.asarray(inputs["L0_attn_A_log"], np.float32)
    Dv = np.asarray(inputs["L0_attn_D"], np.float32)
    W_out = np.asarray(inputs["L0_attn_W_out"], np.float32)

    nc = build_fused(T=T)
    CHW = 512
    nstr = T // CHW
    in_maps = []
    for c in range(NCORES):
        b = c // 4
        stripes = [(4 * j + c % 4) * CHW for j in range(nstr)]
        xTc = np.concatenate([x[b, s:s + CHW, :].T for s in stripes], axis=1)
        sl = slice(c * DSL, (c + 1) * DSL)
        rows = np.r_[c * DSL:(c + 1) * DSL, DIN + c * DSL:DIN + (c + 1) * DSL]
        in_maps.append(dict(
            xT=np.ascontiguousarray(xTc),
            k0=k0, k1=k1, k2=k2, s0r=s0, s1r=s1, s2r=s2, lng=lng, lnb=lnb,
            winT=np.ascontiguousarray(W_in[rows].T),
            wxT=np.ascontiguousarray(W_x[:, sl].T),
            wdtT=np.ascontiguousarray(W_dt[sl].T),
            cw=_tile2(conv_w[sl]), cb=_tile2(conv_b[sl]), dtb=_tile2(dt_bias[sl]),
            alog=_tile2(A_log[sl]), dsl=_tile2(Dv[sl]),
            woutTs=np.ascontiguousarray(W_out.T[sl]),
        ))
    res = run_bass_kernel_spmd(nc, in_maps, core_ids=list(range(NCORES))).results
    out = np.empty((BATCH, L, E), np.float32)
    for c in range(NCORES):
        b = c // 4
        o = np.asarray(res[c]["outT"])
        for j in range(nstr):
            s = (4 * j + c % 4) * CHW
            out[b, s:s + CHW, :] = o[:, j * CHW:(j + 1) * CHW].T
    return out



def _prep_common(inputs):
    s0 = np.ascontiguousarray(np.broadcast_to(np.asarray(inputs["L0_ffn_s0"], np.float32).reshape(1, 1), (128, 1)))
    s1 = np.ascontiguousarray(np.broadcast_to(np.asarray(inputs["L0_ffn_s1"], np.float32).reshape(1, 1), (128, 1)))
    s2 = np.ascontiguousarray(np.broadcast_to(np.asarray(inputs["L0_ffn_s2"], np.float32).reshape(1, 1), (128, 1)))
    k0 = np.ascontiguousarray(np.asarray(inputs["L0_ffn_k0"]).view(np.uint8))
    k1 = np.ascontiguousarray(np.asarray(inputs["L0_ffn_k1"]).view(np.uint8))
    k2 = np.ascontiguousarray(np.asarray(inputs["L0_ffn_k2"]).view(np.uint8))
    return s0, s1, s2, k0, k1, k2


def _tile2(v):
    v = np.asarray(v, np.float32)
    if v.ndim == 1:
        v = v[:, None]
    k = v.shape[1]
    return np.ascontiguousarray(v.reshape(2, 128, k).transpose(1, 0, 2).reshape(128, 2 * k))


def kernel(**inputs):
    return _forward_fused(inputs)


def _forward(inputs, collect=False):
    x = np.asarray(inputs["L0_attn_x"], np.float32)
    L = x.shape[1]
    T = L // 4
    s0, s1, s2, k0, k1, k2 = _prep_common(inputs)
    lng = np.ascontiguousarray(np.asarray(inputs["L0_attn_ln_g"], np.float32).reshape(8, 128).T)
    lnb = np.ascontiguousarray(np.asarray(inputs["L0_attn_ln_b"], np.float32).reshape(8, 128).T)

    nc1 = build_launch1(T=T)
    in1 = []
    for c in range(NCORES):
        b, lo = c // 4, (c % 4) * T
        xT = np.ascontiguousarray(x[b, lo:lo + T, :].T)
        in1.append(dict(xT=xT, k0=k0, k1=k1, k2=k2, s0r=s0, s1r=s1, s2r=s2, lng=lng, lnb=lnb))
    r1 = run_bass_kernel_spmd(nc1, in1, core_ids=list(range(NCORES))).results
    xn_full = np.empty((BATCH * E, L), dtype=BF)
    x2p = [np.ascontiguousarray(np.asarray(r1[c]["x2p_out"])) for c in range(NCORES)]
    for c in range(NCORES):
        b, lo = c // 4, (c % 4) * T
        xn_full[b * E:(b + 1) * E, lo:lo + T] = np.asarray(r1[c]["xn_out"])

    W_in = np.asarray(inputs["L0_attn_W_in"], np.float32)
    W_x = np.asarray(inputs["L0_attn_W_x"], np.float32)
    W_dt = np.asarray(inputs["L0_attn_W_dt"], np.float32)
    conv_w = np.asarray(inputs["L0_attn_conv_w"], np.float32)
    conv_b = np.asarray(inputs["L0_attn_conv_b"], np.float32)
    dt_bias = np.asarray(inputs["L0_attn_dt_bias"], np.float32)
    A_log = np.asarray(inputs["L0_attn_A_log"], np.float32)
    Dv = np.asarray(inputs["L0_attn_D"], np.float32)
    nc2 = build_launch2(Lb=L, SCH=1024)
    in2 = []
    for c in range(NCORES):
        sl = slice(c * DSL, (c + 1) * DSL)
        rows = np.r_[c * DSL:(c + 1) * DSL, DIN + c * DSL:DIN + (c + 1) * DSL]
        in2.append(dict(
            xn_in=xn_full,
            winT=np.ascontiguousarray(W_in[rows].T),
            wxT=np.ascontiguousarray(W_x[:, sl].T),
            wdtT=np.ascontiguousarray(W_dt[sl].T),
            cw=_tile2(conv_w[sl]),
            cb=_tile2(conv_b[sl]),
            dtb=_tile2(dt_bias[sl]),
            alog=_tile2(A_log[sl]),
            dsl=_tile2(Dv[sl]),
        ))
    r2 = run_bass_kernel_spmd(nc2, in2, core_ids=list(range(NCORES))).results
    yg_full = np.empty((BATCH, DIN, L), dtype=BF)
    for c in range(NCORES):
        ygc = np.asarray(r2[c]["yg_out"]).reshape(BATCH, DSL, L)
        yg_full[:, c * DSL:(c + 1) * DSL, :] = ygc

    W_out = np.asarray(inputs["L0_attn_W_out"], np.float32)
    nc3 = build_launch3(T=T)
    in3 = []
    for c in range(NCORES):
        b, lo = c // 4, (c % 4) * T
        in3.append(dict(
            yg_in=np.ascontiguousarray(yg_full[b, :, lo:lo + T]),
            x2p_in=x2p[c],
            woutT=np.ascontiguousarray(W_out.T),
            k0=k0, k1=k1, k2=k2, s0r=s0, s1r=s1, s2r=s2,
        ))
    r3 = run_bass_kernel_spmd(nc3, in3, core_ids=list(range(NCORES))).results
    out = np.empty((BATCH, L, E), np.float32)
    for c in range(NCORES):
        b, lo = c // 4, (c % 4) * T
        out[b, lo:lo + T, :] = np.asarray(r3[c]["outT"]).T
    inter = None
    if collect:
        x2p_full = np.empty((BATCH * E, L), np.float32)
        for c in range(NCORES):
            b, lo = c // 4, (c % 4) * T
            x2p_full[b * E:(b + 1) * E, lo:lo + T] = x2p[c]
        inter = dict(xn=xn_full, x2p=x2p_full, yg=yg_full)
    return out, inter


# version 22 (final)
# speedup vs baseline: 1.1918x; 1.0305x over previous
import contextlib

import numpy as np
import ml_dtypes

import concourse.bass as bass
import concourse.mybir as mybir
from concourse.bass_utils import run_bass_kernel_spmd

F32 = mybir.dt.float32
BF16 = mybir.dt.bfloat16
U8 = mybir.dt.uint8
AF = mybir.ActivationFunctionType
OP = mybir.AluOpType
BF = ml_dtypes.bfloat16

NCORES = 8
E = 1024
DIN = 2048
NST = 16
DCONV = 4
DTR = 64
BATCH = 2
SEQ = 16384
TPC = SEQ // 4
DSL = DIN // NCORES
CH = 512
EPS = 1e-5


class Prog:

    ENGINES = ("sp", "act", "dve", "pe", "pool")

    def __init__(self, nc, stack):
        self.nc = nc
        self.stack = stack
        self.sems = {}
        self.squeues = {}
        self.equeues = {e: [] for e in self.ENGINES}
        for e in self.ENGINES:
            self._get_sem(e)
        self.last_writer = {}
        self.readers = {}

    def _get_sem(self, key):
        if key not in self.sems:
            self.sems[key] = self.stack.enter_context(self.nc.semaphore("sem_" + str(key)))
            self.squeues[key] = []
        return self.sems[key]

    def _append(self, engine, fn, reads, writes, sem_key, inc, flagged):
        need = {}

        def add(src):
            if src is None:
                return
            sk, idx, eng2 = src
            if eng2 == engine and sk == "pe" and engine == "pe":
                return
            if need.get(sk, -1) < idx:
                need[sk] = idx
        for r in reads:
            add(self.last_writer.get(r))
        for w in writes:
            add(self.last_writer.get(w))
            for rr in self.readers.get(w, ()):
                add(rr)
        for sk, idx in need.items():
            self.squeues[sk][idx][3] = True
        rec = [need, fn, inc, flagged, sem_key]
        self.equeues[engine].append(rec)
        sidx = len(self.squeues[sem_key])
        self.squeues[sem_key].append(rec)
        src = (sem_key, sidx, engine)
        for r in reads:
            self.readers.setdefault(r, []).append(src)
        for w in writes:
            self.last_writer[w] = src
            self.readers[w] = []

    def op(self, engine, fn, reads=(), writes=(), inc=1):
        self._append(engine, fn, reads, writes, engine, inc, False)

    def dma(self, engine, out, in_, stream, reads=(), writes=(), **kw):
        self._get_sem(stream)
        self._append(engine, lambda e: e.dma_start(out=out, in_=in_, **kw),
                     reads, writes, stream, 16, True)

    def collective(self, fn, reads=(), writes=()):
        self._get_sem("cc")
        self._append("pool", fn, reads, writes, "cc", 1, True)

    def emit(self, block):
        decos = dict(sp=block.sync, act=block.scalar, dve=block.vector,
                     pe=block.tensor, pool=block.gpsimd)
        cums = {}
        for sk, q in self.squeues.items():
            c = 0
            cl = []
            for rec in q:
                if rec[3]:
                    c += rec[2]
                cl.append(c)
            cums[sk] = cl
        for ename in self.ENGINES:
            queue = self.equeues[ename]
            if not queue:
                continue

            def body(eng, queue=queue):
                floor = {}
                for (need, fn, inc, flagged, sem_key) in queue:
                    for sk, idx in need.items():
                        val = cums[sk][idx]
                        if floor.get(sk, 0) < val:
                            eng.wait_ge(self.sems[sk], val)
                            floor[sk] = val
                    instr = fn(eng)
                    if flagged:
                        instr.then_inc(self.sems[sem_key], inc)

            decos[ename](body)



def emit_ln(p, env, src, dst, ntiles, width, gb=None, gbkeys=(("s_gb",),)):
    xb, sq = env["lnxb"], env["lnsq"]
    pstatA, pstatB = env["pstatA"], env["pstatB"]
    rows = env["rows"]
    pbc1, pbc2 = env["pbc1"], env["pbc2"]
    onesb, ones1f, cst = env["onesb"], env["ones1f"], env["cst"]
    scr = env["lnscr"]
    inv_n = 1.0 / (128 * ntiles)

    for t in range(ntiles):
        a, k = src[t]
        s2_ = t % 4
        p.op("act", lambda e, a=a, s2_=s2_: e.activation(
            out=xb[:, s2_ * width:(s2_ + 1) * width], in_=a, func=AF.Copy),
             reads=[k], writes=[("lnxb", s2_)])
        p.op("act", lambda e, a=a, s2_=s2_: e.activation(
            out=sq[:, s2_ * width:(s2_ + 1) * width], in_=a, func=AF.Square),
             reads=[k], writes=[("lnsq", s2_)])
        p.op("pe", lambda e, s2_=s2_, t=t: e.matmul(out=pstatA[0:1, :], lhsT=onesb[:, 0:1],
                                                    rhs=xb[:, s2_ * width:(s2_ + 1) * width],
                                                    start=(t == 0), stop=(t == ntiles - 1)),
             reads=[("lnxb", s2_), ("onesb",)], writes=[("pstatA",)])
        p.op("pe", lambda e, s2_=s2_, t=t: e.matmul(out=pstatB[0:1, :], lhsT=onesb[:, 0:1],
                                                    rhs=sq[:, s2_ * width:(s2_ + 1) * width],
                                                    start=(t == 0), stop=(t == ntiles - 1)),
             reads=[("lnsq", s2_), ("onesb",)], writes=[("pstatB",)])
    m_row = rows[0:1, 0:width]
    var_row = rows[0:1, width:2 * width]
    rstd_row = rows[0:1, 2 * width:3 * width]
    tr_row = rows[0:1, 3 * width:4 * width]
    p.op("dve", lambda e: e.tensor_scalar(out=m_row, in0=pstatA[0:1, :], scalar1=inv_n,
                                          scalar2=0.0, op0=OP.mult, op1=OP.add),
         reads=[("pstatA",)], writes=[("m_row",)])
    p.op("act", lambda e: e.activation(out=var_row, in_=m_row, func=AF.Square),
         reads=[("m_row",)], writes=[("var_row",)])
    p.op("dve", lambda e: e.scalar_tensor_tensor(out=var_row, in0=pstatB[0:1, :], scalar=inv_n,
                                                 in1=var_row, op0=OP.mult, op1=OP.subtract),
         reads=[("pstatB",), ("var_row",)], writes=[("var_row",)])
    p.op("act", lambda e: e.activation(out=rstd_row, in_=var_row, func=AF.Ln, bias=cst[0:1, 2:3]),
         reads=[("var_row",), ("cst",)], writes=[("rstd_row",)])
    p.op("act", lambda e: e.activation(out=rstd_row, in_=rstd_row, func=AF.Exp, scale=cst[0:1, 3:4]),
         reads=[("rstd_row",), ("cst",)], writes=[("rstd_row",)])
    p.op("dve", lambda e: e.tensor_tensor(out=tr_row, in0=m_row, in1=rstd_row, op=OP.mult),
         reads=[("m_row",), ("rstd_row",)], writes=[("tr_row",)])
    p.op("pe", lambda e: e.matmul(out=pbc1[:, 0:width], lhsT=ones1f[0:1, :], rhs=tr_row,
                                  start=True, stop=True),
         reads=[("tr_row",), ("ones1f",)], writes=[("pbc1",)])
    p.op("pe", lambda e: e.matmul(out=pbc2[:, 0:width], lhsT=ones1f[0:1, :], rhs=rstd_row,
                                  start=True, stop=True),
         reads=[("rstd_row",), ("ones1f",)], writes=[("pbc2",)])
    for t in range(ntiles):
        a, k = src[t]
        d, dk = dst[t]
        s = scr[t % 2]
        sk = ("lnscr", t % 2)
        p.op("dve", lambda e, a=a, s=s: e.tensor_tensor(out=s[:], in0=a, in1=pbc2[:, 0:width], op=OP.mult),
             reads=[k, ("pbc2",)], writes=[sk])
        if gb is None:
            p.op("dve", lambda e, s=s, d=d: e.tensor_tensor(out=d, in0=s[:], in1=pbc1[:, 0:width], op=OP.subtract),
                 reads=[sk, ("pbc1",)], writes=[dk])
        else:
            p.op("dve", lambda e, s=s: e.tensor_tensor(out=s[:], in0=s[:], in1=pbc1[:, 0:width], op=OP.subtract),
                 reads=[sk, ("pbc1",)], writes=[sk])
            g_ap, b_ap = gb[t]
            p.op("dve", lambda e, s=s, d=d, g_ap=g_ap, b_ap=b_ap: e.tensor_scalar(
                out=d, in0=s[:], scalar1=g_ap, scalar2=b_ap, op0=OP.mult, op1=OP.add),
                 reads=[sk] + list(gbkeys), writes=[dk])


def emit_weight_convert(p, env, kdram, wf_ap, n_ktiles, n_mtiles, mode, tag):
    cva2, cvb2, cvw2 = env["cva"], env["cvb"], env["cvw"]
    width = n_mtiles * 128
    HW = min(width, env.get("cv_hw", 2048))
    mth = HW // 128
    it = 0
    tails = []
    for kt in range(n_ktiles):
        for h in range(width // HW):
            c0 = h * HW
            pa = it % 2
            it += 1
            cva, cvb, cvw = cva2[pa], cvb2[pa], cvw2[pa]
            if mode == "fold":
                p.dma("pool", out=cva[:, 0:HW], in_=kdram[kt * 128:(kt + 1) * 128, c0:c0 + HW],
                      stream="cva%d" % pa, reads=[("cva_rd", pa)], writes=[("cva", pa)])
                p.dma("pool", out=cvb[:, 0:HW], in_=kdram[E + kt * 128:E + (kt + 1) * 128, c0:c0 + HW],
                      stream="cvb%d" % pa, reads=[("cvb_rd", pa)], writes=[("cvb", pa)])
                p.op("dve", lambda e, cva=cva, cvb=cvb, cvw=cvw, HW=HW: e.tensor_tensor(
                    out=cvw[:, 0:HW], in0=cva[:, 0:HW], in1=cvb[:, 0:HW], op=OP.subtract),
                     reads=[("cva", pa), ("cvb", pa), ("cvw_rd", pa)],
                     writes=[("cvw", pa), ("cva_rd", pa), ("cvb_rd", pa)])
            else:
                p.dma("pool", out=cva[:, 0:HW], in_=kdram[kt * 128:(kt + 1) * 128, c0:c0 + HW],
                      stream="cva%d" % pa, reads=[("cva_rd", pa)], writes=[("cva", pa)])
                p.op("dve", lambda e, cva=cva, cvw=cvw, HW=HW: e.tensor_scalar(
                    out=cvw[:, 0:HW], in0=cva[:, 0:HW],
                    scalar1=2.0, scalar2=-1.0, op0=OP.mult, op1=OP.add),
                     reads=[("cva", pa), ("cvw_rd", pa)], writes=[("cvw", pa), ("cva_rd", pa)])
            for mtl in range(mth):
                mt = h * mth + mtl
                last = (mtl == mth - 1)
                p.dma("sp", out=wf_ap[mt, kt], in_=cvw[:, mtl * 128:(mtl + 1) * 128],
                      stream="wfst%d" % pa, reads=[("cvw", pa)],
                      writes=[(tag, mt, kt), ("cvw_rd", pa)] if last else [(tag, mt, kt)])
                if last:
                    tails.append((tag, mt, kt))
    return tails[-2:]


def emit_ffn(p, env, nxb_tiles, res_tiles, wf0, wf1, wf2, n1k, n1m, n2m, n3m):
    emit_ffn_l1(p, env, nxb_tiles, wf0, n1k, n1m)
    emit_ffn_l2(p, env, wf1, n1m, n2m)
    emit_ffn_l3(p, env, res_tiles, wf2, n2m, n3m)


def emit_ffn_l1(p, env, nxb_tiles, wf0, n1k, n1m):
    wbuf0 = env["wbuf0"]
    h1b = env["h1b"]
    pL = env["pL"]
    s0x4 = env["s0x4"]
    for mt in range(n1m):
        wb = wbuf0[mt % len(wbuf0)]
        wk = ("wbuf0", mt % len(wbuf0))
        p.dma("sp", out=wb[:].rearrange("p (k f) -> p k f", k=n1k), in_=wf0[mt],
              stream="wb0_%d" % (mt % len(wbuf0)), reads=[wk + ("free",), ("wfdone", 0)], writes=[wk])
        ps = pL[mt % 4]
        pk = ("pL", mt % 4)
        for k in range(n1k):
            a, ak = nxb_tiles[k]
            last = (k == n1k - 1)
            p.op("pe", lambda e, wb=wb, k=k, a=a, ps=ps, s=(k == 0), t=last: e.matmul(
                out=ps[:], lhsT=wb[:, k * 128:(k + 1) * 128], rhs=a, start=s, stop=t),
                 reads=[wk, ak], writes=[pk, wk + ("free",)] if last else [pk])
        p.op("act", lambda e, ps=ps, mt=mt: e.activation(
            out=h1b[:, mt * CH:(mt + 1) * CH], in_=ps[:], func=AF.Relu, scale=s0x4[:]),
             reads=[pk, ("s0x4",)], writes=[("h1b", mt)])


def emit_ffn_l2(p, env, wf1, n1m, n2m):
    wbuf1 = env["wbuf1"]
    h1b, h2b = env["h1b"], env["h2b"]
    pL = env["pL"]
    s1x2 = env["s1x2"]
    for mt in range(n2m):
        wb = wbuf1[mt % len(wbuf1)]
        wk = ("wbuf1", mt % len(wbuf1))
        p.dma("sp", out=wb[:].rearrange("p (k f) -> p k f", k=n1m), in_=wf1[mt],
              stream="wb1_%d" % (mt % len(wbuf1)), reads=[wk + ("free",), ("wfdone", 1)], writes=[wk])
        ps = pL[mt % 4]
        pk = ("pL", mt % 4)
        for k in range(n1m):
            last = (k == n1m - 1)
            p.op("pe", lambda e, wb=wb, k=k, ps=ps, s=(k == 0), t=last: e.matmul(
                out=ps[:], lhsT=wb[:, k * 128:(k + 1) * 128],
                rhs=h1b[:, k * CH:(k + 1) * CH], start=s, stop=t),
                 reads=[wk, ("h1b", k)], writes=[pk, wk + ("free",)] if last else [pk])
        p.op("act", lambda e, ps=ps, mt=mt: e.activation(
            out=h2b[:, mt * CH:(mt + 1) * CH], in_=ps[:], func=AF.Relu, scale=s1x2[:]),
             reads=[pk, ("s1x2",)], writes=[("h2b", mt)])


def emit_ffn_l3(p, env, res_tiles, wf2, n2m, n3m):
    wbuf2 = env["wbuf2"]
    h2b = env["h2b"]
    pL = env["pL"]
    halfs2 = env["halfs2"]
    for mt in range(n3m):
        wb = wbuf2[mt % len(wbuf2)]
        wk = ("wbuf2", mt % len(wbuf2))
        p.dma("sp", out=wb[:].rearrange("p (k f) -> p k f", k=n2m), in_=wf2[mt],
              stream="wb2_%d" % (mt % len(wbuf2)), reads=[wk + ("free",), ("wfdone", 2)], writes=[wk])
        ps = pL[mt % 4]
        pk = ("pL", mt % 4)
        for k in range(n2m):
            last = (k == n2m - 1)
            p.op("pe", lambda e, wb=wb, k=k, ps=ps, s=(k == 0), t=last: e.matmul(
                out=ps[:], lhsT=wb[:, k * 128:(k + 1) * 128],
                rhs=h2b[:, k * CH:(k + 1) * CH], start=s, stop=t),
                 reads=[wk, ("h2b", k)], writes=[pk, wk + ("free",)] if last else [pk])
        r, rk = res_tiles[mt]
        p.op("dve", lambda e, ps=ps, r=r: e.scalar_tensor_tensor(
            out=r, in0=ps[:], scalar=halfs2[:], in1=r, op0=OP.mult, op1=OP.add),
             reads=[pk, rk, ("halfs2",)], writes=[rk])


def make_ffn_env(nc, stack, env, nwb=2):
    sb = lambda name, shape, dt: stack.enter_context(nc.sbuf_tensor(name, shape, dt))
    ps = lambda name, shape: stack.enter_context(nc.psum_tensor(name, shape, F32))
    env["cst"] = sb("cst", [128, 8], F32)
    env["onesb"] = sb("onesb", [128, 1], BF16)
    env["ones1f"] = sb("ones1f", [1, 128], F32)
    env["cva"] = [sb("cva%d" % i, [128, 2048], BF16) for i in range(2)]
    env["cvb"] = [sb("cvb%d" % i, [128, 2048], BF16) for i in range(2)]
    env["cvw"] = [sb("cvw%d" % i, [128, 2048], BF16) for i in range(2)]
    env["wbuf0"] = [sb("wbuf0_%d" % i, [128, 8 * 128], BF16) for i in range(3)]
    env["wbuf1"] = [sb("wbuf1_%d" % i, [128, 32 * 128], BF16) for i in range(3)]
    env["wbuf2"] = [sb("wbuf2_%d" % i, [128, 32 * 128], BF16) for i in range(nwb)]
    env["h1b"] = sb("h1b", [128, 32 * CH], BF16)
    env["h2b"] = sb("h2b", [128, 32 * CH], BF16)
    env["pL"] = [ps("pL%d" % i, [128, CH]) for i in range(4)]
    env["s0x4"] = sb("s0x4", [128, 1], F32)
    env["s1x2"] = sb("s1x2", [128, 1], F32)
    env["halfs2"] = sb("halfs2", [128, 1], F32)
    env["lnxb"] = sb("lnxb", [128, 4 * CH], BF16)
    env["lnsq"] = sb("lnsq", [128, 4 * CH], BF16)
    env["lnscr"] = [sb("lnscr%d" % i, [128, CH], F32) for i in range(2)]
    env["rows"] = sb("rows", [1, 4 * CH], F32)
    env["pstatA"] = ps("pstatA", [1, CH])
    env["pstatB"] = ps("pstatB", [1, CH])
    env["pbc1"] = ps("pbc1", [128, CH])
    env["pbc2"] = ps("pbc2", [128, CH])


def emit_consts_scales(p, env, s_s0, s_s1, s_s2):
    cst, onesb, ones1f = env["cst"], env["onesb"], env["ones1f"]
    p.op("dve", lambda e: e.memset(cst[:, 0:1], 1.0), writes=[("cst",)])
    p.op("dve", lambda e: e.memset(cst[:, 1:2], -1.0), writes=[("cst",)])
    p.op("dve", lambda e: e.memset(cst[:, 2:3], EPS), writes=[("cst",)])
    p.op("dve", lambda e: e.memset(cst[:, 3:4], -0.5), writes=[("cst",)])
    p.op("dve", lambda e: e.memset(onesb[:], 1.0), writes=[("onesb",)])
    p.op("dve", lambda e: e.memset(ones1f[:], 1.0), writes=[("ones1f",)])
    for name, src, f in (("s0x4", s_s0, 4.0), ("s1x2", s_s1, 2.0), ("halfs2", s_s2, 0.5)):
        dst = env[name]
        p.op("dve", lambda e, dst=dst, src=src, f=f: e.tensor_scalar(
            out=dst[:], in0=src[:], scalar1=f, scalar2=0.0, op0=OP.mult, op1=OP.add),
             reads=[(name, "in")], writes=[(name,)])


def setup_ffn_weights(nc, p, env, k0, k1, k2):
    wf0d = nc.dram_tensor("wf0d", [32, 8, 128, 128], BF16).ap()
    wf1d = nc.dram_tensor("wf1d", [32, 32, 128, 128], BF16).ap()
    wf2d = nc.dram_tensor("wf2d", [8, 32, 128, 128], BF16).ap()
    t0 = emit_weight_convert(p, env, k0, wf0d, 8, 32, "fold", "wf0")
    p.op("dve", lambda e: e.memset(env["cst"][:, 4:5], 0.0),
         reads=t0, writes=[("wfdone", 0)])
    t1 = emit_weight_convert(p, env, k1, wf1d, 32, 32, "pm1", "wf1")
    p.op("dve", lambda e: e.memset(env["cst"][:, 5:6], 0.0),
         reads=t1, writes=[("wfdone", 1)])
    t2 = emit_weight_convert(p, env, k2, wf2d, 32, 8, "pm1", "wf2")
    p.op("dve", lambda e: e.memset(env["cst"][:, 6:7], 0.0),
         reads=t2, writes=[("wfdone", 2)])
    wf0 = [wf0d[mt].rearrange("k p f -> p k f") for mt in range(32)]
    wf1 = [wf1d[mt].rearrange("k p f -> p k f") for mt in range(32)]
    wf2 = [wf2d[mt].rearrange("k p f -> p k f") for mt in range(8)]
    return wf0, wf1, wf2



def build_launch1(T=TPC):
    nc = bass.Bass("TRN2", target_bir_lowering=False, debug=False)
    xT = nc.dram_tensor("xT", [E, T], F32, kind="ExternalInput").ap()
    k0 = nc.dram_tensor("k0", [2 * E, 4096], U8, kind="ExternalInput").ap()
    k1 = nc.dram_tensor("k1", [4096, 4096], U8, kind="ExternalInput").ap()
    k2 = nc.dram_tensor("k2", [4096, E], U8, kind="ExternalInput").ap()
    s0r = nc.dram_tensor("s0r", [128, 1], F32, kind="ExternalInput").ap()
    s1r = nc.dram_tensor("s1r", [128, 1], F32, kind="ExternalInput").ap()
    s2r = nc.dram_tensor("s2r", [128, 1], F32, kind="ExternalInput").ap()
    lng = nc.dram_tensor("lng", [128, 8], F32, kind="ExternalInput").ap()
    lnb = nc.dram_tensor("lnb", [128, 8], F32, kind="ExternalInput").ap()
    xn_out = nc.dram_tensor("xn_out", [E, T], BF16, kind="ExternalOutput").ap()
    x2p_out = nc.dram_tensor("x2p_out", [E, T], F32, kind="ExternalOutput").ap()

    with contextlib.ExitStack() as stack:
        env = {}
        sb = lambda name, shape, dt: stack.enter_context(nc.sbuf_tensor(name, shape, dt))
        make_ffn_env(nc, stack, env)
        xc = [sb("xc%d" % t, [128, CH], F32) for t in range(8)]
        xin = [sb("xin%d" % t, [128, CH], F32) for t in range(8)]
        nxb = [sb("nxb%d" % t, [128, CH], BF16) for t in range(8)]
        s_s0 = sb("s_s0", [128, 1], F32)
        s_s1 = sb("s_s1", [128, 1], F32)
        s_s2 = sb("s_s2", [128, 1], F32)
        s_g = sb("s_g", [128, 8], F32)
        s_b = sb("s_b", [128, 8], F32)

        p = Prog(nc, stack)
        p.dma("sp", out=s_s0[:], in_=s0r, stream="ss0", writes=[("s0x4", "in")])
        p.dma("sp", out=s_s1[:], in_=s1r, stream="ss1", writes=[("s1x2", "in")])
        p.dma("sp", out=s_s2[:], in_=s2r, stream="ss2", writes=[("halfs2", "in")])
        p.dma("sp", out=s_g[:], in_=lng, stream="sgv", writes=[("s_g",)])
        p.dma("sp", out=s_b[:], in_=lnb, stream="sbv", writes=[("s_gb",)])
        emit_consts_scales(p, env, s_s0, s_s1, s_s2)
        wf0, wf1, wf2 = setup_ffn_weights(nc, p, env, k0, k1, k2)

        nch = T // CH
        store_keys = []
        for c in range(nch):
            c0 = c * CH
            for t in range(8):
                p.dma("sp", out=xc[t][:], in_=xT[t * 128:(t + 1) * 128, c0:c0 + CH],
                      stream="xc%d" % t, writes=[("xc", t)])
            emit_ln(p, env, [(xc[t][:], ("xc", t)) for t in range(8)],
                    [(nxb[t][:], ("nxb", t)) for t in range(8)], 8, CH, gb=None)
            emit_ffn(p, env, [(nxb[t][:], ("nxb", t)) for t in range(8)],
                     [(xc[t][:], ("xc", t)) for t in range(8)],
                     wf0, wf1, wf2, 8, 32, 32, 8)
            emit_ln(p, env, [(xc[t][:], ("xc", t)) for t in range(8)],
                    [(xin[t][:], ("xin", t)) for t in range(8)], 8, CH, gb=None)
            for t in range(8):
                p.op("dve", lambda e, t=t: e.tensor_tensor(out=xc[t][:], in0=xc[t][:], in1=xin[t][:], op=OP.add),
                     reads=[("xc", t), ("xin", t)], writes=[("xc", t)])
                skey = ("x2p_st", c, t)
                p.dma("sp", out=x2p_out[t * 128:(t + 1) * 128, c0:c0 + CH], in_=xc[t][:],
                      stream="x2pst", reads=[("xc", t)], writes=[skey])
                store_keys.append(skey)
            emit_ln(p, env, [(xin[t][:], ("xin", t)) for t in range(8)],
                    [(nxb[t][:], ("nxb", t)) for t in range(8)], 8, CH,
                    gb=[(s_g[:, t:t + 1], s_b[:, t:t + 1]) for t in range(8)], gbkeys=[("s_g",), ("s_gb",)])
            for t in range(8):
                skey = ("xn_st", c, t)
                p.dma("sp", out=xn_out[t * 128:(t + 1) * 128, c0:c0 + CH], in_=nxb[t][:],
                      stream="xnst", reads=[("nxb", t)], writes=[skey])
                store_keys.append(skey)
        p.op("pool", lambda e: e.nop(nofuse=True), reads=store_keys[-16:])

        with nc.Block() as block:
            p.emit(block)
    return nc



def build_launch2(Lb=SEQ, SCH=1024):
    nc = bass.Bass("TRN2", target_bir_lowering=False, debug=False)
    xn_in = nc.dram_tensor("xn_in", [BATCH * E, Lb], BF16, kind="ExternalInput").ap()
    winT = nc.dram_tensor("winT", [E, 2 * DSL], F32, kind="ExternalInput").ap()
    wxT = nc.dram_tensor("wxT", [DSL, 96], F32, kind="ExternalInput").ap()
    wdtT = nc.dram_tensor("wdtT", [DTR, DSL], F32, kind="ExternalInput").ap()
    cw = nc.dram_tensor("cw", [128, 2 * DCONV], F32, kind="ExternalInput").ap()
    cb = nc.dram_tensor("cb", [128, 2], F32, kind="ExternalInput").ap()
    dtb = nc.dram_tensor("dtb", [128, 2], F32, kind="ExternalInput").ap()
    alog = nc.dram_tensor("alog", [128, 2 * NST], F32, kind="ExternalInput").ap()
    dsl_in = nc.dram_tensor("dsl", [128, 2], F32, kind="ExternalInput").ap()
    yg_out = nc.dram_tensor("yg_out", [BATCH * DSL, Lb], BF16, kind="ExternalOutput").ap()

    ar_in = nc.dram_tensor("ar_in", [96, SCH], F32).ap()
    ar_out = nc.dram_tensor("ar_out", [96, SCH], F32).ap()

    NT = DSL // 128
    NJ = SCH // 512

    with contextlib.ExitStack() as stack:
        sb = lambda name, shape, dt: stack.enter_context(nc.sbuf_tensor(name, shape, dt))
        psf = lambda name, shape: stack.enter_context(nc.psum_tensor(name, shape, F32))
        xnb = [sb("xnb%d" % t, [128, SCH], BF16) for t in range(8)]
        winsb = sb("winsb", [128, 8 * 4 * 128], BF16)
        wxsb = sb("wxsb", [128, 2 * 96], BF16)
        wdtsb = sb("wdtsb", [DTR, 2 * 128], BF16)
        s_cw = sb("s_cw", [128, 2 * DCONV], F32)
        s_cb = sb("s_cb", [128, 2], F32)
        s_dtb = sb("s_dtb", [128, 2], F32)
        s_aneg = sb("s_aneg", [128, 2 * NST], F32)
        s_d = sb("s_d", [128, 2], F32)
        cst = sb("cstl2", [128, 2], F32)
        u_raw = [sb("u_raw%d" % t, [128, DCONV - 1 + SCH], F32) for t in range(NT)]
        z = [sb("z%d" % t, [128, SCH], F32) for t in range(NT)]
        conv = [sb("conv%d" % t, [128, SCH], F32) for t in range(NT)]
        us = [sb("us%d" % t, [128, SCH], F32) for t in range(NT)]
        ub = [sb("ub%d" % t, [128, SCH], BF16) for t in range(NT)]
        dbcs = sb("dbcs", [96, SCH], F32)
        dtinb = sb("dtinb", [DTR, SCH], BF16)
        dt = [sb("dt%d" % t, [128, SCH], F32) for t in range(NT)]
        dtu = [sb("dtu%d" % t, [128, SCH], F32) for t in range(NT)]
        sp1 = sb("sp1", [128, 512], F32)
        sp2 = sb("sp2", [128, 512], F32)
        bcB = sb("bcB", [128, SCH], F32)
        bcC = sb("bcC", [128, SCH], F32)
        dA = sb("dA", [128, SCH], F32)
        dBu = sb("dBu", [128, SCH], F32)
        hb = sb("hb", [128, SCH], F32)
        hcb = sb("hcb", [128, SCH], F32)
        y = [sb("y%d" % t, [128, SCH], F32) for t in range(NT)]
        ygb = sb("ygb", [128, SCH], BF16)
        hcar = [sb("hcar%d" % t, [128, NST], F32) for t in range(NT)]
        pW = [psf("pW%d" % i, [128, 512]) for i in range(4)]

        p = Prog(nc, stack)
        for k in range(8):
            p.dma("pool", out=winsb[:, k * 512:(k + 1) * 512], in_=winT[k * 128:(k + 1) * 128, :],
                  stream="winl", writes=[("winsb",)])
        for kt in range(NT):
            p.dma("pool", out=wxsb[:, kt * 96:(kt + 1) * 96], in_=wxT[kt * 128:(kt + 1) * 128, :],
                  stream="wxl", writes=[("wxsb",)])
        p.dma("pool", out=wdtsb[:], in_=wdtT, stream="wdtl", writes=[("wdtsb",)])
        p.dma("sp", out=s_cw[:], in_=cw, stream="scw", writes=[("s_cw",)])
        p.dma("sp", out=s_cb[:], in_=cb, stream="scb", writes=[("s_cb",)])
        p.dma("sp", out=s_dtb[:], in_=dtb, stream="sdtb", writes=[("s_dtb",)])
        p.dma("sp", out=s_aneg[:], in_=alog, stream="saneg", writes=[("s_aneg",)])
        p.dma("sp", out=s_d[:], in_=dsl_in, stream="sdv", writes=[("s_d",)])
        p.op("dve", lambda e: e.memset(cst[:, 0:1], 1.0), writes=[("cst2",)])
        p.op("dve", lambda e: e.memset(cst[:, 1:2], -1.0), writes=[("cst2",)])
        p.op("act", lambda e: e.activation(out=s_aneg[:], in_=s_aneg[:], func=AF.Exp),
             reads=[("s_aneg",)], writes=[("s_aneg",)])
        p.op("dve", lambda e: e.tensor_scalar(out=s_aneg[:], in0=s_aneg[:], scalar1=-1.0,
                                              scalar2=0.0, op0=OP.mult, op1=OP.add),
             reads=[("s_aneg",)], writes=[("s_aneg",)])

        store_keys = []
        for b in range(BATCH):
            for t in range(NT):
                p.op("dve", lambda e, t=t: e.memset(hcar[t][:], 0.0),
                     reads=[("hcar_rd", t)],
                     writes=[("hcar", t, n) for n in range(NST)])
            for ci in range(Lb // SCH):
                c0 = ci * SCH
                for t8 in range(8):
                    p.dma("sp", out=xnb[t8][:], in_=xn_in[b * E + t8 * 128:b * E + (t8 + 1) * 128, c0:c0 + SCH],
                          stream="xnb%d" % t8, writes=[("xnb", t8)])
                for t in range(NT):
                    if ci == 0:
                        p.op("dve", lambda e, t=t: e.memset(u_raw[t][:, 0:DCONV - 1], 0.0),
                             reads=[("u_raw", t)], writes=[("ucar", t)])
                    else:
                        p.op("dve", lambda e, t=t: e.tensor_copy(
                            out=u_raw[t][:, 0:DCONV - 1], in_=u_raw[t][:, SCH:SCH + DCONV - 1]),
                             reads=[("u_raw", t)], writes=[("ucar", t)])
                for mt in range(4):
                    for j in range(NJ):
                        pp = pW[(mt * NJ + j) % 4]
                        pk = ("pW", (mt * NJ + j) % 4)
                        for k in range(8):
                            p.op("pe", lambda e, pp=pp, k=k, mt=mt, j=j, s=(k == 0), t_=(k == 7): e.matmul(
                                out=pp[:], lhsT=winsb[:, k * 512 + mt * 128:k * 512 + (mt + 1) * 128],
                                rhs=xnb[k][:, j * 512:(j + 1) * 512], start=s, stop=t_),
                                 reads=[("winsb",), ("xnb", k)], writes=[pk])
                        if mt < NT:
                            t = mt
                            p.op("act", lambda e, pp=pp, t=t, j=j: e.activation(
                                out=u_raw[t][:, DCONV - 1 + j * 512:DCONV - 1 + (j + 1) * 512],
                                in_=pp[:], func=AF.Copy),
                                 reads=[pk, ("ucar", t)], writes=[("u_raw", t)])
                        else:
                            t = mt - NT
                            p.op("act", lambda e, pp=pp, t=t, j=j: e.activation(
                                out=z[t][:, j * 512:(j + 1) * 512], in_=pp[:], func=AF.Copy),
                                 reads=[pk, ("z_free", t)], writes=[("z", t)])
                for t in range(NT):
                    p.op("dve", lambda e, t=t: e.tensor_scalar(
                        out=conv[t][:], in0=u_raw[t][:, 0:SCH], scalar1=s_cw[:, t * DCONV:t * DCONV + 1],
                        scalar2=0.0, op0=OP.mult, op1=OP.add),
                         reads=[("u_raw", t), ("ucar", t), ("s_cw",), ("conv_rd", t)], writes=[("conv", t)])
                    for kk in range(1, DCONV):
                        p.op("dve", lambda e, t=t, kk=kk: e.scalar_tensor_tensor(
                            out=conv[t][:], in0=u_raw[t][:, kk:kk + SCH],
                            scalar=s_cw[:, t * DCONV + kk:t * DCONV + kk + 1],
                            in1=conv[t][:], op0=OP.mult, op1=OP.add),
                             reads=[("u_raw", t), ("conv", t), ("s_cw",)], writes=[("conv", t)])
                    p.op("act", lambda e, t=t: e.activation(
                        out=us[t][:], in_=conv[t][:], func=AF.Silu, bias=s_cb[:, t:t + 1]),
                         reads=[("conv", t), ("s_cb",), ("us_free", t)], writes=[("us", t), ("conv_rd", t)])
                    p.op("act", lambda e, t=t: e.activation(out=ub[t][:], in_=us[t][:], func=AF.Copy),
                         reads=[("us", t)], writes=[("ub", t)])
                for j in range(NJ):
                    pp = pW[j % 2]
                    pk = ("pW", j % 2)
                    for kt in range(NT):
                        p.op("pe", lambda e, pp=pp, kt=kt, j=j, s=(kt == 0), t_=(kt == NT - 1): e.matmul(
                            out=pp[0:96, :], lhsT=wxsb[:, kt * 96:(kt + 1) * 96],
                            rhs=ub[kt][:, j * 512:(j + 1) * 512], start=s, stop=t_),
                             reads=[("wxsb",), ("ub", kt)], writes=[pk])
                    p.op("act", lambda e, pp=pp, j=j: e.activation(
                        out=dbcs[0:96, j * 512:(j + 1) * 512], in_=pp[0:96, :], func=AF.Copy),
                         reads=[pk, ("dbcs_rd",)], writes=[("dbcs",)])
                p.dma("sp", out=ar_in, in_=dbcs[0:96, :], stream="arin", reads=[("dbcs",)], writes=[("ar_in",), ("dbcs_rd",)])
                p.collective(lambda e: e.collective_compute(
                    "AllReduce", OP.add, replica_groups=[list(range(NCORES))],
                    ins=[ar_in.opt()], outs=[ar_out.opt()]),
                     reads=[("ar_in",)], writes=[("ar_out",)])
                p.dma("pool", out=dtinb[:], in_=ar_out[0:DTR, :], stream="dtinb",
                      reads=[("ar_out",)], writes=[("dtinb",)])
                for mt in range(NT):
                    for j in range(NJ):
                        pp = pW[2 + (j % 2)]
                        pk = ("pW", 2 + (j % 2))
                        p.op("pe", lambda e, pp=pp, mt=mt, j=j: e.matmul(
                            out=pp[:], lhsT=wdtsb[:, mt * 128:(mt + 1) * 128],
                            rhs=dtinb[:, j * 512:(j + 1) * 512], start=True, stop=True),
                             reads=[("wdtsb",), ("dtinb",)], writes=[pk])
                        p.op("act", lambda e, pp=pp, mt=mt, j=j: e.activation(
                            out=dt[mt][:, j * 512:(j + 1) * 512], in_=pp[:], func=AF.Relu,
                            bias=s_dtb[:, mt:mt + 1]),
                             reads=[pk, ("s_dtb",)], writes=[("dt", mt)])
                        p.op("act", lambda e, pp=pp, mt=mt: e.activation(
                            out=sp1[:], in_=pp[:], func=AF.Abs, bias=s_dtb[:, mt:mt + 1]),
                             reads=[pk, ("s_dtb",), ("sp1_rd",)], writes=[("sp1",)])
                        p.op("act", lambda e: e.activation(out=sp2[:], in_=sp1[:], func=AF.Exp,
                                                           scale=cst[:, 1:2]),
                             reads=[("sp1",), ("cst2",), ("sp2_rd",)], writes=[("sp2",), ("sp1_rd",)])
                        p.op("act", lambda e: e.activation(out=sp1[:], in_=sp2[:], func=AF.Ln,
                                                           bias=cst[:, 0:1]),
                             reads=[("sp2",), ("cst2",), ("sp1_rd",)], writes=[("sp1",), ("sp2_rd",)])
                        p.op("dve", lambda e, mt=mt, j=j: e.tensor_tensor(
                            out=dt[mt][:, j * 512:(j + 1) * 512],
                            in0=dt[mt][:, j * 512:(j + 1) * 512], in1=sp1[:], op=OP.add),
                             reads=[("dt", mt), ("sp1",)], writes=[("dt", mt), ("sp1_rd",)])
                for t in range(NT):
                    p.op("dve", lambda e, t=t: e.tensor_tensor(out=dtu[t][:], in0=dt[t][:], in1=us[t][:], op=OP.mult),
                         reads=[("dt", t), ("us", t)], writes=[("dtu", t)])
                for n in range(NST):
                    p.dma("sp", out=bcB[:], in_=ar_out[DTR + n, :].partition_broadcast(128),
                          stream="bcB", reads=[("ar_out",)], writes=[("bcB",)])
                    p.dma("sp", out=bcC[:], in_=ar_out[DTR + NST + n, :].partition_broadcast(128),
                          stream="bcC", reads=[("ar_out",)], writes=[("bcC",)])
                    for t in range(NT):
                        p.op("act", lambda e, t=t, n=n: e.activation(
                            out=dA[:], in_=dt[t][:], func=AF.Exp, scale=s_aneg[:, t * NST + n:t * NST + n + 1]),
                             reads=[("dt", t), ("s_aneg",), ("dA_rd",)], writes=[("dA",)])
                        p.op("dve", lambda e, t=t: e.tensor_tensor(out=dBu[:], in0=dtu[t][:], in1=bcB[:], op=OP.mult),
                             reads=[("dtu", t), ("bcB",), ("dBu_rd",)], writes=[("dBu",)])
                        p.op("dve", lambda e, t=t, n=n: e.tensor_tensor_scan(
                            out=hb[:], data0=dA[:], data1=dBu[:], initial=hcar[t][:, n:n + 1],
                            op0=OP.mult, op1=OP.add),
                             reads=[("dA",), ("dBu",), ("hcar", t, n), ("hb_rd",)],
                             writes=[("hb",), ("dA_rd",), ("dBu_rd",)])
                        p.op("act", lambda e, t=t, n=n: e.activation(
                            out=hcar[t][:, n:n + 1], in_=hb[:, SCH - 1:SCH], func=AF.Copy),
                             reads=[("hb",)], writes=[("hcar", t, n), ("hcar_rd", t)])
                        if n == 0:
                            p.op("dve", lambda e, t=t: e.tensor_tensor(out=y[t][:], in0=hb[:], in1=bcC[:], op=OP.mult),
                                 reads=[("hb",), ("bcC",), ("y_free", t)], writes=[("y", t), ("hb_rd",)])
                        else:
                            p.op("dve", lambda e, t=t: e.tensor_tensor(out=hcb[:], in0=hb[:], in1=bcC[:], op=OP.mult),
                                 reads=[("hb",), ("bcC",), ("hcb_rd",)], writes=[("hcb",), ("hb_rd",)])
                            p.op("dve", lambda e, t=t: e.tensor_tensor(out=y[t][:], in0=y[t][:], in1=hcb[:], op=OP.add),
                                 reads=[("y", t), ("hcb",)], writes=[("y", t), ("hcb_rd",)])

                for t in range(NT):
                    p.op("dve", lambda e, t=t: e.scalar_tensor_tensor(
                        out=y[t][:], in0=us[t][:], scalar=s_d[:, t:t + 1], in1=y[t][:],
                        op0=OP.mult, op1=OP.add),
                         reads=[("us", t), ("y", t), ("s_d",)], writes=[("y", t), ("us_free", t)])
                    p.op("act", lambda e, t=t: e.activation(out=z[t][:], in_=z[t][:], func=AF.Silu),
                         reads=[("z", t)], writes=[("z", t)])
                    p.op("dve", lambda e, t=t: e.tensor_tensor(out=ygb[:], in0=y[t][:], in1=z[t][:], op=OP.mult),
                         reads=[("y", t), ("z", t), ("ygb_rd",)], writes=[("ygb",), ("y_free", t), ("z_free", t)])
                    skey = ("yg_st", b, ci, t)
                    p.dma("sp", out=yg_out[b * DSL + t * 128:b * DSL + (t + 1) * 128, c0:c0 + SCH],
                          in_=ygb[:], stream="ygst", reads=[("ygb",)], writes=[skey, ("ygb_rd",)])
                    store_keys.append(skey)

        p.op("pool", lambda e: e.nop(nofuse=True), reads=store_keys[-2 * NT:])
        with nc.Block() as block:
            p.emit(block)
    return nc



def build_launch3(T=TPC):
    nc = bass.Bass("TRN2", target_bir_lowering=False, debug=False)
    yg_in = nc.dram_tensor("yg_in", [DIN, T], BF16, kind="ExternalInput").ap()
    x2p_in = nc.dram_tensor("x2p_in", [E, T], F32, kind="ExternalInput").ap()
    woutT = nc.dram_tensor("woutT", [DIN, E], F32, kind="ExternalInput").ap()
    k0 = nc.dram_tensor("k0", [2 * E, 4096], U8, kind="ExternalInput").ap()
    k1 = nc.dram_tensor("k1", [4096, 4096], U8, kind="ExternalInput").ap()
    k2 = nc.dram_tensor("k2", [4096, E], U8, kind="ExternalInput").ap()
    s0r = nc.dram_tensor("s0r", [128, 1], F32, kind="ExternalInput").ap()
    s1r = nc.dram_tensor("s1r", [128, 1], F32, kind="ExternalInput").ap()
    s2r = nc.dram_tensor("s2r", [128, 1], F32, kind="ExternalInput").ap()
    out_d = nc.dram_tensor("outT", [E, T], F32, kind="ExternalOutput").ap()

    with contextlib.ExitStack() as stack:
        env = {}
        sb = lambda name, shape, dt: stack.enter_context(nc.sbuf_tensor(name, shape, dt))
        make_ffn_env(nc, stack, env, nwb=1)
        woutsb = sb("woutsb", [128, 16 * 1024], BF16)
        ygt = [sb("ygt%d" % t, [128, CH], BF16) for t in range(16)]
        x3 = [sb("x3_%d" % t, [128, CH], F32) for t in range(8)]
        nxb = [sb("nxb%d" % t, [128, CH], BF16) for t in range(8)]
        s_s0 = sb("s_s0", [128, 1], F32)
        s_s1 = sb("s_s1", [128, 1], F32)
        s_s2 = sb("s_s2", [128, 1], F32)

        p = Prog(nc, stack)
        p.dma("sp", out=s_s0[:], in_=s0r, stream="ss0", writes=[("s0x4", "in")])
        p.dma("sp", out=s_s1[:], in_=s1r, stream="ss1", writes=[("s1x2", "in")])
        p.dma("sp", out=s_s2[:], in_=s2r, stream="ss2", writes=[("halfs2", "in")])
        emit_consts_scales(p, env, s_s0, s_s1, s_s2)
        for k in range(16):
            p.dma("pool", out=woutsb[:, k * 1024:(k + 1) * 1024], in_=woutT[k * 128:(k + 1) * 128, :],
                  stream="woutl", writes=[("woutsb",)])
        wf0, wf1, wf2 = setup_ffn_weights(nc, p, env, k0, k1, k2)

        nch = T // CH
        store_keys = []
        for c in range(nch):
            c0 = c * CH
            for t in range(16):
                p.dma("sp", out=ygt[t][:], in_=yg_in[t * 128:(t + 1) * 128, c0:c0 + CH],
                      stream="ygt%d" % t, writes=[("ygt", t)])
            for t in range(8):
                p.dma("sp", out=x3[t][:], in_=x2p_in[t * 128:(t + 1) * 128, c0:c0 + CH],
                      stream="x3l%d" % t, writes=[("x3", t)])
            for mt in range(8):
                pp = env["pL"][mt % 4]
                pk = ("pL", mt % 4)
                for k in range(16):
                    p.op("pe", lambda e, pp=pp, k=k, mt=mt, s=(k == 0), t_=(k == 15): e.matmul(
                        out=pp[:], lhsT=woutsb[:, k * 1024 + mt * 128:k * 1024 + (mt + 1) * 128],
                        rhs=ygt[k][:], start=s, stop=t_),
                         reads=[("woutsb",), ("ygt", k)], writes=[pk])
                p.op("dve", lambda e, pp=pp, mt=mt: e.scalar_tensor_tensor(
                    out=x3[mt][:], in0=pp[:], scalar=env["cst"][:, 0:1], in1=x3[mt][:],
                    op0=OP.mult, op1=OP.add),
                     reads=[pk, ("x3", mt), ("cst",)], writes=[("x3", mt)])

            emit_ln(p, env, [(x3[t][:], ("x3", t)) for t in range(8)],
                    [(nxb[t][:], ("nxb", t)) for t in range(8)], 8, CH, gb=None)
            emit_ffn(p, env, [(nxb[t][:], ("nxb", t)) for t in range(8)],
                     [(x3[t][:], ("x3", t)) for t in range(8)],
                     wf0, wf1, wf2, 8, 32, 32, 8)
            for t in range(8):
                skey = ("out_st", c, t)
                p.dma("sp", out=out_d[t * 128:(t + 1) * 128, c0:c0 + CH], in_=x3[t][:],
                      stream="outst", reads=[("x3", t)], writes=[skey])
                store_keys.append(skey)
        p.op("pool", lambda e: e.nop(nofuse=True), reads=store_keys[-8:])
        with nc.Block() as block:
            p.emit(block)
    return nc




def make_persist_env(nc, stack, env):
    sb = lambda name, shape, dt: stack.enter_context(nc.sbuf_tensor(name, shape, dt))
    env["cst"] = sb("cst", [128, 8], F32)
    env["onesb"] = sb("onesb", [128, 1], BF16)
    env["ones1f"] = sb("ones1f", [1, 128], F32)
    env["s0x4"] = sb("s0x4", [128, 1], F32)
    env["s1x2"] = sb("s1x2", [128, 1], F32)
    env["halfs2"] = sb("halfs2", [128, 1], F32)


def make_ffn_phase_env(nc, stack, env, nwb=2, tag=""):
    sb = lambda name, shape, dt: stack.enter_context(nc.sbuf_tensor(tag + name, shape, dt))
    ps = lambda name, shape: stack.enter_context(nc.psum_tensor(tag + name, shape, F32))
    env["wbuf0"] = [sb("wbuf0_%d" % i, [128, 8 * 128], BF16) for i in range(3)]
    env["wbuf1"] = [sb("wbuf1_%d" % i, [128, 32 * 128], BF16) for i in range(2)]
    env["wbuf2"] = [sb("wbuf2_%d" % i, [128, 32 * 128], BF16) for i in range(nwb)]
    env["h1b"] = sb("h1b", [128, 32 * CH], BF16)
    env["h2b"] = sb("h2b", [128, 32 * CH], BF16)
    env["pL"] = [ps("pL%d" % i, [128, CH]) for i in range(4)]
    env["lnxb"] = sb("lnxb", [128, 4 * CH], BF16)
    env["lnsq"] = sb("lnsq", [128, 4 * CH], BF16)
    env["lnscr"] = [sb("lnscr%d" % i, [128, CH], F32) for i in range(2)]
    env["rows"] = sb("rows", [1, 4 * CH], F32)
    env["pstatA"] = ps("pstatA", [1, CH])
    env["pstatB"] = ps("pstatB", [1, CH])
    env["pbc1"] = ps("pbc1", [128, CH])
    env["pbc2"] = ps("pbc2", [128, CH])


def phase_barrier(p, keys):
    for e in Prog.ENGINES:
        p.op(e, lambda en: en.nop(nofuse=True), reads=list(keys))


def build_fused(T=TPC, SCH=512):
    SCH = min(SCH, T)
    Lb = 4 * T
    NT = DSL // 128
    NJ = SCH // 512
    NCH1 = T // CH
    NQ = T // SCH
    nc = bass.Bass("TRN2", target_bir_lowering=False, debug=False)
    xT = nc.dram_tensor("xT", [E, T], F32, kind="ExternalInput").ap()
    k0 = nc.dram_tensor("k0", [2 * E, 4096], U8, kind="ExternalInput").ap()
    k1 = nc.dram_tensor("k1", [4096, 4096], U8, kind="ExternalInput").ap()
    k2 = nc.dram_tensor("k2", [4096, E], U8, kind="ExternalInput").ap()
    s0r = nc.dram_tensor("s0r", [128, 1], F32, kind="ExternalInput").ap()
    s1r = nc.dram_tensor("s1r", [128, 1], F32, kind="ExternalInput").ap()
    s2r = nc.dram_tensor("s2r", [128, 1], F32, kind="ExternalInput").ap()
    lng = nc.dram_tensor("lng", [128, 8], F32, kind="ExternalInput").ap()
    lnb = nc.dram_tensor("lnb", [128, 8], F32, kind="ExternalInput").ap()
    winT = nc.dram_tensor("winT", [E, 2 * DSL], F32, kind="ExternalInput").ap()
    wxT = nc.dram_tensor("wxT", [DSL, 96], F32, kind="ExternalInput").ap()
    wdtT = nc.dram_tensor("wdtT", [DTR, DSL], F32, kind="ExternalInput").ap()
    cw = nc.dram_tensor("cw", [128, 2 * DCONV], F32, kind="ExternalInput").ap()
    cb = nc.dram_tensor("cb", [128, 2], F32, kind="ExternalInput").ap()
    dtb = nc.dram_tensor("dtb", [128, 2], F32, kind="ExternalInput").ap()
    alog = nc.dram_tensor("alog", [128, 2 * NST], F32, kind="ExternalInput").ap()
    dsl_in = nc.dram_tensor("dsl", [128, 2], F32, kind="ExternalInput").ap()
    woutTs = nc.dram_tensor("woutTs", [DSL, E], F32, kind="ExternalInput").ap()
    out_d = nc.dram_tensor("outT", [E, T], F32, kind="ExternalOutput").ap()

    x2p_d = nc.dram_tensor("x2p_d", [E, T], F32).ap()
    ag_in = [nc.dram_tensor("ag_in%d" % k, [E, CH], BF16).ap() for k in range(NCH1)]
    ag_out = [nc.dram_tensor("ag_out%d" % k, [NCORES * E, CH], BF16).ap() for k in range(NCH1)]
    rs_in = [nc.dram_tensor("rs_in%d" % q, [NCORES * E, SCH], F32).ap() for q in range(NQ)]
    rs_out = [nc.dram_tensor("rs_out%d" % q, [E, SCH], F32).ap() for q in range(NQ)]
    ar_in = [[nc.dram_tensor("ar_in%d_%d" % (b, i), [96, SCH], F32).ap() for i in range(3)]
             for b in range(BATCH)]
    ar_out = [[nc.dram_tensor("ar_out%d_%d" % (b, i), [96, SCH], F32).ap() for i in range(3)]
              for b in range(BATCH)]

    with contextlib.ExitStack() as stack:
        env = {}
        make_persist_env(nc, stack, env)
        sbp = lambda name, shape, dt: stack.enter_context(nc.sbuf_tensor(name, shape, dt))
        s_s0 = sbp("s_s0", [128, 1], F32)
        s_s1 = sbp("s_s1", [128, 1], F32)
        s_s2 = sbp("s_s2", [128, 1], F32)
        p = Prog(nc, stack)
        p.dma("sp", out=s_s0[:], in_=s0r, stream="ss0", writes=[("s0x4", "in")])
        p.dma("sp", out=s_s1[:], in_=s1r, stream="ss1", writes=[("s1x2", "in")])
        p.dma("sp", out=s_s2[:], in_=s2r, stream="ss2", writes=[("halfs2", "in")])
        emit_consts_scales(p, env, s_s0, s_s1, s_s2)

        ph1 = contextlib.ExitStack()
        env1 = dict(env)
        make_ffn_phase_env(nc, ph1, env1, nwb=2, tag="p1")
        sb1 = lambda name, shape, dt: ph1.enter_context(nc.sbuf_tensor(name, shape, dt))
        env1["cv_hw"] = 1024
        env1["cva"] = [sb1("cva%d" % i, [128, 1024], BF16) for i in range(2)]
        env1["cvb"] = [sb1("cvb%d" % i, [128, 1024], BF16) for i in range(2)]
        env1["cvw"] = [sb1("cvw%d" % i, [128, 1024], BF16) for i in range(2)]
        xc = [[sb1("xc%d_%d" % (i, t), [128, CH], F32) for t in range(8)] for i in range(2)]
        xin = [sb1("xin%d" % t, [128, CH], F32) for t in range(8)]
        nxb = [[sb1("nxb%d_%d" % (i, t), [128, CH], BF16) for t in range(8)] for i in range(2)]

        xnout = [sb1("xnout%d" % t, [128, CH], BF16) for t in range(8)]
        s_g = sb1("s_g", [128, 8], F32)
        s_b = sb1("s_b", [128, 8], F32)
        p.dma("sp", out=s_g[:], in_=lng, stream="sgv", writes=[("s_g",)])
        p.dma("sp", out=s_b[:], in_=lnb, stream="sbv", writes=[("s_gb",)])

        def p1_load(c):
            i = c % 2
            for t in range(8):
                p.dma("pool", out=xc[i][t][:], in_=xT[t * 128:(t + 1) * 128, c * CH:(c + 1) * CH],
                      stream="xc%d" % t, writes=[("xc", i, t)])

        def p1_ln1(c):
            i = c % 2
            emit_ln(p, env1, [(xc[i][t][:], ("xc", i, t)) for t in range(8)],
                    [(nxb[i][t][:], ("nxb", i, t)) for t in range(8)], 8, CH, gb=None)

        def p1_tail1(c):
            i = c % 2
            emit_ln(p, env1, [(xc[i][t][:], ("xc", i, t)) for t in range(8)],
                    [(xin[t][:], ("xin", t)) for t in range(8)], 8, CH, gb=None)
            for t in range(8):
                p.op("dve", lambda e, i=i, t=t: e.tensor_tensor(
                    out=xc[i][t][:], in0=xc[i][t][:], in1=xin[t][:], op=OP.add),
                     reads=[("xc", i, t), ("xin", t)], writes=[("xc", i, t)])
                p.dma("act", out=x2p_d[t * 128:(t + 1) * 128, c * CH:(c + 1) * CH], in_=xc[i][t][:],
                      stream="x2pst", reads=[("xc", i, t)], writes=[("x2p_st", c, t)])

        def p1_tail2(c):
            i = c % 2
            emit_ln(p, env1, [(xin[t][:], ("xin", t)) for t in range(8)],
                    [(xnout[t][:], ("xnout", t)) for t in range(8)], 8, CH,
                    gb=[(s_g[:, t:t + 1], s_b[:, t:t + 1]) for t in range(8)],
                    gbkeys=[("s_g",), ("s_gb",)])
            for t in range(8):
                p.dma("act", out=ag_in[c][t * 128:(t + 1) * 128, :], in_=xnout[t][:],
                      stream="xnst", reads=[("xnout", t)], writes=[("xn_st", c, t)])
            p.collective(lambda e, c=c: e.collective_compute(
                "AllGather", OP.bypass, replica_groups=[list(range(NCORES))],
                ins=[ag_in[c].opt()], outs=[ag_out[c].opt()]),
                 reads=[("xn_st", c, t) for t in range(8)], writes=[("ag", c)])

        p1_load(0)
        p1_ln1(0)
        wf0, wf1, wf2 = setup_ffn_weights(nc, p, env1, k0, k1, k2)
        for c in range(NCH1):
            i = c % 2
            if c + 1 < NCH1:
                p1_load(c + 1)
            emit_ffn_l1(p, env1, [(nxb[i][t][:], ("nxb", i, t)) for t in range(8)], wf0, 8, 32)
            if c + 1 < NCH1:
                p1_ln1(c + 1)
            if c >= 1:
                p1_tail2(c - 1)
            emit_ffn_l2(p, env1, wf1, 32, 32)
            emit_ffn_l3(p, env1, [(xc[i][t][:], ("xc", i, t)) for t in range(8)], wf2, 32, 8)
            p1_tail1(c)
        p1_tail2(NCH1 - 1)
        ph1.close()
        phase_barrier(p, [("x2p_st", NCH1 - 1, 7), ("wfdone", 2),
                          ("cvw", 0), ("cvw", 1), ("cva", 0), ("cva", 1), ("cvb", 0), ("cvb", 1),
                          ("ag", NCH1 - 1)])

        ph2 = contextlib.ExitStack()
        sb2 = lambda name, shape, dt: ph2.enter_context(nc.sbuf_tensor(name, shape, dt))
        ps2 = lambda name, shape: ph2.enter_context(nc.psum_tensor(name, shape, F32))
        xnb = [sb2("xnb%d" % t, [128, SCH], BF16) for t in range(8)]
        winsb = sb2("winsb", [128, 8 * 4 * 128], BF16)
        wxsb = sb2("wxsb", [128, 2 * 96], BF16)
        wdtsb = sb2("wdtsb", [DTR, 2 * 128], BF16)
        woutsb2 = sb2("woutsb2", [128, 2 * 1024], BF16)
        s_cw = sb2("s_cw", [128, 2 * DCONV], F32)
        s_cb = sb2("s_cb", [128, 2], F32)
        s_dtb = sb2("s_dtb", [128, 2], F32)
        s_aneg = sb2("s_aneg", [128, 2 * NST], F32)
        s_d = sb2("s_d", [128, 2], F32)
        cst2 = sb2("cstl2", [128, 2], F32)
        u_raw = [[[sb2("ur%d_%d_%d" % (b, i, t), [128, DCONV - 1 + SCH], F32) for t in range(NT)]
                  for i in range(3)] for b in range(BATCH)]
        zt = [[[sb2("z%d_%d_%d" % (b, i, t), [128, SCH], F32) for t in range(NT)]
               for i in range(3)] for b in range(BATCH)]
        ust = [[[sb2("us%d_%d_%d" % (b, i, t), [128, SCH], F32) for t in range(NT)]
                for i in range(3)] for b in range(BATCH)]
        conv = [sb2("conv%d" % t, [128, SCH], F32) for t in range(NT)]
        ub = [sb2("ub%d" % t, [128, SCH], BF16) for t in range(NT)]
        dbcs = sb2("dbcs", [96, SCH], F32)
        dtinb = sb2("dtinb", [DTR, SCH], BF16)
        dt = [[sb2("dt%d_%d" % (b, t), [128, SCH], F32) for t in range(NT)] for b in range(BATCH)]
        dtu = [sb2("dtu%d" % t, [128, SCH], F32) for t in range(NT)]
        sp1 = sb2("sp1", [128, 512], F32)
        sp2 = sb2("sp2", [128, 512], F32)
        bcB = [sb2("bcB%d" % i, [128, SCH], F32) for i in range(3)]
        bcC = [sb2("bcC%d" % i, [128, SCH], F32) for i in range(3)]
        dA = [[sb2("dA%d_%d" % (t, i), [128, SCH], F32) for i in range(2)] for t in range(NT)]
        dBu = [sb2("dBu%d" % t, [128, SCH], F32) for t in range(NT)]
        hb = [sb2("hb%d" % i, [128, SCH], F32) for i in range(4)]
        hcb = [sb2("hcb%d" % t, [128, SCH], F32) for t in range(NT)]
        y = [sb2("y%d" % t, [128, SCH], F32) for t in range(NT)]
        ygb = [[sb2("ygb%d_%d" % (b, t), [128, SCH], BF16) for t in range(NT)] for b in range(BATCH)]
        motile = [sb2("motile%d" % i, [128, 512], F32) for i in range(2)]
        hcar = [[sb2("hcar%d_%d" % (b, t), [128, NST], F32) for t in range(NT)] for b in range(BATCH)]
        pW = [ps2("pW%d" % i, [128, 512]) for i in range(4)]

        for k in range(8):
            p.dma("pool", out=winsb[:, k * 512:(k + 1) * 512], in_=winT[k * 128:(k + 1) * 128, :],
                  stream="winl", writes=[("winsb",)])
        for kt in range(NT):
            p.dma("pool", out=wxsb[:, kt * 96:(kt + 1) * 96], in_=wxT[kt * 128:(kt + 1) * 128, :],
                  stream="wxl", writes=[("wxsb",)])
        p.dma("pool", out=wdtsb[:], in_=wdtT, stream="wdtl", writes=[("wdtsb",)])
        for kt in range(NT):
            p.dma("pool", out=woutsb2[:, kt * 1024:(kt + 1) * 1024], in_=woutTs[kt * 128:(kt + 1) * 128, :],
                  stream="wo2l", writes=[("woutsb2",)])
        p.dma("sp", out=s_cw[:], in_=cw, stream="scw", writes=[("s_cw",)])
        p.dma("sp", out=s_cb[:], in_=cb, stream="scb", writes=[("s_cb",)])
        p.dma("sp", out=s_dtb[:], in_=dtb, stream="sdtb", writes=[("s_dtb",)])
        p.dma("sp", out=s_aneg[:], in_=alog, stream="saneg", writes=[("s_aneg",)])
        p.dma("sp", out=s_d[:], in_=dsl_in, stream="sdv", writes=[("s_d",)])
        p.op("dve", lambda e: e.memset(cst2[:, 0:1], 1.0), writes=[("cst2",)])
        p.op("dve", lambda e: e.memset(cst2[:, 1:2], -1.0), writes=[("cst2",)])
        p.op("act", lambda e: e.activation(out=s_aneg[:], in_=s_aneg[:], func=AF.Exp),
             reads=[("s_aneg",)], writes=[("s_aneg",)])
        p.op("dve", lambda e: e.tensor_scalar(out=s_aneg[:], in0=s_aneg[:], scalar1=-1.0,
                                              scalar2=0.0, op0=OP.mult, op1=OP.add),
             reads=[("s_aneg",)], writes=[("s_aneg",)])
        for b in range(BATCH):
            for t in range(NT):
                p.op("dve", lambda e, b=b, t=t: e.memset(hcar[b][t][:], 0.0),
                     writes=[("hcar", b, t, n) for n in range(NST)])

        def rro(b, ci):
            r = b * 4 + ci % 4
            o = (ci // 4) * SCH
            return r, o

        def pre_phase(b, ci):
            par = ci % 3
            r, o = rro(b, ci)
            kc = o // CH
            ur = u_raw[b][par]
            z = zt[b][par]
            us = ust[b][par]
            for t8 in range(8):
                p.dma("pool", out=xnb[t8][:],
                      in_=ag_out[kc][r * E + t8 * 128:r * E + (t8 + 1) * 128, :],
                      stream="xnb%d" % t8, reads=[("ag", kc)], writes=[("xnb", t8)])
            for t in range(NT):
                if ci == 0:
                    p.op("dve", lambda e, b=b, par=par, t=t: e.memset(ur[t][:, 0:DCONV - 1], 0.0),
                         reads=[("ur", b, par, t)], writes=[("ucar", b, par, t)])
                else:
                    pprev = (ci - 1) % 3
                    p.op("dve", lambda e, b=b, par=par, t=t, pprev=pprev: e.tensor_copy(
                        out=u_raw[b][par][t][:, 0:DCONV - 1],
                        in_=u_raw[b][pprev][t][:, SCH:SCH + DCONV - 1]),
                         reads=[("ur", b, pprev, t), ("ur", b, par, t)],
                         writes=[("ucar", b, par, t)])
            for mt in range(4):
                pp = pW[mt % 4]
                pk = ("pW", mt % 4)
                for k in range(8):
                    p.op("pe", lambda e, pp=pp, k=k, mt=mt, s=(k == 0), t_=(k == 7): e.matmul(
                        out=pp[:], lhsT=winsb[:, k * 512 + mt * 128:k * 512 + (mt + 1) * 128],
                        rhs=xnb[k][:], start=s, stop=t_),
                         reads=[("winsb",), ("xnb", k)], writes=[pk])
                if mt < NT:
                    t = mt
                    p.op("act", lambda e, pp=pp, t=t: e.activation(
                        out=ur[t][:, DCONV - 1:DCONV - 1 + SCH], in_=pp[:], func=AF.Copy),
                         reads=[pk, ("ucar", b, par, t)], writes=[("ur", b, par, t)])
                else:
                    t = mt - NT
                    p.op("act", lambda e, pp=pp, t=t: e.activation(
                        out=z[t][:], in_=pp[:], func=AF.Copy),
                         reads=[pk, ("z", b, par, t)], writes=[("z", b, par, t)])
            for t in range(NT):
                p.op("dve", lambda e, t=t, ur=ur: e.tensor_scalar(
                    out=conv[t][:], in0=ur[t][:, 0:SCH], scalar1=s_cw[:, t * DCONV:t * DCONV + 1],
                    scalar2=0.0, op0=OP.mult, op1=OP.add),
                     reads=[("ur", b, par, t), ("ucar", b, par, t), ("s_cw",), ("conv", t)],
                     writes=[("conv", t)])
                for kk in range(1, DCONV):
                    p.op("dve", lambda e, t=t, kk=kk, ur=ur: e.scalar_tensor_tensor(
                        out=conv[t][:], in0=ur[t][:, kk:kk + SCH],
                        scalar=s_cw[:, t * DCONV + kk:t * DCONV + kk + 1],
                        in1=conv[t][:], op0=OP.mult, op1=OP.add),
                         reads=[("ur", b, par, t), ("conv", t), ("s_cw",)], writes=[("conv", t)])
                p.op("act", lambda e, t=t, us=us: e.activation(
                    out=us[t][:], in_=conv[t][:], func=AF.Silu, bias=s_cb[:, t:t + 1]),
                     reads=[("conv", t), ("s_cb",), ("us", b, par, t)], writes=[("us", b, par, t)])
                p.op("act", lambda e, t=t, us=us: e.activation(out=ub[t][:], in_=us[t][:], func=AF.Copy),
                     reads=[("us", b, par, t), ("ub", t)], writes=[("ub", t)])
            pp = pW[0]
            pk = ("pW", 0)
            for kt in range(NT):
                p.op("pe", lambda e, pp=pp, kt=kt, s=(kt == 0), t_=(kt == NT - 1): e.matmul(
                    out=pp[0:96, :], lhsT=wxsb[:, kt * 96:(kt + 1) * 96],
                    rhs=ub[kt][:], start=s, stop=t_),
                     reads=[("wxsb",), ("ub", kt)], writes=[pk])
            p.op("act", lambda e, pp=pp: e.activation(
                out=dbcs[0:96, :], in_=pp[0:96, :], func=AF.Copy),
                 reads=[pk, ("dbcs",)], writes=[("dbcs",)])
            p.dma("sp", out=ar_in[b][par], in_=dbcs[0:96, :], stream="arin%d%d" % (b, par),
                  reads=[("dbcs",)], writes=[("ar_in", b, par)])
            p.collective(lambda e, b=b, par=par: e.collective_compute(
                "AllReduce", OP.add, replica_groups=[list(range(NCORES))],
                ins=[ar_in[b][par].opt()], outs=[ar_out[b][par].opt()]),
                 reads=[("ar_in", b, par), ("ar_out", b, par)], writes=[("ar_out", b, par)])

        def post_dt(b, ci):
            par = ci % 3
            p.dma("pool", out=dtinb[:], in_=ar_out[b][par][0:DTR, :], stream="dtinb",
                  reads=[("ar_out", b, par), ("dtinb",)], writes=[("dtinb",)])
            for mt in range(NT):
                pp = pW[2 + mt % 2]
                pk = ("pW", 2 + mt % 2)
                p.op("pe", lambda e, pp=pp, mt=mt: e.matmul(
                    out=pp[:], lhsT=wdtsb[:, mt * 128:(mt + 1) * 128],
                    rhs=dtinb[:], start=True, stop=True),
                     reads=[("wdtsb",), ("dtinb",)], writes=[pk])
                p.op("act", lambda e, pp=pp, b=b, mt=mt: e.activation(
                    out=dt[b][mt][:], in_=pp[:], func=AF.Relu, bias=s_dtb[:, mt:mt + 1]),
                     reads=[pk, ("s_dtb",), ("dt", b, mt)], writes=[("dt", b, mt)])
                p.op("act", lambda e, pp=pp, mt=mt: e.activation(
                    out=sp1[:], in_=pp[:], func=AF.Abs, bias=s_dtb[:, mt:mt + 1]),
                     reads=[pk, ("s_dtb",), ("sp1",)], writes=[("sp1",)])
                p.op("act", lambda e: e.activation(out=sp2[:], in_=sp1[:], func=AF.Exp,
                                                   scale=cst2[:, 1:2]),
                     reads=[("sp1",), ("cst2",), ("sp2",)], writes=[("sp2",)])
                p.op("act", lambda e: e.activation(out=sp1[:], in_=sp2[:], func=AF.Ln,
                                                   bias=cst2[:, 0:1]),
                     reads=[("sp2",), ("cst2",), ("sp1",)], writes=[("sp1",)])
                p.op("dve", lambda e, b=b, mt=mt: e.tensor_tensor(
                    out=dt[b][mt][:], in0=dt[b][mt][:], in1=sp1[:], op=OP.add),
                     reads=[("dt", b, mt), ("sp1",)], writes=[("dt", b, mt)])

        def post_scan(b, ci, mid=None):
            par = ci % 3
            r, o = rro(b, ci)
            q = o // SCH
            us = ust[b][par]
            z = zt[b][par]
            for t in range(NT):
                p.op("dve", lambda e, b=b, t=t, us=us: e.tensor_tensor(
                    out=dtu[t][:], in0=dt[b][t][:], in1=us[t][:], op=OP.mult),
                     reads=[("dt", b, t), ("us", b, par, t), ("dtu", t)], writes=[("dtu", t)])
                p.op("act", lambda e, t=t, z=z: e.activation(out=z[t][:], in_=z[t][:], func=AF.Silu),
                     reads=[("z", b, par, t)], writes=[("z", b, par, t)])
            for np_ in range(0, NST, 2):
                if np_ == 8 and mid is not None:
                    mid()
                for n in (np_, np_ + 1):
                    i2 = n % 3
                    p.dma("sp", out=bcB[i2][:], in_=ar_out[b][par][DTR + n, :].partition_broadcast(128),
                          stream="bcB%d" % i2, reads=[("ar_out", b, par), ("bcB", i2)], writes=[("bcB", i2)])
                    p.dma("pool", out=bcC[i2][:], in_=ar_out[b][par][DTR + NST + n, :].partition_broadcast(128),
                          stream="bcC%d" % i2, reads=[("ar_out", b, par), ("bcC", i2)], writes=[("bcC", i2)])
                    for t in range(NT):
                        iA = n % 2
                        p.op("act", lambda e, b=b, t=t, n=n, iA=iA: e.activation(
                            out=dA[t][iA][:], in_=dt[b][t][:], func=AF.Exp,
                            scale=s_aneg[:, t * NST + n:t * NST + n + 1]),
                             reads=[("dt", b, t), ("s_aneg",), ("dA", t, iA)], writes=[("dA", t, iA)])
                for n in (np_, np_ + 1):
                    i2 = n % 3
                    iA = n % 2
                    for t in range(NT):
                        hi = (n * NT + t) % 4
                        p.op("dve", lambda e, t=t, i2=i2: e.tensor_tensor(
                            out=dBu[t][:], in0=dtu[t][:], in1=bcB[i2][:], op=OP.mult),
                             reads=[("dtu", t), ("bcB", i2), ("dBu", t)], writes=[("dBu", t)])
                        p.op("dve", lambda e, b=b, t=t, n=n, hi=hi, iA=iA: e.tensor_tensor_scan(
                            out=hb[hi][:], data0=dA[t][iA][:], data1=dBu[t][:],
                            initial=hcar[b][t][:, n:n + 1], op0=OP.mult, op1=OP.add),
                             reads=[("dA", t, iA), ("dBu", t), ("hcar", b, t, n), ("hb", hi)],
                             writes=[("hb", hi)])
                        p.op("dve", lambda e, b=b, t=t, n=n, hi=hi: e.tensor_copy(
                            out=hcar[b][t][:, n:n + 1], in_=hb[hi][:, SCH - 1:SCH]),
                             reads=[("hb", hi)], writes=[("hcar", b, t, n)])
                        if n == 0:
                            p.op("dve", lambda e, t=t, i2=i2, hi=hi: e.tensor_tensor(
                                out=y[t][:], in0=hb[hi][:], in1=bcC[i2][:], op=OP.mult),
                                 reads=[("hb", hi), ("bcC", i2), ("y", t)], writes=[("y", t)])
                        else:
                            p.op("dve", lambda e, t=t, i2=i2, hi=hi: e.tensor_tensor(
                                out=hcb[t][:], in0=hb[hi][:], in1=bcC[i2][:], op=OP.mult),
                                 reads=[("hb", hi), ("bcC", i2), ("hcb", t)], writes=[("hcb", t)])
                            p.op("dve", lambda e, t=t: e.tensor_tensor(
                                out=y[t][:], in0=y[t][:], in1=hcb[t][:], op=OP.add),
                                 reads=[("y", t), ("hcb", t)], writes=[("y", t)])
            for t in range(NT):
                p.op("dve", lambda e, b=b, t=t, us=us: e.scalar_tensor_tensor(
                    out=y[t][:], in0=us[t][:], scalar=s_d[:, t:t + 1], in1=y[t][:],
                    op0=OP.mult, op1=OP.add),
                     reads=[("us", b, par, t), ("y", t), ("s_d",)], writes=[("y", t)])
                p.op("dve", lambda e, b=b, t=t, z=z: e.tensor_tensor(
                    out=ygb[b][t][:], in0=y[t][:], in1=z[t][:], op=OP.mult),
                     reads=[("y", t), ("z", b, par, t), ("ygb", b, t)], writes=[("ygb", b, t)])

        def post_wout(b, ci):
            r, o = rro(b, ci)
            q = o // SCH
            for mt in range(8):
                pp = pW[mt % 2]
                pk = ("pW", mt % 2)
                for kt in range(NT):
                    p.op("pe", lambda e, pp=pp, b=b, kt=kt, mt=mt, s=(kt == 0), t_=(kt == NT - 1): e.matmul(
                        out=pp[:], lhsT=woutsb2[:, kt * 1024 + mt * 128:kt * 1024 + (mt + 1) * 128],
                        rhs=ygb[b][kt][:], start=s, stop=t_),
                         reads=[("woutsb2",), ("ygb", b, kt)], writes=[pk])
                mo = motile[mt % 2]
                mok = ("motile", mt % 2)
                p.op("act", lambda e, pp=pp, mo=mo: e.activation(out=mo[:], in_=pp[:], func=AF.Copy),
                     reads=[pk, mok], writes=[mok])
                p.dma("act", out=rs_in[q][r * E + mt * 128:r * E + (mt + 1) * 128, :],
                      in_=mo[:], stream="rsst", reads=[mok], writes=[("rs_st", q)])

        NCI = Lb // SCH
        for b in range(BATCH):
            pre_phase(b, 0)
        for b in range(BATCH):
            if NCI > 1:
                pre_phase(b, 1)
        for b in range(BATCH):
            post_dt(b, 0)
        def fire_rs(q):
            p.collective(lambda e, q=q: e.collective_compute(
                "ReduceScatter", OP.add, replica_groups=[list(range(NCORES))],
                ins=[rs_in[q].opt()], outs=[rs_out[q].opt()]),
                 reads=[("rs_st", q)], writes=[("rs_done", q)])

        for ci in range(NCI):
            for b in range(BATCH):
                def mid(b=b, ci=ci):
                    if ci + 1 < NCI:
                        post_dt(b, ci + 1)
                    if b == 0:
                        if ci >= 1:
                            post_wout(1, ci - 1)
                            if (ci - 1) % 4 == 3:
                                fire_rs((ci - 1) // 4)
                    else:
                        post_wout(0, ci)
                post_scan(b, ci, mid=mid)
                if ci + 2 < NCI:
                    pre_phase(b, ci + 2)
        post_wout(1, NCI - 1)
        fire_rs(NQ - 1)
        ph2.close()
        hkeys = [("hcar", b, t, n) for b in range(BATCH) for t in range(NT) for n in range(NST)]
        phase_barrier(p, [("rs_done", q) for q in range(NQ)] + hkeys
                      + [("ucar", b, par, t) for b in range(BATCH) for par in range(3) for t in range(NT)])

        ph3 = contextlib.ExitStack()
        env3 = dict(env)
        make_ffn_phase_env(nc, ph3, env3, nwb=2, tag="p3")
        sb3 = lambda name, shape, dt: ph3.enter_context(nc.sbuf_tensor(name, shape, dt))
        x3 = [[sb3("x3_%d_%d" % (i, t), [128, CH], F32) for t in range(8)] for i in range(2)]
        mo3 = [sb3("mo3_%d" % t, [128, CH], F32) for t in range(8)]
        nxb3 = [[sb3("nxb3_%d_%d" % (i, t), [128, CH], BF16) for t in range(8)] for i in range(2)]
        store_keys = []

        def p3_load(c):
            i = c % 2
            c0 = c * CH
            q = c0 // SCH
            qo = c0 - q * SCH
            for t in range(8):
                p.dma("pool", out=x3[i][t][:], in_=x2p_d[t * 128:(t + 1) * 128, c0:c0 + CH],
                      stream="x3l%d" % t, writes=[("x3", i, t)])
                p.dma("pool", out=mo3[t][:], in_=rs_out[q][t * 128:(t + 1) * 128, qo:qo + CH],
                      stream="mo%d" % t, reads=[("rs_done", q)], writes=[("mo3", t)])

        def p3_ln(c):
            i = c % 2
            for t in range(8):
                p.op("dve", lambda e, i=i, t=t: e.tensor_tensor(
                    out=x3[i][t][:], in0=x3[i][t][:], in1=mo3[t][:], op=OP.add),
                     reads=[("x3", i, t), ("mo3", t)], writes=[("x3", i, t)])
            emit_ln(p, env3, [(x3[i][t][:], ("x3", i, t)) for t in range(8)],
                    [(nxb3[i][t][:], ("nxb3", i, t)) for t in range(8)], 8, CH, gb=None)

        p3_load(0)
        p3_ln(0)
        for c in range(NCH1):
            i = c % 2
            if c + 1 < NCH1:
                p3_load(c + 1)
            emit_ffn_l1(p, env3, [(nxb3[i][t][:], ("nxb3", i, t)) for t in range(8)], wf0, 8, 32)
            if c + 1 < NCH1:
                p3_ln(c + 1)
            emit_ffn_l2(p, env3, wf1, 32, 32)
            emit_ffn_l3(p, env3, [(x3[i][t][:], ("x3", i, t)) for t in range(8)], wf2, 32, 8)
            for t in range(8):
                skey = ("out_st", c, t)
                p.dma("act", out=out_d[t * 128:(t + 1) * 128, c * CH:(c + 1) * CH], in_=x3[i][t][:],
                      stream="outst", reads=[("x3", i, t)], writes=[skey])
                store_keys.append(skey)
        ph3.close()
        p.op("pool", lambda e: e.nop(nofuse=True), reads=store_keys[-8:])
        with nc.Block() as block:
            p.emit(block)
    return nc


def _forward_fused(inputs):
    x = np.asarray(inputs["L0_attn_x"], np.float32)
    L = x.shape[1]
    T = L // 4
    s0, s1, s2, k0, k1, k2 = _prep_common(inputs)
    lng = np.ascontiguousarray(np.asarray(inputs["L0_attn_ln_g"], np.float32).reshape(8, 128).T)
    lnb = np.ascontiguousarray(np.asarray(inputs["L0_attn_ln_b"], np.float32).reshape(8, 128).T)
    W_in = np.asarray(inputs["L0_attn_W_in"], np.float32)
    W_x = np.asarray(inputs["L0_attn_W_x"], np.float32)
    W_dt = np.asarray(inputs["L0_attn_W_dt"], np.float32)
    conv_w = np.asarray(inputs["L0_attn_conv_w"], np.float32)
    conv_b = np.asarray(inputs["L0_attn_conv_b"], np.float32)
    dt_bias = np.asarray(inputs["L0_attn_dt_bias"], np.float32)
    A_log = np.asarray(inputs["L0_attn_A_log"], np.float32)
    Dv = np.asarray(inputs["L0_attn_D"], np.float32)
    W_out = np.asarray(inputs["L0_attn_W_out"], np.float32)

    nc = build_fused(T=T)
    CHW = 512
    nstr = T // CHW
    in_maps = []
    for c in range(NCORES):
        b = c // 4
        stripes = [(4 * j + c % 4) * CHW for j in range(nstr)]
        xTc = np.concatenate([x[b, s:s + CHW, :].T for s in stripes], axis=1)
        sl = slice(c * DSL, (c + 1) * DSL)
        rows = np.r_[c * DSL:(c + 1) * DSL, DIN + c * DSL:DIN + (c + 1) * DSL]
        in_maps.append(dict(
            xT=np.ascontiguousarray(xTc),
            k0=k0, k1=k1, k2=k2, s0r=s0, s1r=s1, s2r=s2, lng=lng, lnb=lnb,
            winT=np.ascontiguousarray(W_in[rows].T),
            wxT=np.ascontiguousarray(W_x[:, sl].T),
            wdtT=np.ascontiguousarray(W_dt[sl].T),
            cw=_tile2(conv_w[sl]), cb=_tile2(conv_b[sl]), dtb=_tile2(dt_bias[sl]),
            alog=_tile2(A_log[sl]), dsl=_tile2(Dv[sl]),
            woutTs=np.ascontiguousarray(W_out.T[sl]),
        ))
    res = run_bass_kernel_spmd(nc, in_maps, core_ids=list(range(NCORES))).results
    out = np.empty((BATCH, L, E), np.float32)
    for c in range(NCORES):
        b = c // 4
        o = np.asarray(res[c]["outT"])
        for j in range(nstr):
            s = (4 * j + c % 4) * CHW
            out[b, s:s + CHW, :] = o[:, j * CHW:(j + 1) * CHW].T
    return out



def _prep_common(inputs):
    s0 = np.ascontiguousarray(np.broadcast_to(np.asarray(inputs["L0_ffn_s0"], np.float32).reshape(1, 1), (128, 1)))
    s1 = np.ascontiguousarray(np.broadcast_to(np.asarray(inputs["L0_ffn_s1"], np.float32).reshape(1, 1), (128, 1)))
    s2 = np.ascontiguousarray(np.broadcast_to(np.asarray(inputs["L0_ffn_s2"], np.float32).reshape(1, 1), (128, 1)))
    k0 = np.ascontiguousarray(np.asarray(inputs["L0_ffn_k0"]).view(np.uint8))
    k1 = np.ascontiguousarray(np.asarray(inputs["L0_ffn_k1"]).view(np.uint8))
    k2 = np.ascontiguousarray(np.asarray(inputs["L0_ffn_k2"]).view(np.uint8))
    return s0, s1, s2, k0, k1, k2


def _tile2(v):
    v = np.asarray(v, np.float32)
    if v.ndim == 1:
        v = v[:, None]
    k = v.shape[1]
    return np.ascontiguousarray(v.reshape(2, 128, k).transpose(1, 0, 2).reshape(128, 2 * k))


def kernel(**inputs):
    return _forward_fused(inputs)


def _forward(inputs, collect=False):
    x = np.asarray(inputs["L0_attn_x"], np.float32)
    L = x.shape[1]
    T = L // 4
    s0, s1, s2, k0, k1, k2 = _prep_common(inputs)
    lng = np.ascontiguousarray(np.asarray(inputs["L0_attn_ln_g"], np.float32).reshape(8, 128).T)
    lnb = np.ascontiguousarray(np.asarray(inputs["L0_attn_ln_b"], np.float32).reshape(8, 128).T)

    nc1 = build_launch1(T=T)
    in1 = []
    for c in range(NCORES):
        b, lo = c // 4, (c % 4) * T
        xT = np.ascontiguousarray(x[b, lo:lo + T, :].T)
        in1.append(dict(xT=xT, k0=k0, k1=k1, k2=k2, s0r=s0, s1r=s1, s2r=s2, lng=lng, lnb=lnb))
    r1 = run_bass_kernel_spmd(nc1, in1, core_ids=list(range(NCORES))).results
    xn_full = np.empty((BATCH * E, L), dtype=BF)
    x2p = [np.ascontiguousarray(np.asarray(r1[c]["x2p_out"])) for c in range(NCORES)]
    for c in range(NCORES):
        b, lo = c // 4, (c % 4) * T
        xn_full[b * E:(b + 1) * E, lo:lo + T] = np.asarray(r1[c]["xn_out"])

    W_in = np.asarray(inputs["L0_attn_W_in"], np.float32)
    W_x = np.asarray(inputs["L0_attn_W_x"], np.float32)
    W_dt = np.asarray(inputs["L0_attn_W_dt"], np.float32)
    conv_w = np.asarray(inputs["L0_attn_conv_w"], np.float32)
    conv_b = np.asarray(inputs["L0_attn_conv_b"], np.float32)
    dt_bias = np.asarray(inputs["L0_attn_dt_bias"], np.float32)
    A_log = np.asarray(inputs["L0_attn_A_log"], np.float32)
    Dv = np.asarray(inputs["L0_attn_D"], np.float32)
    nc2 = build_launch2(Lb=L, SCH=1024)
    in2 = []
    for c in range(NCORES):
        sl = slice(c * DSL, (c + 1) * DSL)
        rows = np.r_[c * DSL:(c + 1) * DSL, DIN + c * DSL:DIN + (c + 1) * DSL]
        in2.append(dict(
            xn_in=xn_full,
            winT=np.ascontiguousarray(W_in[rows].T),
            wxT=np.ascontiguousarray(W_x[:, sl].T),
            wdtT=np.ascontiguousarray(W_dt[sl].T),
            cw=_tile2(conv_w[sl]),
            cb=_tile2(conv_b[sl]),
            dtb=_tile2(dt_bias[sl]),
            alog=_tile2(A_log[sl]),
            dsl=_tile2(Dv[sl]),
        ))
    r2 = run_bass_kernel_spmd(nc2, in2, core_ids=list(range(NCORES))).results
    yg_full = np.empty((BATCH, DIN, L), dtype=BF)
    for c in range(NCORES):
        ygc = np.asarray(r2[c]["yg_out"]).reshape(BATCH, DSL, L)
        yg_full[:, c * DSL:(c + 1) * DSL, :] = ygc

    W_out = np.asarray(inputs["L0_attn_W_out"], np.float32)
    nc3 = build_launch3(T=T)
    in3 = []
    for c in range(NCORES):
        b, lo = c // 4, (c % 4) * T
        in3.append(dict(
            yg_in=np.ascontiguousarray(yg_full[b, :, lo:lo + T]),
            x2p_in=x2p[c],
            woutT=np.ascontiguousarray(W_out.T),
            k0=k0, k1=k1, k2=k2, s0r=s0, s1r=s1, s2r=s2,
        ))
    r3 = run_bass_kernel_spmd(nc3, in3, core_ids=list(range(NCORES))).results
    out = np.empty((BATCH, L, E), np.float32)
    for c in range(NCORES):
        b, lo = c // 4, (c % 4) * T
        out[b, lo:lo + T, :] = np.asarray(r3[c]["outT"]).T
    inter = None
    if collect:
        x2p_full = np.empty((BATCH * E, L), np.float32)
        for c in range(NCORES):
            b, lo = c // 4, (c % 4) * T
            x2p_full[b * E:(b + 1) * E, lo:lo + T] = x2p[c]
        inter = dict(xn=xn_full, x2p=x2p_full, yg=yg_full)
    return out, inter
